# version 19
# baseline (speedup 1.0000x reference)
import math
from contextlib import ExitStack

import numpy as np
import ml_dtypes
import concourse.bass as bass
import concourse.mybir as mybir

DEBUG_P1B = False

F32 = mybir.dt.float32
BF16 = mybir.dt.bfloat16
AF = mybir.ActivationFunctionType
AL = mybir.AluOpType
AX = mybir.AxisListType.X

D = 1024
NH = 8
HD = 128
I_FFN = 2816
NI = I_FFN // 128
NE = 8
F_E = 1024
D_MLP = 4096
QMAX = 127.0
MAGIC = 1.5 * 2 ** 23
NEGBIG = -3.0e38
EPS_LN = 1e-5
RT128 = float(math.sqrt(128.0))


def bfc(x):
    return np.asarray(x, dtype=ml_dtypes.bfloat16)


class Sem:
    def __init__(self, nc, stack, name):
        self.h = stack.enter_context(nc.semaphore(name))
        self.n = 0

    def inc(self, instr, k=1):
        instr.then_inc(self.h, k)
        self.n += k
        return instr

    def wait(self, eng, val=None):
        eng.wait_ge(self.h, self.n if val is None else val)


class Bld:
    def __init__(self, nc, stack):
        self.nc = nc
        self.stack = stack
        self.t, self.v, self.s, self.g, self.y = (
            nc.tensor, nc.vector, nc.scalar, nc.gpsimd, nc.sync)
        self._i = 0
        self.ct = self.sem("ch_t")
        self.cv = self.sem("ch_v")
        self.cs = self.sem("ch_s")
        self.cg = self.sem("ch_g")

    def sem(self, name=None):
        self._i += 1
        s = Sem(self.nc, self.stack, name or "s%d" % self._i)
        if not hasattr(self, "all_sems"):
            self.all_sems = []
        self.all_sems.append(s)
        return s

    def full_barrier(self):
        for eng in (self.t, self.v, self.s, self.g, self.y):
            for s in self.all_sems:
                if s.n:
                    eng.wait_ge(s.h, s.n)

    def begin_phase(self):
        self._ph_mark = len(self.all_sems)
        self._ph_snap = self.nc.snapshot_sems()
        self.full_barrier()

    def end_phase(self):
        self.full_barrier()
        freed = self.nc.allocated_since(self._ph_snap)
        if freed:
            for eng in (self.t, self.v, self.s, self.g, self.y):
                eng.drain()
            self.nc.all_engine_barrier()
            self.nc.clear_and_free_semaphores(freed)
            self.nc.all_engine_barrier()
        del self.all_sems[self._ph_mark:]

    def chain(self, eng):
        return {id(self.t): self.ct, id(self.v): self.cv,
                id(self.s): self.cs, id(self.g): self.cg}[id(eng)]

    def tm(self, ins):
        return self.ct.inc(ins)

    def vm(self, ins):
        return self.cv.inc(ins)

    def sm(self, ins):
        return self.cs.inc(ins)

    def gm(self, ins):
        return self.cg.inc(ins)

    def wt(self, eng):
        self.ct.wait(eng)

    def wv(self, eng):
        self.cv.wait(eng)

    def ws(self, eng):
        self.cs.wait(eng)

    def wg(self, eng):
        self.cg.wait(eng)

    def wall(self, eng):
        self.wt(eng); self.wv(eng); self.ws(eng); self.wg(eng)


class DBuf:

    def __init__(self, b, eng, nslots=2, name=None):
        self.b = b
        self.eng = eng
        self.sems = [b.sem((name or "db") + str(i)) for i in range(nslots)]
        self.free = [None] * nslots
        self.k = nslots

    def slot(self, i):
        return self.sems[i % self.k]

    def gate(self, i):
        f = self.free[i % self.k]
        if f is not None:
            sem, cnt = f
            self.eng.wait_ge(sem.h, cnt)
        s = self.sems[i % self.k]
        if s.n:
            self.eng.wait_ge(s.h, s.n)

    def done(self, i, sem):
        self.free[i % self.k] = (sem, sem.n)

    def wait(self, i, eng):
        self.sems[i % self.k].wait(eng)

    def wait_all(self, eng):
        for s in self.sems:
            s.wait(eng)


def host_constants(T):
    c = {}
    c["c_ident_bf"] = bfc(np.eye(128, dtype=np.float32))
    c["c_ident_f32"] = np.eye(128, dtype=np.float32)
    c["c_onesrow_f32"] = np.ones((1, 128), dtype=np.float32)
    c["c_ones128_f32"] = np.ones((128, 128), dtype=np.float32)
    c["c_onescol_f32"] = np.ones((128, 1), dtype=np.float32)
    m = np.zeros((4, 128, 512), dtype=np.float32)
    fq = np.arange(512)[None, :]
    pk = np.arange(128)[:, None]
    for j in range(4):
        m[j] = (fq >= j * 128 + pk).astype(np.float32)
    c["c_mask_l0"] = bfc(m.reshape(512, 512))
    a = np.zeros((4, 128, 512), dtype=np.float32)
    fk = np.arange(512)[None, :]
    pq = np.arange(128)[:, None]
    for j in range(4):
        a[j] = np.where(fk <= j * 128 + pq, 0.0, NEGBIG).astype(np.float32)
    c["c_maskadd_l1"] = a.reshape(512, 512)
    t = np.array([[(i >> 2 & 1) * 2 - 1, (i >> 1 & 1) * 2 - 1, (i & 1) * 2 - 1]
                  for i in range(8)], dtype=np.float32)
    dirs = t / np.sqrt(3.0, dtype=np.float32)
    c["c_dirsT"] = np.ascontiguousarray(dirs.T)
    dp = np.zeros((1, 8 * 128), dtype=np.float32)
    for h in range(8):
        dp[0, h * 128:h * 128 + 3] = dirs[h]
    c["c_dirs_pad"] = dp
    eh = np.zeros((128, 64), dtype=np.float32)
    for h in range(8):
        eh[:, h * 8 + h] = 1.0
    c["c_eh"] = eh
    half = 512
    freqs = np.exp(-np.arange(half, dtype=np.float32)
                   * np.float32(math.log(10000.0) / half)).astype(np.float32)
    th = np.arange(T, dtype=np.float32)[:, None] * freqs[None, :]
    c["c_rope_cos"] = np.cos(th).astype(np.float32)
    c["c_rope_sin"] = np.sin(th).astype(np.float32)
    s_ = np.arange(128)[:, None]
    t_ = np.arange(128)[None, :]
    c["c_tril_incl"] = (s_ <= t_).astype(np.float32)
    return c


CONST_SPECS = {
    "c_ident_bf": ([128, 128], BF16),
    "c_ident_f32": ([128, 128], F32),
    "c_onesrow_f32": ([1, 128], F32),
    "c_ones128_f32": ([128, 128], F32),
    "c_onescol_f32": ([128, 1], F32),
    "c_mask_l0": ([512, 512], BF16),
    "c_maskadd_l1": ([512, 512], F32),
    "c_dirsT": ([3, 8], F32),
    "c_dirs_pad": ([1, 1024], F32),
    "c_eh": ([128, 64], F32),
    "c_rope_cos": (None, F32),
    "c_rope_sin": (None, F32),
    "c_tril_incl": ([128, 128], F32),
}


def build_program(T, n_cores, upto="full"):
    nT = T // 128
    nC = T // 512
    N_ALL = n_cores * T
    CAP = min(int(math.ceil(N_ALL / NE) * 1.0), N_ALL)

    nc = bass.Bass("TRN2", target_bir_lowering=False)

    x_in = nc.dram_tensor("L0_attn_x", [T, D], F32, kind="ExternalInput")
    w_qkv = nc.dram_tensor("L0_attn_w_qkv", [D, 3 * D], F32, kind="ExternalInput")
    w_out0 = nc.dram_tensor("L0_attn_w_out", [D, D], F32, kind="ExternalInput")
    head_sc = nc.dram_tensor("L0_attn_head_scales", [8], F32, kind="ExternalInput")
    clip_x = nc.dram_tensor("L0_ffn_clip_x", [1], F32, kind="ExternalInput")
    clip_dn = nc.dram_tensor("L0_ffn_clip_down", [1], F32, kind="ExternalInput")
    w_gate = nc.dram_tensor("L0_ffn_w_gate", [I_FFN, D], F32, kind="ExternalInput")
    w_up = nc.dram_tensor("L0_ffn_w_up", [I_FFN, D], F32, kind="ExternalInput")
    w_down = nc.dram_tensor("L0_ffn_w_down", [D, I_FFN], F32, kind="ExternalInput")
    l1w = {}
    for nm, shp in [("Wq", [D, D]), ("Wk", [D, D]), ("Wv", [D, D]),
                    ("gate_W", [D, D]), ("gate_b", [D]), ("p_W", [D, D]),
                    ("p_b", [D]), ("clock_W", [D, 1]), ("clock_b", [1]),
                    ("out_W", [D, D]), ("out_b", [D]), ("ln1_g", [D]),
                    ("ln1_b", [D]), ("ln2_g", [D]), ("ln2_b", [D]),
                    ("mlp_W1", [D, D_MLP]), ("mlp_b1", [D_MLP]),
                    ("mlp_W2", [D_MLP, D]), ("mlp_b2", [D])]:
        l1w[nm] = nc.dram_tensor("L1_attn_" + nm, shp, F32, kind="ExternalInput")
    gw_moe = nc.dram_tensor("L1_ffn_gate_w", [NE, D], F32, kind="ExternalInput")
    w1_moe = nc.dram_tensor("L1_ffn_w1", [NE, F_E, D], F32, kind="ExternalInput")
    w2_moe = nc.dram_tensor("L1_ffn_w2", [NE, D, F_E], F32, kind="ExternalInput")
    out_t = nc.dram_tensor("out", [T, D], F32, kind="ExternalOutput")

    cin = {k: nc.dram_tensor(k, s or [T, 512], dt, kind="ExternalInput")
           for k, (s, dt) in CONST_SPECS.items()}

    dram = lambda n, s, d: nc.dram_tensor(n, s, d, kind="Internal")
    d_xn = dram("d_xn", [T, D], F32)
    d_xn_bf = dram("d_xn_bf", [T, D], BF16)
    d_q0 = dram("d_q0", [D, T], BF16)
    d_k0 = dram("d_k0", [D, T], BF16)
    d_v0 = dram("d_v0", [D, T], BF16)
    d_attn0 = dram("d_attn0", [T, D], F32)
    d_xq = dram("d_xq", [T, D], BF16)
    d_h = dram("d_h", [I_FFN, T], F32)
    d_wg_int = dram("d_wg_int", [I_FFN, D], BF16)
    d_wu_int = dram("d_wu_int", [I_FFN, D], BF16)
    d_wd_int = dram("d_wd_int", [D, I_FFN], BF16)
    d_x1 = dram("d_x1", [T, D], F32)
    d_z = dram("d_z", [T, D], F32)
    d_z_bf = dram("d_z_bf", [T, D], BF16)
    d_zT = dram("d_zT", [D, T], BF16)
    d_zin0 = dram("d_zin0", [T, D], BF16)
    d_p = dram("d_p", [D, T], F32)
    d_pexp0 = dram("d_pexp0", [D, T], BF16)
    d_q1 = dram("d_q1", [T, D], BF16)
    d_k1 = dram("d_k1", [T, D], BF16)
    d_v1 = dram("d_v1", [T, D], BF16)
    d_av = dram("d_av", [T, D], BF16)
    d_x2 = dram("d_x2", [T, D], F32)
    d_h2 = dram("d_h2", [T, D], BF16)
    d_moe = dram("d_moe", [T, D], F32)
    d_w1T = dram("d_w1T", [NE * D, F_E], BF16)
    d_w2T = dram("d_w2T", [NE * F_E, D], BF16)
    d_w1bf = dram("d_w1bf", [F_E, D], BF16)
    d_w2bf = dram("d_w2bf", [D, F_E], BF16)
    d_cc1i = dram("d_cc1i", [1, 16], F32)
    d_cc1o = dram("d_cc1o", [1, 16], F32)
    d_cc2i = dram("d_cc2i", [1, 16], F32)
    d_cc2o = dram("d_cc2o", [1, 16], F32)
    d_R = dram("d_R", [T, NE], F32)
    d_dbg1 = dram("d_dbg1", [512, 512], F32)
    d_dbg2 = dram("d_dbg2", [512, 136], F32)
    d_Rall = dram("d_Rall", [N_ALL, NE], F32)
    rg = [list(range(n_cores))]

    H = dict(T=T, nT=nT, nC=nC, N_ALL=N_ALL, CAP=CAP, rg=rg, x_in=x_in,
             w_qkv=w_qkv, w_out0=w_out0, head_sc=head_sc, clip_x=clip_x,
             clip_dn=clip_dn, w_gate=w_gate, w_up=w_up, w_down=w_down,
             l1w=l1w, gw_moe=gw_moe, w1_moe=w1_moe, w2_moe=w2_moe,
             out_t=out_t, cin=cin, d_xn=d_xn, d_xn_bf=d_xn_bf, d_q0=d_q0,
             d_k0=d_k0, d_v0=d_v0, d_attn0=d_attn0, d_xq=d_xq, d_h=d_h,
             d_wg_int=d_wg_int, d_wu_int=d_wu_int, d_wd_int=d_wd_int,
             d_x1=d_x1, d_z=d_z, d_z_bf=d_z_bf, d_zT=d_zT, d_zin0=d_zin0, d_p=d_p,
             d_pexp0=d_pexp0, d_q1=d_q1, d_k1=d_k1, d_v1=d_v1, d_x2=d_x2,
             d_av=d_av, d_h2=d_h2, d_moe=d_moe, d_w1T=d_w1T, d_w2T=d_w2T,
             d_w1bf=d_w1bf, d_w2bf=d_w2bf, d_cc1i=d_cc1i, d_cc1o=d_cc1o,
             d_cc2i=d_cc2i, d_cc2o=d_cc2o, d_R=d_R, d_Rall=d_Rall,
             d_dbg1=d_dbg1, d_dbg2=d_dbg2)

    with ExitStack() as ST:
        b = Bld(nc, ST)
        C = load_consts(b, ST, H)

        def run(fn, *a):
            b.begin_phase()
            with nc.named_scope(fn.__name__):
                fn(b, *a)
            b.end_phase()

        run(p0_ln0, ST, H, C)
        run(p1a_qkv, ST, H, C)
        run(p1b_attn0, ST, H, C)
        run(p2_quant, ST, H, C)
        run(p3_gateup, ST, H, C)
        run(p4_down_combine, ST, H, C)
        if upto == "l0":
            run(debug_copy, ST, H, d_x1, out_t)
        else:
            run(stage_moe_weights, ST, H, C)
            with ExitStack() as S56:
                H["S56"] = S56
                run(p5_pclock, S56, H, C)
                run(p6_gating, S56, H, C)
            run(p7_attn1, ST, H, C)
            run(p7b_outproj, ST, H, C)
            run(p9_moe, ST, H, C)
            run(p8_mlp_final, ST, H, C)
    return nc


def load_consts(b, ST, H):
    nc, y_, v_ = b.nc, b.y, b.v
    cin = H["cin"]
    sb = lambda n, s, d: ST.enter_context(nc.sbuf_tensor(n, s, d))
    C = {}
    C["ident_bf"] = sb("ident_bf", [128, 128], BF16)
    C["ident_f"] = sb("ident_f", [128, 128], F32)
    C["onesrow"] = sb("onesrow", [1, 128], F32)
    C["ones128"] = sb("ones128", [128, 128], F32)
    C["onescol"] = sb("onescol", [128, 1], F32)
    C["tril"] = sb("tril", [128, 128], F32)
    C["mask_l0"] = sb("mask_l0", [128, 4 * 512], BF16)
    C["maskadd_l1"] = sb("maskadd_l1", [128, 4 * 512], F32)
    C["dirsT"] = sb("dirsT", [3, 8], F32)
    C["dirs_pad"] = sb("dirs_pad", [1, 1024], F32)
    C["eh"] = sb("eh", [128, 64], F32)
    C["hs_row"] = sb("hs_row", [1, 8], F32)
    C["gmax1"] = sb("gmax1", [1, 16], F32)
    C["gmax2"] = sb("gmax2", [1, 16], F32)
    C["scal"] = sb("scal", [128, 16], F32)
    C["bias_h"] = sb("bias_h", [128, 8], F32)
    C["lrec_all"] = sb("lrec_all", [128, 32], F32)
    C["sw_g"] = sb("sw_g", [128, NI], F32)
    C["sw_u"] = sb("sw_u", [128, NI], F32)
    C["sw_d"] = sb("sw_d", [128, 8], F32)
    H["cc1"] = b.sem("cc1")
    H["cc2"] = b.sem("cc2")
    H["ccag"] = b.sem("ccag")
    ld = b.sem("ldc")
    for dst, src in [(C["ident_bf"][:, :], cin["c_ident_bf"][:, :]),
                     (C["ident_f"][:, :], cin["c_ident_f32"][:, :]),
                     (C["onesrow"][:, :], cin["c_onesrow_f32"][:, :]),
                     (C["ones128"][:, :], cin["c_ones128_f32"][:, :]),
                     (C["onescol"][:, :], cin["c_onescol_f32"][:, :]),
                     (C["tril"][:, :], cin["c_tril_incl"][:, :]),
                     (C["dirsT"][:, :], cin["c_dirsT"][:, :]),
                     (C["eh"][:, :], cin["c_eh"][:, :]),
                     (C["dirs_pad"][:, :], cin["c_dirs_pad"][:, :]),
                     (C["hs_row"][:, :], H["head_sc"].ap()[None, :])]:
        ld.inc(y_.dma_start(dst, src), 16)
    for j in range(4):
        ld.inc(y_.dma_start(C["mask_l0"][:, j * 512:(j + 1) * 512],
                            cin["c_mask_l0"][j * 128:(j + 1) * 128, :]), 16)
        ld.inc(y_.dma_start(C["maskadd_l1"][:, j * 512:(j + 1) * 512],
                            cin["c_maskadd_l1"][j * 128:(j + 1) * 128, :]), 16)
    ld.wait(v_)
    C["ld_const"] = ld
    return C


def ln_rowstats(b, st1, xin, c_mean, c_sq, c_var, c_rstd, c_tmp, sqd):
    v_, s_ = b.v, b.s
    b.wv(s_)
    b.vm(v_.reduce_sum(st1[:, c_mean:c_mean + 1], xin[:, :], axis=AX))
    b.sm(s_.activation(sqd[:, :], xin[:, :], AF.Square,
                       accum_out=st1[:, c_sq:c_sq + 1]))
    b.wv(v_)
    b.ws(v_)
    b.vm(v_.tensor_scalar_mul(st1[:, c_mean:c_mean + 1], st1[:, c_mean:c_mean + 1], 1.0 / D))
    b.wv(v_)
    b.vm(v_.tensor_mul(st1[:, c_var:c_var + 1], st1[:, c_mean:c_mean + 1],
                       st1[:, c_mean:c_mean + 1]))
    b.vm(v_.tensor_scalar(st1[:, c_tmp:c_tmp + 1], st1[:, c_sq:c_sq + 1],
                          1.0 / D, None, AL.mult))
    b.wv(v_)
    b.vm(v_.tensor_sub(st1[:, c_var:c_var + 1], st1[:, c_tmp:c_tmp + 1],
                       st1[:, c_var:c_var + 1]))
    b.wv(v_)
    b.vm(v_.tensor_scalar_add(st1[:, c_var:c_var + 1], st1[:, c_var:c_var + 1], EPS_LN))
    b.wv(s_)
    b.sm(s_.activation(st1[:, c_rstd:c_rstd + 1], st1[:, c_var:c_var + 1], AF.Sqrt))
    b.ws(v_)
    b.vm(v_.reciprocal(st1[:, c_rstd:c_rstd + 1], st1[:, c_rstd:c_rstd + 1]))
    b.wv(v_)
    b.vm(v_.tensor_mul(st1[:, c_tmp:c_tmp + 1], st1[:, c_rstd:c_rstd + 1],
                       st1[:, c_rstd:c_rstd + 1]))
    b.wv(v_)
    b.vm(v_.tensor_mul(st1[:, c_tmp:c_tmp + 1], st1[:, c_tmp:c_tmp + 1],
                       st1[:, c_var:c_var + 1]))
    b.wv(v_)
    b.vm(v_.tensor_scalar(st1[:, c_tmp:c_tmp + 1], st1[:, c_tmp:c_tmp + 1],
                          -0.5, 1.5, AL.mult, AL.add))
    b.wv(v_)
    b.vm(v_.tensor_mul(st1[:, c_rstd:c_rstd + 1], st1[:, c_rstd:c_rstd + 1],
                       st1[:, c_tmp:c_tmp + 1]))
    b.wv(v_)


def p0_ln0(b, ST, H, C):
    nc, t_, v_, s_, g_, y_ = b.nc, b.t, b.v, b.s, b.g, b.y
    T, nT = H["T"], H["nT"]
    with ExitStack() as S:
        sb = lambda n, s, d: S.enter_context(nc.sbuf_tensor(n, s, d))
        xt = [sb("p0x%d" % i, [128, D], F32) for i in range(2)]
        xn = [sb("p0n%d" % i, [128, D], F32) for i in range(2)]
        sqd = sb("p0sq", [128, D], F32)
        st1 = sb("p0st", [128, 8], F32)
        ps_tr = S.enter_context(nc.psum_tensor("p0ps", [1, 128], F32))
        ldb = DBuf(b, y_, 2, "p0ld")
        stb = DBuf(b, g_, 2, "p0st")
        b.vm(v_.memset(st1[:, 5:6], -1.0))
        for i in range(nT + 1):
            if i < nT:
                ldb.gate(i)
                ldb.slot(i).inc(y_.dma_start(xt[i % 2][:, :],
                                             H["x_in"][i * 128:(i + 1) * 128, :]), 16)
            if i == 0:
                continue
            j = i - 1
            xj, nj = xt[j % 2], xn[j % 2]
            ldb.wait(j, v_)
            ldb.wait(j, s_)
            if j >= 2:
                stb.sems[j % 2].wait(v_)
            ln_rowstats(b, st1, xj, 0, 1, 2, 3, 4, sqd)
            b.vm(v_.tensor_scalar(nj[:, :], xj[:, :], st1[:, 0:1], st1[:, 3:4],
                                  AL.subtract, AL.mult))
            b.wv(v_)
            b.vm(v_.reduce_max(st1[:, 4:5], nj[:, :], axis=AX,
                               apply_absolute_value=True))
            b.wv(v_)
            b.vm(v_.tensor_max(st1[:, 5:6], st1[:, 5:6], st1[:, 4:5]))
            stb.gate(j)
            b.wv(g_)
            stb.slot(j).inc(g_.dma_start(H["d_xn"][j * 128:(j + 1) * 128, :], nj[:, :]), 16)
            stb.slot(j).inc(g_.dma_start(H["d_xn_bf"][j * 128:(j + 1) * 128, :], nj[:, :]), 16)
            ldb.done(j, b.cv)
        b.wv(t_)
        b.tm(t_.matmul(ps_tr[:, :], st1[:, 5:6], C["ident_f"][:, :],
                       start=True, stop=True))
        b.wt(v_)
        b.vm(v_.memset(C["gmax1"][:, :], 0.0))
        b.wv(v_)
        b.vm(v_.reduce_max(C["gmax1"][:, 0:1], ps_tr[:, :], axis=AX))
        b.wv(y_)
        stp = b.sem("p0cc")
        stp.inc(y_.dma_start(H["d_cc1i"][:, :], C["gmax1"][:, :]), 16)
        stp.wait(g_)
        b.wg(g_)
        cc1 = H["cc1"]
        cc1.inc(g_.collective_compute("AllReduce", AL.max, replica_groups=H["rg"],
                                      ins=[H["d_cc1i"].ap()], outs=[H["d_cc1o"].ap()]))
        H["p0_stb"] = stb


def p1a_qkv(b, ST, H, C):
    nc, t_, v_, s_, g_, y_ = b.nc, b.t, b.v, b.s, b.g, b.y
    T, nC = H["T"], H["nC"]
    with ExitStack() as S:
        sb = lambda n, s, d: S.enter_context(nc.sbuf_tensor(n, s, d))
        wq = sb("wqkv", [128, 8 * 3072], BF16)
        xnf = [sb("xnf%d" % i, [128, 8 * 512], BF16) for i in range(2)]
        evb = sb("evb", [128, 3 * 512], BF16)
        tmpf = sb("tmpf", [128, 512], F32)
        sqf = sb("sqf", [128, 2 * 512], F32)
        kpr = sb("kpr", [1, 512], F32)
        kf32 = sb("kf32", [128, 512], F32)
        qn2 = sb("qn2", [8, nC * 512], F32)
        kn2 = sb("kn2", [8, nC * 512], F32)
        nrm = sb("nrm", [8, 4], F32)
        negb = sb("negb", [1, 8], F32)
        bias_h = C["bias_h"]
        ps_a = [S.enter_context(nc.psum_tensor("p1a%d" % i, [128, 512], F32)) for i in range(2)]
        ps_q = S.enter_context(nc.psum_tensor("p1q", [8, 512], F32))
        ps_k = S.enter_context(nc.psum_tensor("p1k", [8, 512], F32))
        ps_kp = S.enter_context(nc.psum_tensor("p1kp", [1, 512], F32))
        ps_b = S.enter_context(nc.psum_tensor("p1bb", [128, 8], F32))
        ps_geo = S.enter_context(nc.psum_tensor("p1geo", [128, 512], F32))

        ldw = b.sem("p1_ldw")
        for cc in range(8):
            ldw.inc(g_.dma_start(wq[:, cc * 3072:(cc + 1) * 3072],
                                 H["w_qkv"][cc * 128:(cc + 1) * 128, :]), 16)
        ldx = DBuf(b, y_, 2, "p1ldx")
        stq = DBuf(b, g_, 3, "p1st")
        ldw.wait(t_)
        b.wv(s_)
        ps_free = [None, None]
        pending = []

        def q_epi(j, h, ps, ev, idx, sq):
            def emit():
                stq.sems[idx % 3].wait(s_)
                b.wt(s_)
                b.sm(s_.copy(ev, ps[:, :]))
                b.sm(s_.activation(sq, ps[:, :], AF.Square))
                ps_free[idx % 2] = (b.cs, b.cs.n)
                stq.gate(idx)
                b.ws(g_)
                stq.slot(idx).inc(g_.dma_start(
                    H["d_q0"][h * 128:(h + 1) * 128, j * 512:(j + 1) * 512], ev), 16)
                return ("stat", ps_q, h, sq)
            return emit

        def k_epi(j, h, ps, ev, idx, sq):
            def emit():
                stq.sems[idx % 3].wait(s_)
                b.wt(s_)
                b.sm(s_.copy(tmpf[:, :], ps[:, :]))
                b.ws(t_)
                b.tm(t_.matmul(ps_kp[:, :], C["dirsT"][0:3, h:h + 1],
                               tmpf[0:3, :], start=True, stop=True))
                b.wt(s_)
                b.sm(s_.activation(kpr[:, :], ps_kp[:, :], AF.Identity,
                                   scale=C["hs_row"][0:1, h:h + 1]))
                b.ws(t_)
                b.wv(t_)
                b.tm(t_.matmul(ps_geo[:, :],
                               C["dirs_pad"][0:1, h * 128:(h + 1) * 128],
                               kpr[:, :], start=True, stop=True))
                b.wt(v_)
                b.ws(v_)
                b.vm(v_.tensor_scalar(kf32[:, :], ps[:, :], 1.0 / RT128,
                                      None, AL.mult))
                b.wv(v_)
                b.vm(v_.tensor_add(kf32[:, :], kf32[:, :], ps_geo[:, :]))
                b.wv(s_)
                b.sm(s_.copy(ev, kf32[:, :]))
                b.sm(s_.activation(sq, kf32[:, :], AF.Square))
                ps_free[idx % 2] = (b.cs, b.cs.n)
                stq.gate(idx)
                b.ws(g_)
                stq.slot(idx).inc(g_.dma_start(
                    H["d_k0"][h * 128:(h + 1) * 128, j * 512:(j + 1) * 512], ev), 16)
                return ("stat", ps_k, h, sq)
            return emit

        def v_epi(j, h, ps, ev, idx):
            def emit():
                stq.sems[idx % 3].wait(s_)
                b.wt(s_)
                b.sm(s_.copy(ev, ps[:, :]))
                ps_free[idx % 2] = (b.cs, b.cs.n)
                stq.gate(idx)
                b.ws(g_)
                stq.slot(idx).inc(g_.dma_start(
                    H["d_v0"][h * 128:(h + 1) * 128, j * 512:(j + 1) * 512], ev), 16)
                return None
            return emit

        pend_stat = []

        def flush_one():
            if not pending:
                return
            emit = pending.pop(0)
            r = emit()
            if r is not None:
                _, pst, h, sq = r
                pend_stat.append((pst, h, sq, b.cs.n))

        def flush_stats():
            while pend_stat:
                pst, h, sq, cnt = pend_stat.pop(0)
                t_.wait_ge(b.cs.h, cnt)
                b.tm(t_.matmul(pst[:, :], C["eh"][:, h * 8:(h + 1) * 8],
                               sq, start=(h == 0), stop=(h == 7)))

        for i in range(nC + 1):
            if i < nC:
                ldx.gate(i)
                for cc in range(8):
                    ldx.slot(i).inc(y_.dma_start(
                        xnf[i % 2][:, cc * 512:(cc + 1) * 512],
                        H["d_xn_bf"][i * 512:(i + 1) * 512, cc * 128:(cc + 1) * 128],
                        transpose=True), 16)
            if i == 0:
                continue
            j = i - 1
            xf = xnf[j % 2]
            ldx.wait(j, t_)
            for co in range(24):
                idx = j * 24 + co
                pb = co % 2
                ps = ps_a[pb]
                h = co % 8
                if ps_free[pb] is not None:
                    sem, cnt = ps_free[pb]
                    t_.wait_ge(sem.h, cnt)
                last = None
                for cc in range(8):
                    last = t_.matmul(
                        ps[:, :],
                        wq[:, cc * 3072 + co * 128: cc * 3072 + (co + 1) * 128],
                        xf[:, cc * 512:(cc + 1) * 512],
                        start=(cc == 0), stop=(cc == 7))
                b.tm(last)
                flush_stats()
                ev = evb[:, (idx % 3) * 512:(idx % 3 + 1) * 512]
                sq = sqf[:, (co % 2) * 512:(co % 2 + 1) * 512]
                if co < 8:
                    pending.append(q_epi(j, h, ps, ev, idx, sq))
                elif co < 16:
                    pending.append(k_epi(j, h, ps, ev, idx, sq))
                else:
                    pending.append(v_epi(j, h, ps, ev, idx))
                flush_one()
            while pending:
                flush_one()
            flush_stats()
            b.wt(v_)
            b.vm(v_.tensor_copy(qn2[:, j * 512:(j + 1) * 512], ps_q[0:8, :]))
            b.vm(v_.tensor_copy(kn2[:, j * 512:(j + 1) * 512], ps_k[0:8, :]))
            b.wv(t_)
            ldx.done(j, b.ct)
        b.wv(v_)
        b.vm(v_.reduce_max(nrm[:, 0:1], qn2[0:8, :], axis=AX))
        b.vm(v_.reduce_max(nrm[:, 1:2], kn2[0:8, :], axis=AX))
        b.wv(v_)
        b.vm(v_.tensor_mul(nrm[:, 2:3], nrm[:, 0:1], nrm[:, 1:2]))
        b.wv(s_)
        b.sm(s_.activation(nrm[:, 3:4], nrm[:, 2:3], AF.Sqrt))
        b.ws(t_)
        b.tm(t_.matmul(ps_kp[:, 0:8], nrm[0:8, 3:4], C["ident_f"][0:8, 0:8],
                       start=True, stop=True))
        b.wt(s_)
        b.sm(s_.activation(negb[:, :], ps_kp[:, 0:8], AF.Identity, scale=-1.01))
        b.ws(v_)
        b.vm(v_.tensor_scalar_add(negb[:, :], negb[:, :], 60.0))
        b.wv(t_)
        b.tm(t_.matmul(ps_b[:, :], C["onesrow"][:, :], negb[:, :],
                       start=True, stop=True))
        b.wt(v_)
        b.vm(v_.tensor_copy(bias_h[:, :], ps_b[:, :]))
        H["bias_h"] = bias_h
        H["p1_stq"] = stq


def p1b_attn0(b, ST, H, C):
    nc, t_, v_, s_, g_, y_ = b.nc, b.t, b.v, b.s, b.g, b.y
    T, nT = H["T"], H["nT"]
    nG = T // 512
    with ExitStack() as S:
        sb = lambda n, s, d: S.enter_context(nc.sbuf_tensor(n, s, d))
        k0sb = sb("k0sb", [128, 8 * T], BF16)
        vsb = sb("vsb", [128, 8 * nT * 144], BF16)
        q0g = [sb("q0g%d" % i, [128, 8 * 512], BF16) for i in range(2)]
        pT = sb("pT", [128, 2 * 512], BF16)
        ongU = sb("ongU", [128, 32 * 144], BF16)
        outf = sb("outf", [128, 8 * 128], BF16)
        lall = sb("lall", [128, 8], F32)
        lrec = sb("lrec", [128, 8], F32)
        rscr = sb("rscr", [128, 16], F32)
        kint = S.enter_context(nc.sbuf_tensor("kint", [128, 8], mybir.dt.int32))
        w0 = sb("w0sb", [128, 8 * 1024], BF16)
        oev = sb("oev", [128, 1024], F32)
        ps_sc = [S.enter_context(nc.psum_tensor("pbs%d" % i, [128, 512], F32)) for i in range(2)]
        ps_av = [S.enter_context(nc.psum_tensor("pba%d" % i, [128, 132], F32)) for i in range(4)]
        ps_tb = S.enter_context(nc.psum_tensor("pbt", [128, 128], BF16))
        ps_o = S.enter_context(nc.psum_tensor("pbo", [128, 512], F32))
        b.vm(b.v.memset(kint[:, :], 0x7EF311C3))
        ldr = b.sem("p1b_res")
        for h in range(8):
            ldr.inc(y_.dma_start(k0sb[:, h * T:(h + 1) * T],
                                 H["d_k0"][h * 128:(h + 1) * 128, :]), 16)
        for h in range(8):
            for kt in range(nT):
                ldr.inc(y_.dma_start(
                    vsb[:, (h * nT + kt) * 144:(h * nT + kt) * 144 + 128],
                    H["d_v0"][h * 128:(h + 1) * 128, kt * 128:(kt + 1) * 128],
                    transpose=True), 16)
        ldw0 = b.sem("p1b_w0")
        for cc in range(8):
            ldw0.inc(g_.dma_start(w0[:, cc * 1024:(cc + 1) * 1024],
                                  H["w_out0"][cc * 128:(cc + 1) * 128, :]), 16)
        b.vm(v_.memset(bass.AP(vsb, 128, [[vsb.shape[1], 128], [144, 8 * nT], [1, 1]]), 1.0))
        ldq = DBuf(b, y_, 2, "p1b_q")
        stA = DBuf(b, g_, 2, "p1b_st")
        ldr.wait(t_)
        ldw0.wait(t_)
        b.wv(t_)
        b.wv(s_)
        sc_free = [None, None]
        pT_free = [None, None]
        av_free = None

        for g in range(nG + 1):
            if g < nG:
                ldq.gate(g)
                for h in range(8):
                    ldq.slot(g).inc(y_.dma_start(
                        q0g[g % 2][:, h * 512:(h + 1) * 512],
                        H["d_q0"][h * 128:(h + 1) * 128, g * 512:(g + 1) * 512]), 16)
            if g == 0:
                continue
            gg = g - 1
            qs = q0g[gg % 2]
            ktmax = 4 * gg + 3
            ldq.wait(gg, t_)
            for h in range(8):
                if av_free is not None:
                    t_.wait_ge(b.cs.h, av_free)
                pending_av = []

                def emit_av(kt, sl, vcnt, scnt):
                    j = kt - 4 * gg
                    qlo = max(0, j)
                    if j >= 0:
                        t_.wait_ge(b.cv.h, vcnt)
                    else:
                        t_.wait_ge(b.cs.h, scnt)
                    for qi in range(qlo, 4):
                        b.tm(t_.matmul(
                            ps_av[qi][:, 0:129],
                            pT[:, sl * 512 + qi * 128: sl * 512 + (qi + 1) * 128],
                            vsb[:, (h * nT + kt) * 144:(h * nT + kt) * 144 + 129],
                            start=(kt == 0), stop=(kt == 4 * gg + qi)))

                for kt in range(ktmax + 1):
                    sl = kt % 2
                    if sc_free[sl] is not None:
                        t_.wait_ge(b.cs.h, sc_free[sl])
                    b.tm(t_.matmul(ps_sc[sl][:, :],
                                   k0sb[:, h * T + kt * 128: h * T + (kt + 1) * 128],
                                   qs[:, h * 512:(h + 1) * 512],
                                   start=True, stop=True))
                    if len(pending_av) >= 2:
                        pa = pending_av.pop(0)
                        emit_av(*pa)
                        pT_free[pa[1]] = b.ct.n
                    b.wt(s_)
                    if pT_free[sl] is not None:
                        s_.wait_ge(b.ct.h, pT_free[sl])
                    b.sm(s_.activation(pT[:, sl * 512:(sl + 1) * 512],
                                       ps_sc[sl][:, :], AF.Exp,
                                       bias=H["bias_h"][:, h:h + 1]))
                    sc_free[sl] = b.cs.n
                    j = kt - 4 * gg
                    if j >= 0:
                        b.ws(v_)
                        b.vm(v_.tensor_mul(pT[:, sl * 512:(sl + 1) * 512],
                                           pT[:, sl * 512:(sl + 1) * 512],
                                           C["mask_l0"][:, j * 512:(j + 1) * 512]))
                    pending_av.append((kt, sl, b.cv.n, b.cs.n))
                while pending_av:
                    pa = pending_av.pop(0)
                    emit_av(*pa)
                    pT_free[pa[1]] = b.ct.n
                b.wt(s_)
                for qi in range(4):
                    b.sm(s_.copy(ongU[:, (qi * 8 + h) * 144: (qi * 8 + h) * 144 + 129],
                                 ps_av[qi][:, 0:129]))
                av_free = b.cs.n
            b.ws(v_)
            b.wt(v_)
            for qi in range(4):
                lv = bass.AP(ongU, (qi * 8) * 144 + 128,
                             [[32 * 144, 128], [144, 8]])
                b.vm(v_.tensor_copy(lall[:, :], lv))
                b.wv(v_)
                safe_recip(b, lrec[:, 0:8], lall[:, 0:8], rscr, kint[:, 0:8], W=8)
                for h in range(8):
                    b.vm(v_.tensor_scalar(
                        ongU[:, (qi * 8 + h) * 144: (qi * 8 + h) * 144 + 128],
                        ongU[:, (qi * 8 + h) * 144: (qi * 8 + h) * 144 + 128],
                        lrec[:, h:h + 1], None, AL.mult))
                b.wv(v_)
            b.wv(t_)
            for qi in range(4):
                qt = 4 * gg + qi
                for cc in range(8):
                    b.tm(t_.transpose(ps_tb[:, :],
                                      ongU[:, (qi * 8 + cc) * 144: (qi * 8 + cc) * 144 + 128],
                                      C["ident_bf"][:, :]))
                    b.wt(s_)
                    b.sm(s_.copy(outf[:, cc * 128:(cc + 1) * 128], ps_tb[:, :]))
                    b.ws(t_)
                osl = 0
                stA.wait_all(v_)
                for hh in range(2):
                    for cc in range(8):
                        b.tm(t_.matmul(ps_o[:, :], outf[:, cc * 128:(cc + 1) * 128],
                                       w0[:, cc * 1024 + hh * 512: cc * 1024 + (hh + 1) * 512],
                                       start=(cc == 0), stop=(cc == 7)))
                    b.wt(v_)
                    b.vm(v_.tensor_copy(oev[:, osl + hh * 512: osl + (hh + 1) * 512],
                                        ps_o[:, :]))
                    b.wv(t_)
                stA.gate(qt)
                b.wv(g_)
                stA.slot(qt).inc(g_.dma_start(
                    H["d_attn0"][qt * 128:(qt + 1) * 128, :],
                    oev[:, 0:1024]), 16)
            ldq.done(gg, b.ct)
        H["p1b_stA"] = stA


def round_clip(b, out_bf, xin, rscale_ap, lo, hi, tmp):
    v_ = b.v
    b.vm(v_.tensor_scalar(tmp[:, :], xin[:, :], rscale_ap, MAGIC, AL.mult, AL.add))
    b.wv(v_)
    b.vm(v_.tensor_scalar_add(tmp[:, :], tmp[:, :], -MAGIC))
    b.wv(v_)
    b.vm(v_.tensor_scalar(out_bf[:, :], tmp[:, :], hi, lo, AL.min, AL.max))
    b.wv(v_)


def p2_quant(b, ST, H, C):
    nc, t_, v_, s_, g_, y_ = b.nc, b.t, b.v, b.s, b.g, b.y
    T, nT = H["T"], H["nT"]
    scal = C["scal"]
    with ExitStack() as S:
        sb = lambda n, s, d: S.enter_context(nc.sbuf_tensor(n, s, d))
        p2s = sb("p2s", [1, 8], F32)
        xl = [sb("p2x%d" % i, [128, D], F32) for i in range(2)]
        tmp = sb("p2t", [128, D], F32)
        qb = [sb("p2q%d" % i, [128, D], BF16) for i in range(2)]
        ps_c = S.enter_context(nc.psum_tensor("p2ps", [128, 2], F32))
        ldc = b.sem("p2ld")
        ldc.inc(y_.dma_start(C["gmax1"][:, :], H["d_cc1o"][:, :]), 16)
        ldc.inc(y_.dma_start(p2s[0:1, 0:1], H["clip_x"].ap()[None, :]), 16)
        ldc.wait(s_)
        b.sm(s_.activation(p2s[:, 1:2], p2s[:, 0:1], AF.Sigmoid))
        b.ws(v_)
        b.vm(v_.tensor_mul(p2s[:, 2:3], p2s[:, 1:2], C["gmax1"][0:1, 0:1]))
        b.wv(v_)
        b.vm(v_.tensor_scalar_mul(p2s[:, 3:4], p2s[:, 2:3], 1.0 / QMAX))
        b.wv(v_)
        b.vm(v_.reciprocal(p2s[:, 2:3], p2s[:, 3:4]))
        b.wv(t_)
        b.tm(t_.matmul(ps_c[:, :], C["onesrow"][:, :], p2s[:, 2:4],
                       start=True, stop=True))
        b.wt(v_)
        b.vm(v_.tensor_copy(scal[:, 0:2], ps_c[:, :]))
        b.wv(v_)
        ldb = DBuf(b, y_, 2, "p2ldx")
        stb = DBuf(b, y_, 2, "p2st")
        for i in range(nT + 1):
            if i < nT:
                ldb.gate(i)
                ldb.slot(i).inc(y_.dma_start(xl[i % 2][:, :],
                                             H["d_xn"][i * 128:(i + 1) * 128, :]), 16)
            if i == 0:
                continue
            j = i - 1
            ldb.wait(j, v_)
            stb.sems[j % 2].wait(v_)
            round_clip(b, qb[j % 2], xl[j % 2], scal[:, 0:1], -QMAX - 1.0, QMAX, tmp)
            stb.gate(j)
            b.wv(y_)
            stb.slot(j).inc(y_.dma_start(H["d_xq"][j * 128:(j + 1) * 128, :],
                                         qb[j % 2][:, :]), 16)
            ldb.done(j, b.cv)
        H["p2_stb"] = stb
    b.full_barrier()
    sw_g = C["sw_g"]
    sw_u = C["sw_u"]
    sw_d = C["sw_d"]
    H["sw_g"], H["sw_u"], H["sw_d"] = sw_g, sw_u, sw_d
    stw = DBuf(b, y_, 2, "p2wst")
    with ExitStack() as S:
        sb = lambda n, s, d: S.enter_context(nc.sbuf_tensor(n, s, d))
        wl = [sb("p2w%d" % i, [128, I_FFN], F32) for i in range(2)]
        wq = [sb("p2wq%d" % i, [128, I_FFN], BF16) for i in range(2)]
        wt = sb("p2wt", [128, I_FFN], F32)
        am = sb("p2am", [128, 4], F32)
        ldb = DBuf(b, y_, 2, "p2wld")
        jobs = []
        for rc in range(NI):
            jobs.append((H["w_gate"], rc, D, H["d_wg_int"], sw_g, rc))
        for rc in range(NI):
            jobs.append((H["w_up"], rc, D, H["d_wu_int"], sw_u, rc))
        for rc in range(8):
            jobs.append((H["w_down"], rc, I_FFN, H["d_wd_int"], sw_d, rc))
        for i in range(len(jobs) + 1):
            if i < len(jobs):
                src, rc, W, dst, swt, scol = jobs[i]
                ldb.gate(i)
                ldb.slot(i).inc(y_.dma_start(wl[i % 2][:, 0:W],
                                             src[rc * 128:(rc + 1) * 128, :]), 16)
            if i == 0:
                continue
            j = i - 1
            src, rc, W, dst, swt, scol = jobs[j]
            ldb.wait(j, v_)
            stb_s = stw.sems[j % 2]
            stb_s.wait(v_)
            b.vm(v_.reduce_max(am[:, 0:1], wl[j % 2][:, 0:W], axis=AX,
                               apply_absolute_value=True))
            b.wv(v_)
            b.vm(v_.reciprocal(am[:, 1:2], am[:, 0:1]))
            b.vm(v_.tensor_scalar_mul(swt[:, scol:scol + 1], am[:, 0:1], 1.0 / QMAX))
            b.wv(v_)
            b.vm(v_.tensor_scalar_mul(am[:, 1:2], am[:, 1:2], QMAX))
            b.wv(v_)
            b.vm(v_.tensor_scalar(wt[:, 0:W], wl[j % 2][:, 0:W], am[:, 1:2],
                                  MAGIC, AL.mult, AL.add))
            b.wv(v_)
            b.vm(v_.tensor_scalar_add(wt[:, 0:W], wt[:, 0:W], -MAGIC))
            b.wv(v_)
            b.vm(v_.tensor_scalar(wq[j % 2][:, 0:W], wt[:, 0:W], QMAX,
                                  -QMAX - 1.0, AL.min, AL.max))
            stw.gate(j)
            b.wv(y_)
            stw.slot(j).inc(y_.dma_start(dst[rc * 128:(rc + 1) * 128, :],
                                         wq[j % 2][:, 0:W]), 16)
            ldb.done(j, b.cv)
    H["p2_stw"] = stw


def p3_gateup(b, ST, H, C):
    nc, t_, v_, s_, g_, y_ = b.nc, b.t, b.v, b.s, b.g, b.y
    T, nC = H["T"], H["nC"]
    scal = C["scal"]
    with ExitStack() as S:
        sb = lambda n, s, d: S.enter_context(nc.sbuf_tensor(n, s, d))
        wgT = sb("wgT", [128, 8 * I_FFN], BF16)
        wuT = sb("wuT", [128, 8 * I_FFN], BF16)
        xqf = [sb("xqf%d" % i, [128, 8 * 512], BF16) for i in range(2)]
        sg = sb("p3sg", [128, 512], F32)
        su = sb("p3su", [128, 512], F32)
        hb = [sb("p3h%d" % i, [128, 512], F32) for i in range(2)]
        swe = sb("p3swe", [128, 2 * NI], F32)
        hmax = sb("p3hm", [128, 2], F32)
        ps_g = [S.enter_context(nc.psum_tensor("p3g%d" % i, [128, 512], F32)) for i in range(2)]
        ps_u = [S.enter_context(nc.psum_tensor("p3u%d" % i, [128, 512], F32)) for i in range(2)]
        ps_t = S.enter_context(nc.psum_tensor("p3t", [1, 128], F32))
        ldr = b.sem("p3res")
        for cc in range(8):
            ldr.inc(y_.dma_start(wgT[:, cc * I_FFN:(cc + 1) * I_FFN],
                                 H["d_wg_int"][:, cc * 128:(cc + 1) * 128],
                                 transpose=True), 16)
            ldr.inc(y_.dma_start(wuT[:, cc * I_FFN:(cc + 1) * I_FFN],
                                 H["d_wu_int"][:, cc * 128:(cc + 1) * 128],
                                 transpose=True), 16)
        b.vm(v_.tensor_scalar(swe[:, 0:NI], H["sw_g"][:, :], scal[:, 1:2],
                              None, AL.mult))
        b.vm(v_.tensor_scalar(swe[:, NI:2 * NI], H["sw_u"][:, :], scal[:, 1:2],
                              None, AL.mult))
        b.vm(v_.memset(hmax[:, 0:1], 0.0))
        b.wv(s_)
        ldx = DBuf(b, y_, 2, "p3ldx")
        sth = DBuf(b, y_, 2, "p3sth")
        ldr.wait(t_)
        ps_free_g = [None, None]
        ps_free_u = [None, None]
        for i in range(nC + 1):
            if i < nC:
                ldx.gate(i)
                for cc in range(8):
                    ldx.slot(i).inc(y_.dma_start(
                        xqf[i % 2][:, cc * 512:(cc + 1) * 512],
                        H["d_xq"][i * 512:(i + 1) * 512, cc * 128:(cc + 1) * 128],
                        transpose=True), 16)
            if i == 0:
                continue
            j = i - 1
            xf = xqf[j % 2]
            ldx.wait(j, t_)
            for ic in range(NI):
                pb = ic % 2
                for psl, wT, free in ((ps_g[pb], wgT, ps_free_g),
                                      (ps_u[pb], wuT, ps_free_u)):
                    if free[pb] is not None:
                        sem, cnt = free[pb]
                        t_.wait_ge(sem.h, cnt)
                    last = None
                    for cc in range(8):
                        last = t_.matmul(
                            psl[:, :],
                            wT[:, cc * I_FFN + ic * 128: cc * I_FFN + (ic + 1) * 128],
                            xf[:, cc * 512:(cc + 1) * 512],
                            start=(cc == 0), stop=(cc == 7))
                    b.tm(last)
                b.wt(s_)
                b.wv(s_)
                b.sm(s_.activation(sg[:, :], ps_g[pb][:, :], AF.Silu,
                                   scale=swe[:, ic:ic + 1]))
                b.sm(s_.activation(su[:, :], ps_u[pb][:, :], AF.Identity,
                                   scale=swe[:, NI + ic:NI + ic + 1]))
                ps_free_g[pb] = (b.cs, b.cs.n)
                ps_free_u[pb] = (b.cs, b.cs.n)
                hsl = (j * NI + ic) % 2
                sth.sems[hsl].wait(v_)
                b.ws(v_)
                b.vm(v_.tensor_mul(hb[hsl][:, :], sg[:, :], su[:, :]))
                b.wv(v_)
                b.vm(v_.reduce_max(hmax[:, 1:2], hb[hsl][:, :], axis=AX,
                                   apply_absolute_value=True))
                b.wv(v_)
                b.vm(v_.tensor_max(hmax[:, 0:1], hmax[:, 0:1], hmax[:, 1:2]))
                sth.gate(j * NI + ic)
                b.wv(y_)
                sth.slot(j * NI + ic).inc(y_.dma_start(
                    H["d_h"][ic * 128:(ic + 1) * 128, j * 512:(j + 1) * 512],
                    hb[hsl][:, :]), 16)
            ldx.done(j, b.ct)
        b.wv(t_)
        b.tm(t_.matmul(ps_t[:, :], hmax[:, 0:1], C["ident_f"][:, :],
                       start=True, stop=True))
        b.wt(v_)
        b.vm(v_.memset(C["gmax2"][:, :], 0.0))
        b.wv(v_)
        b.vm(v_.reduce_max(C["gmax2"][:, 0:1], ps_t[:, :], axis=AX))
        b.wv(y_)
        stc = b.sem("p3cc")
        stc.inc(y_.dma_start(H["d_cc2i"][:, :], C["gmax2"][:, :]), 16)
        stc.wait(g_)
        cc2 = H["cc2"]
        cc2.inc(g_.collective_compute("AllReduce", AL.max, replica_groups=H["rg"],
                                      ins=[H["d_cc2i"].ap()], outs=[H["d_cc2o"].ap()]))
        H["p3_sth"] = sth


def p4_down_combine(b, ST, H, C):
    nc, t_, v_, s_, g_, y_ = b.nc, b.t, b.v, b.s, b.g, b.y
    T = H["T"]
    CH = 256
    nCH = T // CH
    scal = C["scal"]
    with ExitStack() as S:
        sb = lambda n, s, d: S.enter_context(nc.sbuf_tensor(n, s, d))
        p4s = sb("p4s", [1, 8], F32)
        wdT = sb("wdT", [128, NI * 1024], BF16)
        hld = [sb("p4h%d" % i, [128, NI * CH], F32) for i in range(2)]
        hq = [sb("p4hq%d" % i, [128, NI * CH], BF16) for i in range(2)]
        yf = sb("p4yf", [128, CH], F32)
        ycomb = sb("p4yc", [128, 2 * 1024], F32)
        xld = [sb("p4x%d" % i, [128, D], F32) for i in range(2)]
        ald = [sb("p4a%d" % i, [128, D], F32) for i in range(2)]
        x1t = sb("p4x1t", [128, D], F32)
        zt = sb("p4z", [128, D], F32)
        zi = sb("p4zi", [128, D], F32)
        sqd = sb("p4sq", [128, D], F32)
        st1 = sb("p4st", [128, 8], F32)
        swe = sb("p4swe", [128, 8], F32)
        ps_d = [S.enter_context(nc.psum_tensor("p4d%d" % i, [128, CH], F32)) for i in range(2)]
        ps_tf = [S.enter_context(nc.psum_tensor("p4tf%d" % i, [128, 128], F32)) for i in range(2)]
        ps_c = S.enter_context(nc.psum_tensor("p4pc", [128, 2], F32))
        ldc = b.sem("p4ld")
        ldc.inc(y_.dma_start(C["gmax2"][:, :], H["d_cc2o"][:, :]), 16)
        ldc.inc(y_.dma_start(p4s[0:1, 0:1], H["clip_dn"].ap()[None, :]), 16)
        ldc.wait(s_)
        b.sm(s_.activation(p4s[:, 1:2], p4s[:, 0:1], AF.Sigmoid))
        b.ws(v_)
        b.vm(v_.tensor_mul(p4s[:, 2:3], p4s[:, 1:2], C["gmax2"][0:1, 0:1]))
        b.wv(v_)
        b.vm(v_.tensor_scalar_mul(p4s[:, 3:4], p4s[:, 2:3], 1.0 / QMAX))
        b.wv(v_)
        b.vm(v_.reciprocal(p4s[:, 2:3], p4s[:, 3:4]))
        b.wv(t_)
        b.tm(t_.matmul(ps_c[:, :], C["onesrow"][:, :], p4s[:, 2:4],
                       start=True, stop=True))
        b.wt(v_)
        b.vm(v_.tensor_copy(scal[:, 2:4], ps_c[:, :]))
        b.wv(v_)
        b.vm(v_.tensor_scalar(swe[:, :], H["sw_d"][:, :], scal[:, 3:4],
                              None, AL.mult))
        ldr = b.sem("p4res")
        for ic in range(NI):
            ldr.inc(y_.dma_start(wdT[:, ic * 1024:(ic + 1) * 1024],
                                 H["d_wd_int"][:, ic * 128:(ic + 1) * 128],
                                 transpose=True), 16)
        ldh = DBuf(b, y_, 2, "p4ldh")
        ldxa = DBuf(b, y_, 2, "p4ldxa")
        stz = DBuf(b, g_, 2, "p4stz")
        st1b = DBuf(b, y_, 2, "p4st1")
        ldr.wait(t_)
        ps_free = [None, None]
        qdone = [None, None]
        combc = None
        ldh.gate(0)
        ldh.slot(0).inc(y_.dma_start(
            bass.AP(hld[0], 0, [[NI * CH, 128], [CH, NI], [1, CH]]),
            bass.AP(H["d_h"], 0, [[T, 128], [128 * T, NI], [1, CH]])), 16)
        for i in range(nCH + 1):
            if i + 1 < nCH:
                ldh.gate(i + 1)
                ldh.slot(i + 1).inc(y_.dma_start(
                    bass.AP(hld[(i + 1) % 2], 0, [[NI * CH, 128], [CH, NI], [1, CH]]),
                    bass.AP(H["d_h"], (i + 1) * CH, [[T, 128], [128 * T, NI], [1, CH]])), 16)
            if i < nCH:
                ldh.wait(i, v_)
                b.wt(v_)
                b.vm(v_.tensor_scalar(hld[i % 2][:, :], hld[i % 2][:, :], scal[:, 2:3],
                                      MAGIC, AL.mult, AL.add))
                b.wv(v_)
                b.vm(v_.tensor_scalar_add(hld[i % 2][:, :], hld[i % 2][:, :], -MAGIC))
                b.wv(v_)
                b.vm(v_.tensor_scalar(hq[i % 2][:, :], hld[i % 2][:, :], QMAX,
                                      -QMAX - 1.0, AL.min, AL.max))
                qdone[i % 2] = b.cv.n
                ldh.done(i, b.cv)
            if i == 0:
                continue
            j = i - 1
            t_.wait_ge(b.cv.h, qdone[j % 2])
            for hc in range(8):
                pb = hc % 2
                if ps_free[pb] is not None:
                    sem, cnt = ps_free[pb]
                    t_.wait_ge(sem.h, cnt)
                last = None
                for ic in range(NI):
                    last = t_.matmul(
                        ps_d[pb][:, :],
                        wdT[:, ic * 1024 + hc * 128: ic * 1024 + (hc + 1) * 128],
                        hq[j % 2][:, ic * CH:(ic + 1) * CH],
                        start=(ic == 0), stop=(ic == NI - 1))
                b.tm(last)
                b.wt(s_)
                b.sm(s_.activation(yf[:, :], ps_d[pb][:, :], AF.Identity,
                                   scale=swe[:, hc:hc + 1]))
                ps_free[pb] = (b.cs, b.cs.n)
                b.ws(t_)
                tf_free = getattr(b, "_p4tfree", [None, None])
                for tt in range(CH // 128):
                    sl = tt % 2
                    if tf_free[sl] is not None:
                        t_.wait_ge(b.cs.h, tf_free[sl])
                    b.tm(t_.transpose(ps_tf[sl][:, :], yf[:, tt * 128:(tt + 1) * 128],
                                      C["ident_f"][:, :]))
                    b.wt(s_)
                    if combc is not None and hc == 0 and tt == 0:
                        s_.wait_ge(b.cv.h, combc)
                    b.sm(s_.copy(
                        ycomb[:, tt * 1024 + hc * 128: tt * 1024 + (hc + 1) * 128],
                        ps_tf[sl][:, :]))
                    tf_free[sl] = b.cs.n
                b._p4tfree = tf_free
            for tt in range(CH // 128):
                r0 = j * CH + tt * 128
                k4 = (CH // 128) * j + tt
                ldxa.gate(k4)
                ldxa.slot(k4).inc(y_.dma_start(xld[k4 % 2][:, :], H["x_in"][r0:r0 + 128, :]), 16)
                ldxa.slot(k4).inc(y_.dma_start(ald[k4 % 2][:, :], H["d_attn0"][r0:r0 + 128, :]), 16)
                ldxa.wait(k4, v_)
                stz.wait_all(v_)
                st1b.wait_all(v_)
                b.vm(v_.tensor_add(x1t[:, :], xld[k4 % 2][:, :], ald[k4 % 2][:, :]))
                b.ws(v_)
                b.wv(v_)
                b.vm(v_.tensor_add(x1t[:, :], x1t[:, :], ycomb[:, tt * 1024:(tt + 1) * 1024]))
                b.wv(v_)
                st1b.gate(k4)
                b.wv(y_)
                st1b.slot(k4).inc(y_.dma_start(H["d_x1"][r0:r0 + 128, :], x1t[:, :]), 16)
                ldxa.done(k4, b.cv)
                ln_rowstats(b, st1, x1t, 0, 1, 2, 3, 4, sqd)
                b.vm(v_.tensor_scalar(zt[:, :], x1t[:, :], st1[:, 0:1], st1[:, 3:4],
                                      AL.subtract, AL.mult))
                b.wv(v_)
                ln_rowstats(b, st1, zt, 0, 1, 2, 3, 4, sqd)
                b.vm(v_.tensor_scalar(zi[:, :], zt[:, :], st1[:, 0:1], st1[:, 3:4],
                                      AL.subtract, AL.mult))
                stz.gate(k4)
                b.wv(g_)
                stz.slot(k4).inc(g_.dma_start(H["d_z"][r0:r0 + 128, :], zt[:, :]), 16)
                stz.slot(k4).inc(g_.dma_start(H["d_z_bf"][r0:r0 + 128, :], zt[:, :]), 16)
                stz.slot(k4).inc(g_.dma_start(H["d_zin0"][r0:r0 + 128, :], zi[:, :]), 16)
            combc = b.cv.n
        H["p4_stz"] = stz
        H["p4_st1"] = st1b


def debug_copy(b, ST, H, src=None, dst=None):
    nc, v_, y_ = b.nc, b.v, b.y
    T = H["T"]
    src = src if src is not None else H["d_x1"]
    dst = dst if dst is not None else H["out_t"]
    fin = b.sem("fin")
    with ExitStack() as S:
        tb = S.enter_context(nc.sbuf_tensor("dbgt", [128, D], F32))
        for i in range(T // 128):
            ld = b.sem(None)
            ld.inc(y_.dma_start(tb[:, :], src[i * 128:(i + 1) * 128, :]), 16)
            ld.wait(y_)
            fin.inc(y_.dma_start(dst[i * 128:(i + 1) * 128, :], tb[:, :]), 16)
            fin.wait(y_)
    fin.wait(y_)


def safe_recip(b, out_ap, in_ap, scr, kint, W=1):
    v_ = b.v
    s0 = scr[:, 0:W]
    s1 = scr[:, W:2 * W]
    s0i = scr.bitcast(mybir.dt.int32)[:, 0:W]
    b.wv(v_)
    b.vm(v_.tensor_copy(s1, in_ap))
    b.wv(v_)
    b.vm(v_.tensor_tensor(s0i, kint, s1.tensor[:, 0:0] if False else
                          scr.bitcast(mybir.dt.int32)[:, W:2 * W], AL.subtract))
    b.wv(v_)
    for _ in range(2):
        b.vm(v_.tensor_mul(out_ap, s1, s0))
        b.wv(v_)
        b.vm(v_.tensor_scalar(out_ap, out_ap, -1.0, 2.0, AL.mult, AL.add))
        b.wv(v_)
        b.vm(v_.tensor_mul(s0, s0, out_ap))
        b.wv(v_)
    b.vm(v_.tensor_copy(out_ap, s0))
    b.wv(v_)


def softplus_emit(b, out_ap, in_ap, tmp_a, tmp_r, ones=None):
    v_, s_ = b.v, b.s
    b.ws(s_)
    b.wv(s_)
    b.sm(s_.activation(tmp_a, in_ap, AF.Abs))
    b.ws(s_)
    b.sm(s_.activation(tmp_a, tmp_a, AF.Exp, scale=-1.0))
    if ones is not None:
        b.ws(s_)
        b.sm(s_.activation(tmp_a, tmp_a, AF.Ln, bias=ones))
    else:
        b.ws(v_)
        b.vm(v_.tensor_scalar_add(tmp_a, tmp_a, 1.0))
        b.wv(s_)
        b.sm(s_.activation(tmp_a, tmp_a, AF.Ln))
    b.vm(v_.tensor_scalar_max(tmp_r, in_ap, 0.0))
    b.ws(v_)
    b.wv(v_)
    b.vm(v_.tensor_add(out_ap, tmp_r, tmp_a))
    b.wv(s_)


def vecload(b, sem, eng, dst_sb, src_dram, n):
    sem.inc(eng.dma_start(bass.AP(dst_sb, 0, [[dst_sb.shape[1], 128], [1, n]]),
                          bass.AP(src_dram, 0, [[1, 128], [128, n], [1, 1]]),
                          allow_slow_non_contiguous=True), 16)


def stage_moe_weights(b, ST, H, C):
    nc, g_, y_ = b.nc, b.g, b.y
    t_, s_ = b.t, b.s
    T, nT = H["T"], H["nT"]
    with ExitStack() as S:
        stg = S.enter_context(nc.sbuf_tensor("stgw", [128, 8 * 1024], BF16))
        ztl = [S.enter_context(nc.sbuf_tensor("ztl%d" % i, [128, 1024], BF16))
               for i in range(2)]
        ztT = [S.enter_context(nc.sbuf_tensor("ztT%d" % i, [128, 1024], BF16))
               for i in range(2)]
        ps_t = [S.enter_context(nc.psum_tensor("stps%d" % i, [128, 128], BF16))
                for i in range(2)]
        ldzt = DBuf(b, s_, 2, "stldz")
        stzt = DBuf(b, s_, 2, "ststz")
        ptfree = [None, None]
        ttdone = [None, None]
        for tt in range(nT + 1):
            if tt < nT:
                ldzt.gate(tt)
                ldzt.slot(tt).inc(s_.dma_start(ztl[tt % 2][:, :],
                                               H["d_z_bf"][tt * 128:(tt + 1) * 128, :]), 16)
            if tt == 0:
                continue
            j = tt - 1
            sl = j % 2
            ldzt.wait(j, t_)
            stzt.wait(j, t_)
            for cc in range(8):
                if ptfree[cc % 2] is not None:
                    t_.wait_ge(b.cs.h, ptfree[cc % 2])
                b.tm(t_.transpose(ps_t[cc % 2][:, :],
                                  ztl[sl][:, cc * 128:(cc + 1) * 128],
                                  C["ident_bf"][:, :]))
                b.wt(s_)
                b.sm(s_.copy(ztT[sl][:, cc * 128:(cc + 1) * 128], ps_t[cc % 2][:, :]))
                ptfree[cc % 2] = b.cs.n
            stzt.gate(j)
            b.ws(s_)
            stzt.slot(j).inc(s_.dma_start(
                bass.AP(H["d_zT"], j * 128, [[T, 128], [128 * T, 8], [1, 128]]),
                bass.AP(ztT[sl], 0, [[1024, 128], [128, 8], [1, 128]])), 16)
            ldzt.done(j, b.ct)
        cast_s = b.sem("mw_cast")
        tr_s = b.sem("mw_tr")
        st_s = b.sem("mw_st")
        for e in range(NE):
            for (wsrc, dbf, dT, R) in ((H["w1_moe"], H["d_w1bf"], H["d_w1T"], F_E),
                                       (H["w2_moe"], H["d_w2bf"], H["d_w2T"], D)):
                Ccols = D if R == F_E else F_E
                g_.wait_ge(tr_s.h, tr_s.n)
                cast_s.inc(g_.dma_start(dbf.ap(), wsrc.ap()[e]), 16)
                y_.wait_ge(cast_s.h, cast_s.n)
                y_.wait_ge(st_s.h, st_s.n)
                for cc in range(Ccols // 128):
                    tr_s.inc(y_.dma_start(stg[:, cc * 1024:cc * 1024 + R],
                                          dbf.ap()[:, cc * 128:(cc + 1) * 128],
                                          transpose=True), 16)
                y_.wait_ge(tr_s.h, tr_s.n)
                st_s.inc(y_.dma_start(
                    bass.AP(dT, e * Ccols * R, [[R, 128], [128 * R, Ccols // 128], [1, R]]),
                    bass.AP(stg, 0, [[8 * 1024, 128], [1024, Ccols // 128], [1, R]])), 16)
        y_.wait_ge(st_s.h, st_s.n)
        H["mw_st"] = st_s


def p5_pclock(b, ST, H, C):
    nc, t_, v_, s_, g_, y_ = b.nc, b.t, b.v, b.s, b.g, b.y
    T, nC = H["T"], H["nC"]
    l1w = H["l1w"]
    clkrow = ST.enter_context(nc.sbuf_tensor("clkrow", [1, T], F32))
    ln1g = ST.enter_context(nc.sbuf_tensor("ln1g", [128, 8], F32))
    ln1b = ST.enter_context(nc.sbuf_tensor("ln1b", [128, 8], F32))
    H["clkrow"], H["ln1g"], H["ln1b"] = clkrow, ln1g, ln1b
    with ExitStack() as S:
        sb = lambda n, s, d: S.enter_context(nc.sbuf_tensor(n, s, d))
        wp = sb("p5wp", [128, 8 * 1024], BF16)
        wclk = sb("p5wc", [128, 8], BF16)
        pbv = sb("p5pb", [128, 8], F32)
        cbv = sb("p5cb", [1, 4], F32)
        cwrk = sb("p5cw", [1, 1536], F32)
        zf = [sb("p5z%d" % i, [128, 8 * 512], BF16) for i in range(2)]
        pf = [sb("p5pf%d" % i, [128, 512], F32) for i in range(2)]
        pmax = sb("p5pm", [128, 18], F32)
        pld = sb("p5pld", [128, T], F32)
        pex = sb("p5pex", [128, T], BF16)
        ps_p = [S.enter_context(nc.psum_tensor("p5p%d" % i, [128, 512], F32)) for i in range(2)]
        ps_c = S.enter_context(nc.psum_tensor("p5c", [1, 512], F32))
        ldw = b.sem("p5ldw")
        for cc in range(8):
            ldw.inc(g_.dma_start(wp[:, cc * 1024:(cc + 1) * 1024],
                                 l1w["p_W"][cc * 128:(cc + 1) * 128, :]), 16)
        ldw.inc(g_.dma_start(bass.AP(wclk, 0, [[8, 128], [1, 8]]),
                             bass.AP(l1w["clock_W"], 0, [[1, 128], [128, 8], [1, 1]]),
                             allow_slow_non_contiguous=True), 16)
        ldv = b.sem("p5ldv")
        vecload(b, ldv, y_, pbv, l1w["p_b"], 8)
        vecload(b, ldv, y_, ln1g, l1w["ln1_g"], 8)
        vecload(b, ldv, y_, ln1b, l1w["ln1_b"], 8)
        ldv.inc(y_.dma_start(cbv[0:1, 0:1], H["l1w"]["clock_b"].ap()[None, :]), 16)
        b.vm(v_.memset(pmax[:, 0:8], -3.0e38))
        ldz = DBuf(b, y_, 2, "p5ldz")
        stp = DBuf(b, y_, 2, "p5stp")
        ldw.wait(t_)
        ldv.wait(v_)
        ldv.wait(s_)
        ps_free = [None, None]
        for i in range(nC + 1):
            if i < nC:
                ldz.gate(i)
                for cc in range(8):
                    ldz.slot(i).inc(y_.dma_start(
                        zf[i % 2][:, cc * 512:(cc + 1) * 512],
                        H["d_zin0"][i * 512:(i + 1) * 512, cc * 128:(cc + 1) * 128],
                        transpose=True), 16)
            if i == 0:
                continue
            j = i - 1
            zc = zf[j % 2]
            ldz.wait(j, v_)
            b.wt(v_)
            for cc in range(8):
                b.vm(v_.tensor_scalar(zc[:, cc * 512:(cc + 1) * 512],
                                      zc[:, cc * 512:(cc + 1) * 512],
                                      ln1g[:, cc:cc + 1], ln1b[:, cc:cc + 1],
                                      AL.mult, AL.add))
            b.wv(t_)
            for co in range(8):
                pb = co % 2
                if ps_free[pb] is not None:
                    sem, cnt = ps_free[pb]
                    t_.wait_ge(sem.h, cnt)
                last = None
                for cc in range(8):
                    last = t_.matmul(ps_p[pb][:, :],
                                     wp[:, cc * 1024 + co * 128: cc * 1024 + (co + 1) * 128],
                                     zc[:, cc * 512:(cc + 1) * 512],
                                     start=(cc == 0), stop=(cc == 7))
                b.tm(last)
                b.wt(s_)
                b.wv(s_)
                stp.sems[pb].wait(s_)
                b.sm(s_.activation(pf[pb][:, :], ps_p[pb][:, :], AF.Identity,
                                   bias=pbv[:, co:co + 1]))
                ps_free[pb] = (b.cs, b.cs.n)
                b.ws(v_)
                b.wv(v_)
                b.vm(v_.reduce_max(pmax[:, 16:17], pf[pb][:, :], axis=AX))
                b.wv(v_)
                b.vm(v_.tensor_max(pmax[:, co:co + 1], pmax[:, co:co + 1],
                                   pmax[:, 16:17]))
                stp.gate(j * 8 + co)
                b.ws(y_)
                stp.slot(j * 8 + co).inc(y_.dma_start(
                    H["d_p"][co * 128:(co + 1) * 128, j * 512:(j + 1) * 512],
                    pf[pb][:, :]), 16)
            last = None
            for cc in range(8):
                last = t_.matmul(ps_c[:, :], wclk[:, cc:cc + 1],
                                 zc[:, cc * 512:(cc + 1) * 512],
                                 start=(cc == 0), stop=(cc == 7))
            b.tm(last)
            b.wt(s_)
            b.ws(s_)
            b.wv(s_)
            b.sm(s_.activation(cwrk[0:1, 0:512], ps_c[:, :], AF.Identity,
                               bias=cbv[0:1, 0:1]))
            b.ws(t_)
            b.ws(v_)
            softplus_emit(b, clkrow[0:1, j * 512:(j + 1) * 512],
                          cwrk[0:1, 0:512], cwrk[0:1, 512:1024],
                          cwrk[0:1, 1024:1536])
            ldz.done(j, b.ct)
        b.wv(v_)
        b.vm(v_.tensor_scalar_mul(pmax[:, 8:16], pmax[:, 0:8], -1.0))
        b.wv(s_)
        ld2 = DBuf(b, y_, 1, "p5ld2")
        st2 = DBuf(b, y_, 1, "p5st2")
        stp.wait_all(y_)
        for co in range(8):
            ld2.gate(co)
            ld2.slot(co).inc(y_.dma_start(pld[:, :],
                                          H["d_p"][co * 128:(co + 1) * 128, :]), 16)
            ld2.wait(co, s_)
            st2.sems[0].wait(s_)
            b.sm(s_.activation(pex[:, :], pld[:, :], AF.Exp,
                               bias=pmax[:, 8 + co:9 + co]))
            st2.gate(co)
            b.ws(y_)
            st2.slot(co).inc(y_.dma_start(H["d_pexp0"][co * 128:(co + 1) * 128, :],
                                          pex[:, :]), 16)
            ld2.done(co, b.cs)
        H["p5_st2"] = st2


def cumsum_tile(b, C, ps_pair, ps_col, carry_sb, src_f32, out_f32, n_half=2,
                post=None):
    t_, v_, s_ = b.t, b.v, b.s
    for hh in range(n_half):
        sl = hh % 2
        b.wv(t_)
        b.tm(t_.matmul(ps_pair[sl][:, :], C["tril"][:, :],
                       src_f32[:, hh * 512:(hh + 1) * 512], start=True, stop=False))
        b.tm(t_.matmul(ps_pair[sl][:, :], C["onesrow"][:, :],
                       carry_sb[0:1, hh * 512:(hh + 1) * 512], start=False, stop=True))
        b.tm(t_.matmul(ps_col[:, :], C["onescol"][:, :],
                       src_f32[:, hh * 512:(hh + 1) * 512], start=True, stop=True))
        b.wt(v_)
        if post == "min60":
            b.vm(v_.tensor_scalar(out_f32[:, hh * 512:(hh + 1) * 512],
                                  ps_pair[sl][:, :], 60.0, None, AL.min))
        elif post == "eps":
            b.vm(v_.tensor_scalar(out_f32[:, hh * 512:(hh + 1) * 512],
                                  ps_pair[sl][:, :], 1e-8, None, AL.add))
        else:
            b.vm(v_.tensor_copy(out_f32[:, hh * 512:(hh + 1) * 512],
                                ps_pair[sl][:, :]))
        b.vm(v_.tensor_add(carry_sb[0:1, hh * 512:(hh + 1) * 512],
                           carry_sb[0:1, hh * 512:(hh + 1) * 512], ps_col[0:1, :]))
        b.wv(t_)


def p6_gating(b, ST, H, C):
    nc, t_, v_, s_, g_, y_ = b.nc, b.t, b.v, b.s, b.g, b.y
    T, nT = H["T"], H["nT"]
    l1w = H["l1w"]
    with ExitStack() as S:
        sb = lambda n, s, d: S.enter_context(nc.sbuf_tensor(n, s, d))
        wqW = sb("p6wq", [128, 8 * 1024], BF16)
        wkW = sb("p6wk", [128, 8 * 1024], BF16)
        wvW = sb("p6wv", [128, 8 * 1024], BF16)
        wgW = sb("p6wg", [128, 8 * 1024], BF16)
        gbb = sb("p6gbb", [128, 1024], F32)
        gbrow = sb("p6gbr", [1, 1024], F32)
        zin = [sb("p6z%d" % i, [128, 8 * 128], BF16) for i in range(2)]
        rc = [sb("p6rc%d" % i, [128, 512], F32) for i in range(2)]
        rs = [sb("p6rs%d" % i, [128, 512], F32) for i in range(2)]
        px = [sb("p6px%d" % i, [128, 1024], BF16) for i in range(2)]
        clkt = [sb("p6clk%d" % i, [128, 1], F32) for i in range(2)]
        qraw = [sb("p6q%d" % i, [128, 1024], F32) for i in range(2)]
        kraw = [sb("p6k%d" % i, [128, 1024], F32) for i in range(2)]
        gjr = [sb("p6gj%d" % i, [128, 1024], F32) for i in range(2)]
        vbf = [sb("p6v%d" % i, [128, 1024], BF16) for i in range(2)]
        spb = sb("p6sp", [128, 1024], F32)
        csb = sb("p6cs", [128, 1024], F32)
        gjcp = sb("p6gcp", [128, 1024], F32)
        pexf = sb("p6pe", [128, 1024], F32)
        pcum = sb("p6pc", [128, 1024], F32)
        rpc = sb("p6rpc", [128, 1024], F32)
        rop = sb("p6rop", [128, 1024], F32)
        dmp = sb("p6dmp", [128, 1024], F32)
        outq = sb("p6oq", [128, 1024], F32)
        outqb = sb("p6oqb", [128, 1024], BF16)
        cgj = sb("p6cgj", [1, 1024], F32)
        cpx = sb("p6cpx", [1, 1024], F32)
        nrm = sb("p6nrm", [128, 6], F32)
        krs = S.enter_context(nc.sbuf_tensor("p6krs", [128, 1], mybir.dt.int32))
        ps_pr = [S.enter_context(nc.psum_tensor("p6p%d" % i, [128, 512], F32)) for i in range(4)]
        ps_cs = [S.enter_context(nc.psum_tensor("p6c%d" % i, [128, 512], F32)) for i in range(2)]
        ps_col = S.enter_context(nc.psum_tensor("p6col", [1, 512], F32))
        ps_clk = S.enter_context(nc.psum_tensor("p6ck", [128, 1], F32))
        ldw = b.sem("p6ldw")
        for wsb_, nm in ((wqW, "Wq"), (wkW, "Wk"), (wvW, "Wv"), (wgW, "gate_W")):
            for cc in range(8):
                ldw.inc(g_.dma_start(wsb_[:, cc * 1024:(cc + 1) * 1024],
                                     l1w[nm][cc * 128:(cc + 1) * 128, :]), 16)
        b.vm(v_.memset(krs[:, :], 0x5F375A86))
        b.wv(v_)
        H["krs6"] = krs
        H["outqb6"] = outqb
        ldv = b.sem("p6ldv")
        ldv.inc(y_.dma_start(gbrow[0:1, :], l1w["gate_b"].ap()[None, :]), 16)
        ldv.wait(t_)
        for hh in range(2):
            b.tm(t_.matmul(ps_cs[hh][:, :], C["onesrow"][:, :],
                           gbrow[0:1, hh * 512:(hh + 1) * 512], start=True, stop=True))
        b.wt(v_)
        for hh in range(2):
            b.vm(v_.tensor_copy(gbb[:, hh * 512:(hh + 1) * 512], ps_cs[hh][:, :]))
        b.vm(v_.memset(cgj[:, :], 0.0))
        b.vm(v_.memset(cpx[:, :], 0.0))
        b.wv(t_)
        ldz = DBuf(b, y_, 2, "p6ldz")
        ldrop = DBuf(b, y_, 2, "p6lrp")
        ldpx = DBuf(b, y_, 2, "p6lpx")
        stqk = DBuf(b, y_, 2, "p6stq")
        ldw.wait(t_)
        cin = H["cin"]
        relV = [None, None]
        relS = [None, None]
        evc = [None, None]
        lnc = [None, None]
        prfree = {0: None, 1: None}
        clkc = [None, None]
        for i in range(-1, nT):
            ip = i + 1
            if ip < nT:
                jp = ip % 2
                ldz.gate(ip)
                for cc in range(8):
                    ldz.slot(ip).inc(y_.dma_start(
                        zin[jp][:, cc * 128:(cc + 1) * 128],
                        H["d_zin0"][ip * 128:(ip + 1) * 128, cc * 128:(cc + 1) * 128],
                        transpose=True), 16)
                ldrop.gate(ip)
                ldrop.slot(ip).inc(y_.dma_start(rc[jp][:, :],
                                                cin["c_rope_cos"][ip * 128:(ip + 1) * 128, :]), 16)
                ldrop.slot(ip).inc(y_.dma_start(rs[jp][:, :],
                                                cin["c_rope_sin"][ip * 128:(ip + 1) * 128, :]), 16)
                ldpx.gate(ip)
                for cc in range(8):
                    ldpx.slot(ip).inc(y_.dma_start(
                        px[jp][:, cc * 128:(cc + 1) * 128],
                        H["d_pexp0"][cc * 128:(cc + 1) * 128, ip * 128:(ip + 1) * 128],
                        transpose=True), 16)
                ldz.wait(ip, g_)
                if relV[jp] is not None:
                    g_.wait_ge(b.ct.h, evc[jp][1])
                for cc in range(8):
                    b.gm(g_.tensor_scalar(zin[jp][:, cc * 128:(cc + 1) * 128],
                                          zin[jp][:, cc * 128:(cc + 1) * 128],
                                          H["ln1g"][:, cc:cc + 1],
                                          H["ln1b"][:, cc:cc + 1],
                                          AL.mult, AL.add))
                lnc[jp] = b.cg.n
            if i >= 0:
                jc = i % 2
                b.sm(s_.activation(dmp[:, :], qraw[jc][:, :], AF.Square,
                                   accum_out=nrm[:, 0:1]))
                b.sm(s_.activation(rop[:, :], kraw[jc][:, :], AF.Square,
                                   accum_out=nrm[:, 3:4]))
                ldpx.wait(i, s_)
                b.ws(s_)
                b.sm(s_.activation(pexf[:, :], px[jc][:, :], AF.Identity,
                                   scale=clkt[jc][:, 0:1]))
                sq_c = b.cs.n
                v_.wait_ge(b.cs.h, evc[jc][0])
                b.vm(v_.tensor_add(gjr[jc][:, :], gjr[jc][:, :], gbb[:, :]))
                v_.wait_ge(b.cs.h, sq_c)
                rsqrt_emit(b, nrm, 0, H["krs6"])
                rsqrt_emit(b, nrm, 3, H["krs6"])
                b.wv(s_)
                softplus_emit(b, spb[:, :], gjr[jc][:, :], dmp[:, :], rop[:, :],
                              ones=C["onescol"][:, 0:1])
                b.ws(t_)
                b.ws(v_)
                b.wv(t_)
                b.wv(v_)
                cumsum_tile(b, C, ps_cs, ps_col, cgj, spb, csb, post="min60")
                b.wv(s_)
                b.sm(s_.activation(gjcp[:, :], csb[:, :], AF.Exp, scale=-1.0))
                b.ws(t_)
                b.ws(v_)
                b.wv(t_)
                cumsum_tile(b, C, ps_cs, ps_col, cpx, pexf, pcum, post="eps")
                b.wv(v_)
                b.vm(v_.reciprocal(rpc[:, :], pcum[:, :]))
                ldrop.wait(i, v_)
                rope_gate(b, C, H, qraw[jc], rc[jc], rs[jc], rop, dmp, outq, nrm, 0,
                          gjcp, rpc, stqk, 3 * i + 1,
                          H["d_q1"], i)
                b.wv(v_)
                b.vm(v_.tensor_scalar(dmp[:, :], gjcp[:, :], 1e-8, None, AL.add))
                b.wv(v_)
                b.vm(v_.reciprocal(rpc[:, :], dmp[:, :]))
                rope_gate(b, C, H, kraw[jc], rc[jc], rs[jc], rop, dmp, outq, nrm, 3,
                          pexf, rpc, stqk, 3 * i + 2,
                          H["d_k1"], i)
                ldrop.done(i, b.cv)
                ldpx.done(i, b.cs)
                relV[jc] = b.cv.n
                relS[jc] = b.cs.n
            if ip < nT:
                jp = ip % 2
                zc = zin[jp]
                if clkc[1 - jp] is not None:
                    t_.wait_ge(b.cs.h, clkc[1 - jp])
                b.tm(t_.matmul(ps_clk[:, :], H["clkrow"][0:1, ip * 128:(ip + 1) * 128],
                               C["onesrow"][0:1, 0:1], start=True, stop=True))
                b.wt(s_)
                b.sm(s_.copy(clkt[jp][:, :], ps_clk[:, :]))
                clkc[jp] = b.cs.n
                t_.wait_ge(b.cg.h, lnc[jp])
                if relV[jp] is not None:
                    t_.wait_ge(b.cv.h, relV[jp])
                    t_.wait_ge(b.cs.h, relS[jp])
                for si, (wsb_, dstb, isbf) in enumerate(
                        ((wqW, qraw[jp], False), (wkW, kraw[jp], False),
                         (wgW, gjr[jp], False), (wvW, vbf[jp], True))):
                    pr = (0, 1) if si % 2 == 0 else (2, 3)
                    if prfree[si % 2] is not None:
                        t_.wait_ge(b.cs.h, prfree[si % 2])
                    for hh in range(2):
                        last = None
                        for cc in range(8):
                            last = t_.matmul(
                                ps_pr[pr[hh]][:, :],
                                zc[:, cc * 128:(cc + 1) * 128],
                                wsb_[:, cc * 1024 + hh * 512: cc * 1024 + (hh + 1) * 512],
                                start=(cc == 0), stop=(cc == 7))
                        b.tm(last)
                    b.wt(s_)
                    if isbf:
                        for ssem in stqk.sems:
                            ssem.wait(s_)
                    for hh in range(2):
                        b.sm(s_.copy(dstb[:, hh * 512:(hh + 1) * 512],
                                     ps_pr[pr[hh]][:, :]))
                    prfree[si % 2] = b.cs.n
                evc[jp] = (b.cs.n, b.ct.n)
                stqk.gate(3 * ip)
                b.ws(y_)
                stqk.slot(3 * ip).inc(y_.dma_start(H["d_v1"][ip * 128:(ip + 1) * 128, :],
                                                   vbf[jp][:, :]), 16)
                ldz.done(ip, b.ct)
        H["p6_stqk"] = stqk


def rsqrt_emit(b, nrm, nb, krs):
    v_ = b.v
    ni = nrm.bitcast(mybir.dt.int32)
    b.vm(v_.tensor_scalar(ni[:, nb + 1:nb + 2], ni[:, nb:nb + 1], 1, None,
                          AL.logical_shift_right))
    b.wv(v_)
    b.vm(v_.tensor_tensor(ni[:, nb + 1:nb + 2], krs[:, :], ni[:, nb + 1:nb + 2],
                          AL.subtract))
    b.wv(v_)
    for _ in range(2):
        b.vm(v_.tensor_mul(nrm[:, nb + 2:nb + 3], nrm[:, nb + 1:nb + 2],
                           nrm[:, nb + 1:nb + 2]))
        b.wv(v_)
        b.vm(v_.tensor_mul(nrm[:, nb + 2:nb + 3], nrm[:, nb + 2:nb + 3],
                           nrm[:, nb:nb + 1]))
        b.wv(v_)
        b.vm(v_.tensor_scalar(nrm[:, nb + 2:nb + 3], nrm[:, nb + 2:nb + 3],
                              -0.5, 1.5, AL.mult, AL.add))
        b.wv(v_)
        b.vm(v_.tensor_mul(nrm[:, nb + 1:nb + 2], nrm[:, nb + 1:nb + 2],
                           nrm[:, nb + 2:nb + 3]))
        b.wv(v_)


def rope_gate(b, C, H, xraw, rc, rs, rop, dmp, outq, nrm, nb, mulA, mulB,
              stqk, stidx, dst_dram, j):
    v_, s_, g_, y_ = b.v, b.s, b.g, b.y
    b.wv(v_)
    b.vm(v_.tensor_scalar(xraw[:, :], xraw[:, :], nrm[:, nb + 1:nb + 2], None,
                          AL.mult))
    b.wv(v_)
    xe = bass.AP(xraw, 0, [[1024, 128], [2, 512]])
    xo = bass.AP(xraw, 1, [[1024, 128], [2, 512]])
    re_ = bass.AP(rop, 0, [[1024, 128], [2, 512]])
    ro_ = bass.AP(rop, 1, [[1024, 128], [2, 512]])
    de_ = bass.AP(dmp, 0, [[1024, 128], [2, 512]])
    b.vm(v_.tensor_mul(re_, xe, rc[:, :]))
    b.vm(v_.tensor_mul(de_, xo, rs[:, :]))
    b.wv(v_)
    b.vm(v_.tensor_sub(re_, re_, de_))
    b.vm(v_.tensor_mul(ro_, xe, rs[:, :]))
    b.wv(v_)
    b.vm(v_.tensor_mul(de_, xo, rc[:, :]))
    b.wv(v_)
    b.vm(v_.tensor_add(ro_, ro_, de_))
    b.wv(v_)
    outqb = H["outqb6"]
    for ssem in stqk.sems:
        ssem.wait(v_)
    b.vm(v_.tensor_mul(outq[:, :], rop[:, :], mulA[:, :]))
    b.wv(v_)
    b.vm(v_.tensor_mul(outqb[:, :], outq[:, :], mulB[:, :]))
    stqk.gate(stidx)
    b.wv(y_)
    stqk.slot(stidx).inc(y_.dma_start(dst_dram[j * 128:(j + 1) * 128, :],
                                      outqb[:, :]), 16)


def p7_attn1(b, ST, H, C):
    nc, t_, v_, s_, g_, y_ = b.nc, b.t, b.v, b.s, b.g, b.y
    T, nT = H["T"], H["nT"]
    with ExitStack() as S:
        sb = lambda n, s, d: S.enter_context(nc.sbuf_tensor(n, s, d))
        k1f = sb("p7k", [128, 8 * T], BF16)
        v1sb = sb("p7v", [128, nT * 1024], BF16)
        q1f = [sb("p7q%d" % i, [128, 8 * 128], BF16) for i in range(2)]
        scrow = [sb("p7sc%d" % i, [128, T], F32) for i in range(2)]
        pbq = sb("p7pb", [128, T], BF16)
        pT2 = sb("p7pT", [128, 4 * 128], BF16)
        avtok = [sb("p7av%d" % i, [128, 1024], BF16) for i in range(2)]
        mrow = sb("p7m", [128, 4], F32)
        ps_sc = [S.enter_context(nc.psum_tensor("p7s%d" % i, [128, 512], F32)) for i in range(2)]
        ps_tb = [S.enter_context(nc.psum_tensor("p7t%d" % i, [128, 128], BF16)) for i in range(2)]
        ps_av = [S.enter_context(nc.psum_tensor("p7a%d" % i, [128, 512], F32)) for i in range(2)]
        ldr = b.sem("p7res")
        for cc in range(8):
            ldr.inc(y_.dma_start(k1f[:, cc * T:(cc + 1) * T],
                                 H["d_k1"][:, cc * 128:(cc + 1) * 128],
                                 transpose=True), 16)
        ldr.inc(y_.dma_start(
            bass.AP(v1sb, 0, [[nT * 1024, 128], [1024, nT], [1, 1024]]),
            bass.AP(H["d_v1"], 0, [[1024, 128], [128 * 1024, nT], [1, 1024]])), 16)
        ldq = DBuf(b, y_, 2, "p7ldq")
        stv = DBuf(b, g_, 2, "p7stv")
        ldr.wait(t_)
        sc_free = [None, None]
        scd = [None, None]
        tbfree = [None, None]
        pt4free = [None] * 4
        avfree = None
        for i in range(nT + 2):
            if i < nT:
                ldq.gate(i)
                for cc in range(8):
                    ldq.slot(i).inc(y_.dma_start(
                        q1f[i % 2][:, cc * 128:(cc + 1) * 128],
                        H["d_q1"][i * 128:(i + 1) * 128, cc * 128:(cc + 1) * 128],
                        transpose=True), 16)
            if 1 <= i <= nT:
                qt = i - 1
                qf = q1f[qt % 2]
                nk = qt + 1
                nch = (nk + 3) // 4
                ldq.wait(qt, t_)
                for kc in range(nch):
                    pb = kc % 2
                    if sc_free[pb] is not None:
                        t_.wait_ge(b.cs.h, sc_free[pb])
                    kw = min(512, nk * 128 - kc * 512)
                    last = None
                    for cc in range(8):
                        last = t_.matmul(
                            ps_sc[pb][:, 0:kw],
                            qf[:, cc * 128:(cc + 1) * 128],
                            k1f[:, cc * T + kc * 512: cc * T + kc * 512 + kw],
                            start=(cc == 0), stop=(cc == 7))
                    b.tm(last)
                    b.wt(s_)
                    b.sm(s_.activation(scrow[qt % 2][:, kc * 512:kc * 512 + kw],
                                       ps_sc[pb][:, 0:kw], AF.Identity,
                                       scale=1.0 / 32.0))
                    sc_free[pb] = b.cs.n
                scd[qt % 2] = b.cs.n
                ldq.done(qt, b.ct)
            if i >= 2:
                qt = i - 2
                scq = scrow[qt % 2]
                nk = qt + 1
                jdiag = qt % 4
                kcd = qt // 4
                kw = min(512, nk * 128 - kcd * 512)
                v_.wait_ge(b.cs.h, scd[qt % 2])
                b.vm(v_.tensor_add(scq[:, kcd * 512:kcd * 512 + kw],
                                   scq[:, kcd * 512:kcd * 512 + kw],
                                   C["maskadd_l1"][:, jdiag * 512:jdiag * 512 + kw]))
                b.wv(v_)
                b.vm(v_.reduce_max(mrow[:, 0:1], scq[:, 0:nk * 128], axis=AX))
                b.wv(v_)
                b.vm(v_.tensor_scalar_mul(mrow[:, 1:2], mrow[:, 0:1], -1.0))
                b.wv(s_)
                b.ws(s_)
                b.sm(s_.activation(pbq[:, 0:nk * 128], scq[:, 0:nk * 128], AF.Exp,
                                   bias=mrow[:, 1:2], accum_out=mrow[:, 2:3]))
                expc = b.cs.n
                b.ws(v_)
                b.vm(v_.reciprocal(C["lrec_all"][:, qt:qt + 1], mrow[:, 2:3]))
                copyc = {}
                pend = []

                def emit_av(kp):
                    t_.wait_ge(b.cs.h, copyc[kp])
                    if kp == 0 and avfree is not None:
                        t_.wait_ge(b.cs.h, avfree)
                    for hh in range(2):
                        b.tm(t_.matmul(
                            ps_av[hh][:, :],
                            pT2[:, (kp % 4) * 128:(kp % 4 + 1) * 128],
                            v1sb[:, kp * 1024 + hh * 512: kp * 1024 + (hh + 1) * 512],
                            start=(kp == 0), stop=(kp == nk - 1)))
                    pt4free[kp % 4] = b.ct.n

                for kt in range(nk):
                    sl2 = kt % 2
                    sl4 = kt % 4
                    if tbfree[sl2] is not None:
                        t_.wait_ge(b.cs.h, tbfree[sl2])
                    if kt == 0:
                        t_.wait_ge(b.cs.h, expc)
                    b.tm(t_.transpose(ps_tb[sl2][:, :], pbq[:, kt * 128:(kt + 1) * 128],
                                      C["ident_bf"][:, :]))
                    b.wt(s_)
                    if pt4free[sl4] is not None:
                        s_.wait_ge(b.ct.h, pt4free[sl4])
                    b.sm(s_.copy(pT2[:, sl4 * 128:(sl4 + 1) * 128], ps_tb[sl2][:, :]))
                    copyc[kt] = b.cs.n
                    tbfree[sl2] = b.cs.n
                    if len(pend) >= 2:
                        emit_av(pend.pop(0))
                    pend.append(kt)
                while pend:
                    emit_av(pend.pop(0))
                b.wt(s_)
                stv.sems[qt % 2].wait(s_)
                for hh in range(2):
                    b.sm(s_.copy(avtok[qt % 2][:, hh * 512:(hh + 1) * 512], ps_av[hh][:, :]))
                avfree = b.cs.n
                stv.gate(qt)
                b.ws(g_)
                stv.slot(qt).inc(g_.dma_start(H["d_av"][qt * 128:(qt + 1) * 128, :],
                                              avtok[qt % 2][:, :]), 16)
        H["p7_stv"] = stv


def p7b_outproj(b, ST, H, C):
    nc, t_, v_, s_, g_, y_ = b.nc, b.t, b.v, b.s, b.g, b.y
    T, nT = H["T"], H["nT"]
    l1w = H["l1w"]
    with ExitStack() as S:
        sb = lambda n, s, d: S.enter_context(nc.sbuf_tensor(n, s, d))
        w1o = sb("p7wo", [128, 8 * 1024], BF16)
        obb = sb("p7obb", [128, 1024], F32)
        obr = sb("p7obr", [1, 1024], F32)
        avf = [sb("p7af%d" % i, [128, 8 * 128], BF16) for i in range(2)]
        zld = [sb("p7z%d" % i, [128, 1024], F32) for i in range(2)]
        x2t = sb("p7x2", [128, 1024], F32)
        h2b = sb("p7h2", [128, 1024], BF16)
        sqd = sb("p7sq", [128, D], F32)
        st1 = sb("p7st", [128, 8], F32)
        ps_o = [S.enter_context(nc.psum_tensor("p7o%d" % i, [128, 512], F32)) for i in range(2)]
        ldw = b.sem("p7bw")
        for cc in range(8):
            ldw.inc(g_.dma_start(w1o[:, cc * 1024:(cc + 1) * 1024],
                                 l1w["out_W"][cc * 128:(cc + 1) * 128, :]), 16)
        ldv = b.sem("p7bv")
        ldv.inc(y_.dma_start(obr[0:1, :], l1w["out_b"].ap()[None, :]), 16)
        ldv.wait(t_)
        for hh in range(2):
            b.tm(t_.matmul(ps_o[hh][:, :], C["onesrow"][:, :],
                           obr[0:1, hh * 512:(hh + 1) * 512], start=True, stop=True))
        b.wt(v_)
        for hh in range(2):
            b.vm(v_.tensor_copy(obb[:, hh * 512:(hh + 1) * 512], ps_o[hh][:, :]))
        b.wv(t_)
        ldq = DBuf(b, y_, 2, "p7bq")
        ldz = DBuf(b, y_, 2, "p7bz")
        stx = DBuf(b, y_, 2, "p7bx")
        sth = DBuf(b, g_, 2, "p7bh")
        ldw.wait(t_)
        for i in range(nT + 1):
            if i < nT:
                ldq.gate(i)
                for cc in range(8):
                    ldq.slot(i).inc(y_.dma_start(
                        avf[i % 2][:, cc * 128:(cc + 1) * 128],
                        H["d_av"][i * 128:(i + 1) * 128, cc * 128:(cc + 1) * 128],
                        transpose=True), 16)
                ldz.gate(i)
                ldz.slot(i).inc(y_.dma_start(zld[i % 2][:, :],
                                             H["d_z"][i * 128:(i + 1) * 128, :]), 16)
            if i == 0:
                continue
            qt = i - 1
            ldq.wait(qt, t_)
            b.wv(t_)
            for cc in range(8):
                for hh in range(2):
                    b.tm(t_.matmul(ps_o[hh][:, :],
                                   avf[qt % 2][:, cc * 128:(cc + 1) * 128],
                                   w1o[:, cc * 1024 + hh * 512: cc * 1024 + (hh + 1) * 512],
                                   start=(cc == 0), stop=(cc == 7)))
            ldz.wait(qt, v_)
            b.wt(v_)
            stx.wait_all(v_)
            sth.wait_all(v_)
            b.wv(v_)
            for hh in range(2):
                b.vm(v_.tensor_scalar(x2t[:, hh * 512:(hh + 1) * 512],
                                      ps_o[hh][:, :], C["lrec_all"][:, qt:qt + 1],
                                      None, AL.mult))
            b.wv(v_)
            b.vm(v_.tensor_add(x2t[:, :], x2t[:, :], obb[:, :]))
            b.wv(v_)
            b.vm(v_.tensor_add(x2t[:, :], x2t[:, :], zld[qt % 2][:, :]))
            b.wv(v_)
            stx.gate(qt)
            b.wv(y_)
            stx.slot(qt).inc(y_.dma_start(H["d_x2"][qt * 128:(qt + 1) * 128, :],
                                          x2t[:, :]), 16)
            ldz.done(qt, b.cv)
            ln_rowstats(b, st1, x2t, 0, 1, 2, 3, 4, sqd)
            b.vm(v_.tensor_scalar(h2b[:, :], x2t[:, :], st1[:, 0:1], st1[:, 3:4],
                                  AL.subtract, AL.mult))
            sth.gate(qt)
            b.wv(g_)
            sth.slot(qt).inc(g_.dma_start(H["d_h2"][qt * 128:(qt + 1) * 128, :],
                                          h2b[:, :]), 16)
            ldq.done(qt, b.ct)
        H["p7_stx"] = stx
        H["p7_sth"] = sth


def p9_moe(b, ST, H, C):
    nc, t_, v_, s_, g_, y_ = b.nc, b.t, b.v, b.s, b.g, b.y
    T, nT, nC = H["T"], H["nT"], H["nC"]
    N_ALL, CAP = H["N_ALL"], H["CAP"]
    ncol = N_ALL // 128
    CH = min(1024, T)
    nCH = T // CH
    ntt = CH // 128
    with ExitStack() as S:
        sb = lambda n, s, d: S.enter_context(nc.sbuf_tensor(n, s, d))
        gwT = sb("p9gw", [128, 8 * 8], BF16)
        zf = [sb("p9zf%d" % i, [128, 8 * CH], BF16) for i in range(2)]
        probs = sb("p9pr", [128, nT * 8], F32)
        assign = sb("p9as", [128, nT * 8], F32)
        Rown = sb("p9ro", [128, nT * 8], F32)
        Pdup = sb("p9pd", [128, 4 * 16], F32)
        Scum = sb("p9scu", [128, 4 * 8], F32)
        rankc = sb("p9rk", [128, 4 * 8], F32)
        tmp8 = sb("p9t8", [128, 4 * 8], F32)
        rall = sb("p9ra", [128, ncol * 8], F32)
        cmpb = sb("p9cm", [128, ncol * 8], F32)
        red8 = sb("p9rd", [128, 8], F32)
        bis = sb("p9bi", [128, 6 * 8], F32)
        rmx = sb("p9rmx", [128, nT], F32)
        rsum = sb("p9rsu", [128, nT], F32)
        ssel = sb("p9ss", [128, nT * 8], F32)
        w1T = [sb("p9w1%d" % i, [128, 8 * 1024], BF16) for i in range(2)]
        w2T = [sb("p9w2%d" % i, [128, 8 * 1024], BF16) for i in range(2)]
        ge = sb("p9ge", [128, 8 * CH], BF16)
        yac = sb("p9ya", [128, ntt * 1024], F32)
        st1 = sb("p9st1", [128, 4], F32)
        ps_R = S.enter_context(nc.psum_tensor("p9pR_", [128, nT * 8], F32))
        ps_hy = [S.enter_context(nc.psum_tensor("p9hy%d" % i, [128, 512], F32)) for i in range(4)]
        ps_c = S.enter_context(nc.psum_tensor("p9c", [1, 8], F32))
        ps_b = S.enter_context(nc.psum_tensor("p9b", [128, 8], F32))
        ldg = b.sem("p9ldg")
        for cc in range(8):
            ldg.inc(g_.dma_start(gwT[:, cc * 8:(cc + 1) * 8],
                                 bass.AP(H["gw_moe"], cc * 128, [[1, 128], [1024, 8], [1, 1]]),
                                 allow_slow_non_contiguous=True), 16)
        ldg.wait(t_)
        stR = DBuf(b, y_, 2, "p9stR")
        ldzf = DBuf(b, y_, 2, "p9ldz")
        ldw = DBuf(b, g_, 2, "p9ldw")
        ldw2 = DBuf(b, y_, 2, "p9lw2")
        for pe_ in range(2):
            ldw.gate(pe_)
            ldw.slot(pe_).inc(g_.dma_start(
                bass.AP(w1T[pe_], 0, [[8 * 1024, 128], [1024, 8], [1, 1024]]),
                bass.AP(H["d_w1T"], pe_ * D * F_E,
                        [[F_E, 128], [128 * F_E, 8], [1, F_E]])), 16)
            ldw2.gate(pe_)
            ldw2.slot(pe_).inc(y_.dma_start(
                bass.AP(w2T[pe_], 0, [[8 * 1024, 128], [1024, 8], [1, 1024]]),
                bass.AP(H["d_w2T"], pe_ * F_E * D,
                        [[D, 128], [128 * D, 8], [1, D]])), 16)
        for ch in range(nCH):
            if ch == 0:
                ldzf.gate(0)
                for cc in range(8):
                    ldzf.slot(0).inc(y_.dma_start(
                        zf[0][:, cc * CH:(cc + 1) * CH],
                        H["d_zT"][cc * 128:(cc + 1) * 128, 0:CH]), 16)
            if ch + 1 < nCH:
                ldzf.gate(ch + 1)
                for cc in range(8):
                    ldzf.slot(ch + 1).inc(y_.dma_start(
                        zf[(ch + 1) % 2][:, cc * CH:(cc + 1) * CH],
                        H["d_zT"][cc * 128:(cc + 1) * 128, (ch + 1) * CH:(ch + 2) * CH]), 16)
            ldzf.wait(ch, t_)
            zfc = zf[ch % 2]
            for tt in range(CH // 128):
                i = ch * (CH // 128) + tt
                last = None
                for cc in range(8):
                    last = t_.matmul(ps_R[:, i * 8:(i + 1) * 8],
                                     zfc[:, cc * CH + tt * 128: cc * CH + tt * 128 + 128],
                                     gwT[:, cc * 8:(cc + 1) * 8],
                                     start=(cc == 0), stop=(cc == 7))
                b.tm(last)
            ldzf.done(ch, b.ct)
        b.wt(v_)
        b.vm(v_.tensor_reduce(rmx[:, :],
                              bass.AP(ps_R, 0, [[nT * 8, 128], [8, nT], [1, 8]]),
                              axis=AX, op=AL.max))
        b.wv(v_)
        b.vm(v_.tensor_tensor(probs[:, :], ps_R[:, :],
                              bass.AP(rmx, 0, [[nT, 128], [1, nT], [0, 8]]),
                              AL.subtract))
        b.wv(s_)
        b.sm(s_.activation(probs[:, :], probs[:, :], AF.Exp))
        b.ws(v_)
        b.vm(v_.tensor_reduce(rsum[:, :],
                              bass.AP(probs, 0, [[nT * 8, 128], [8, nT], [1, 8]]),
                              axis=AX, op=AL.add))
        b.wv(v_)
        b.vm(v_.reciprocal(rsum[:, :], rsum[:, :]))
        b.wv(v_)
        b.vm(v_.tensor_tensor(probs[:, :], probs[:, :],
                              bass.AP(rsum, 0, [[nT, 128], [1, nT], [0, 8]]),
                              AL.mult))
        b.wv(v_)
        b.vm(v_.memset(yac[:, :], 0.0))
        b.wv(v_)
        for gq in range(nT // 4):
            pr4 = probs[:, gq * 32:(gq + 1) * 32]
            d0 = bass.AP(Pdup, 0, [[64, 128], [16, 4], [1, 8]])
            d1 = bass.AP(Pdup, 8, [[64, 128], [16, 4], [1, 8]])
            b.vm(v_.tensor_copy(d0, pr4))
            b.vm(v_.tensor_copy(d1, pr4))
            b.vm(v_.memset(Scum[:, :], 0.0))
            b.vm(v_.memset(rankc[:, :], 0.0))
            b.wv(v_)
            for dd in range(1, 8):
                shifted = bass.AP(Pdup, dd, [[64, 128], [16, 4], [1, 8]])
                b.vm(v_.tensor_tensor(tmp8[:, :], shifted, pr4, AL.is_gt))
                b.wv(v_)
                b.vm(v_.tensor_add(rankc[:, :], rankc[:, :], tmp8[:, :]))
                b.vm(v_.tensor_tensor(tmp8[:, :], tmp8[:, :], shifted, AL.mult))
                b.wv(v_)
                b.vm(v_.tensor_add(Scum[:, :], Scum[:, :], tmp8[:, :]))
                b.wv(v_)
            b.vm(v_.tensor_scalar(assign[:, gq * 32:(gq + 1) * 32], Scum[:, :],
                                  0.9, None, AL.is_lt))
            b.vm(v_.tensor_sub(tmp8[:, :], pr4, rankc[:, :]))
            b.wv(v_)
            b.vm(v_.tensor_scalar_add(tmp8[:, :], tmp8[:, :], 99.0))
            b.wv(v_)
            b.vm(v_.tensor_mul(tmp8[:, :], tmp8[:, :],
                               assign[:, gq * 32:(gq + 1) * 32]))
            b.wv(v_)
            b.vm(v_.tensor_scalar_add(Rown[:, gq * 32:(gq + 1) * 32], tmp8[:, :],
                                      -100.0))
            b.wv(v_)
            stR.gate(gq)
            b.wv(y_)
            stR.slot(gq).inc(y_.dma_start(
                bass.AP(H["d_R"], gq * 512 * 8, [[8, 128], [128 * 8, 4], [1, 8]]),
                bass.AP(Rown, gq * 32, [[nT * 8, 128], [8, 4], [1, 8]])), 16)
        stR.wait_all(g_)
        ccg = H["ccag"]
        ccg.inc(g_.collective_compute("AllGather", AL.bypass, replica_groups=H["rg"],
                                      ins=[H["d_R"].ap()], outs=[H["d_Rall"].ap()]))
        ccg.wait(y_)
        ldra = b.sem("p9lra")
        ldra.inc(y_.dma_start(rall[:, :], H["d_Rall"].ap().rearrange(
            "(p n) e -> p (n e)", p=128)), 16)
        ldra.wait(v_)
        b.vm(v_.memset(bis[:, 0:8], -101.0))
        b.vm(v_.memset(bis[:, 8:16], 1.0))
        b.wv(v_)
        for it in range(30):
            b.vm(v_.tensor_add(bis[:, 16:24], bis[:, 0:8], bis[:, 8:16]))
            b.wv(v_)
            b.vm(v_.tensor_scalar_mul(bis[:, 16:24], bis[:, 16:24], 0.5))
            b.wv(v_)
            b.vm(v_.tensor_tensor(cmpb[:, :], rall[:, :],
                                  bass.AP(bis, 16, [[48, 128], [0, ncol], [1, 8]]),
                                  AL.is_ge))
            b.wv(v_)
            b.vm(v_.tensor_reduce(red8[:, :],
                                  bass.AP(cmpb, 0, [[ncol * 8, 128], [1, 8], [8, ncol]]),
                                  axis=AX, op=AL.add))
            b.wv(t_)
            b.tm(t_.matmul(ps_b[:, :], C["ones128"][:, :], red8[:, :],
                           start=True, stop=True))
            b.wt(v_)
            b.vm(v_.tensor_scalar(bis[:, 24:32], ps_b[:, :], float(CAP), None,
                                  AL.is_ge))
            b.wv(v_)
            b.vm(v_.tensor_sub(bis[:, 32:40], bis[:, 16:24], bis[:, 0:8]))
            b.wv(v_)
            b.vm(v_.tensor_mul(bis[:, 32:40], bis[:, 32:40], bis[:, 24:32]))
            b.wv(v_)
            b.vm(v_.tensor_add(bis[:, 0:8], bis[:, 0:8], bis[:, 32:40]))
            b.vm(v_.tensor_scalar(bis[:, 40:48], bis[:, 24:32], -1.0, 1.0,
                                  AL.mult, AL.add))
            b.wv(v_)
            b.vm(v_.tensor_sub(bis[:, 32:40], bis[:, 8:16], bis[:, 16:24]))
            b.wv(v_)
            b.vm(v_.tensor_mul(bis[:, 32:40], bis[:, 32:40], bis[:, 40:48]))
            b.wv(v_)
            b.vm(v_.tensor_sub(bis[:, 8:16], bis[:, 8:16], bis[:, 32:40]))
            b.wv(v_)
        b.vm(v_.tensor_tensor(ssel[:, :], Rown[:, :],
                              bass.AP(bis, 0, [[48, 128], [0, nT], [1, 8]]),
                              AL.is_ge))
        b.wv(v_)
        b.vm(v_.tensor_mul(ssel[:, :], ssel[:, :], assign[:, :]))
        b.wv(v_)
        b.vm(v_.tensor_mul(ssel[:, :], ssel[:, :], probs[:, :]))
        b.wv(v_)
        b.wv(s_)
        ldz2 = DBuf(b, y_, 2, "p9ldz2")
        stym = DBuf(b, y_, 2, "p9sty")
        hyfree = [None] * 4

        def _wait_free(pb):
            if hyfree[pb] is not None:
                kind, cnt = hyfree[pb]
                t_.wait_ge(b.cs.h if kind == 's' else b.cv.h, cnt)
        ldz2.free[0] = (b.ct, b.ct.n)
        ldz2.free[1] = (b.ct, b.ct.n)
        for ch in range(nCH):
            if ch == 0:
                ldz2.gate(0)
                for cc in range(8):
                    ldz2.slot(0).inc(y_.dma_start(
                        zf[0][:, cc * CH:(cc + 1) * CH],
                        H["d_zT"][cc * 128:(cc + 1) * 128, 0:CH]), 16)
            if ch + 1 < nCH:
                ldz2.gate(ch + 1)
                for cc in range(8):
                    ldz2.slot(ch + 1).inc(y_.dma_start(
                        zf[(ch + 1) % 2][:, cc * CH:(cc + 1) * CH],
                        H["d_zT"][cc * 128:(cc + 1) * 128, (ch + 1) * CH:(ch + 2) * CH]), 16)
            if ch > 0:
                stym.wait_all(v_)
                b.wt(v_)
                b.vm(v_.memset(yac[:, :], 0.0))
                b.wv(t_)
            ldz2.wait(ch, t_)
            zfc = zf[ch % 2]
            for e in range(NE):
                ei = ch * NE + e
                ldw.wait(ei, t_)
                ldw2.wait(ei, t_)
                wa, wb = w1T[ei % 2], w2T[ei % 2]
                nth = CH // 512
                for fc in range(8):
                    pr = (0, 1) if fc % 2 == 0 else (2, 3)
                    _wait_free(pr[0])
                    _wait_free(pr[1])
                    lasts = [None] * nth
                    for cc in range(8):
                        for th in range(nth):
                            lasts[th] = t_.matmul(
                                ps_hy[pr[th % 2]][:, :],
                                wa[:, cc * 1024 + fc * 128: cc * 1024 + (fc + 1) * 128],
                                zfc[:, cc * CH + th * 512: cc * CH + (th + 1) * 512],
                                start=(cc == 0), stop=(cc == 7))
                    for th in range(nth):
                        b.tm(lasts[th])
                    b.wt(s_)
                    for th in range(nth):
                        b.sm(s_.activation(ge[:, fc * CH + th * 512: fc * CH + (th + 1) * 512],
                                           ps_hy[pr[th % 2]][:, :], AF.Gelu))
                    hyfree[pr[0]] = ('s', b.cs.n)
                    hyfree[pr[1]] = ('s', b.cs.n)
                b.ws(t_)
                for tt in range(ntt):
                    pr = (0, 1) if tt % 2 == 0 else (2, 3)
                    _wait_free(pr[0])
                    _wait_free(pr[1])
                    lasts = [None, None]
                    for fc in range(8):
                        for hh in range(2):
                            lasts[hh] = t_.matmul(
                                ps_hy[pr[hh]][:, :],
                                ge[:, fc * CH + tt * 128: fc * CH + (tt + 1) * 128],
                                wb[:, fc * 1024 + hh * 512: fc * 1024 + (hh + 1) * 512],
                                start=(fc == 0), stop=(fc == 7))
                    for hh in range(2):
                        b.tm(lasts[hh])
                    b.wt(v_)
                    gt = ch * ntt + tt
                    for hh in range(2):
                        b.vm(v_.scalar_tensor_tensor(
                            yac[:, tt * 1024 + hh * 512: tt * 1024 + (hh + 1) * 512],
                            ps_hy[pr[hh]][:, :], ssel[:, gt * 8 + e: gt * 8 + e + 1],
                            yac[:, tt * 1024 + hh * 512: tt * 1024 + (hh + 1) * 512],
                            AL.mult, AL.add))
                    hyfree[pr[0]] = ('v', b.cv.n)
                    hyfree[pr[1]] = ('v', b.cv.n)
                ldw.done(ei, b.ct)
                ldw2.done(ei, b.ct)
                en = (e + 2) % NE
                if ei + 2 < nCH * NE:
                    nsl = ei % 2
                    ldw.gate(ei + 2)
                    ldw.slot(ei + 2).inc(g_.dma_start(
                        bass.AP(w1T[nsl], 0, [[8 * 1024, 128], [1024, 8], [1, 1024]]),
                        bass.AP(H["d_w1T"], en * D * F_E, [[F_E, 128], [128 * F_E, 8], [1, F_E]])), 16)
                    ldw2.gate(ei + 2)
                    ldw2.slot(ei + 2).inc(y_.dma_start(
                        bass.AP(w2T[nsl], 0, [[8 * 1024, 128], [1024, 8], [1, 1024]]),
                        bass.AP(H["d_w2T"], en * F_E * D, [[D, 128], [128 * D, 8], [1, D]])), 16)
            b.wv(y_)
            stym.gate(ch)
            stym.slot(ch).inc(y_.dma_start(
                bass.AP(H["d_moe"], ch * CH * 1024,
                        [[1024, 128], [128 * 1024, ntt], [1, 1024]]),
                bass.AP(yac, 0, [[ntt * 1024, 128], [1024, ntt], [1, 1024]])), 16)
            ldz2.done(ch, b.ct)
        H["p9_sty"] = stym


def p8_mlp_final(b, ST, H, C):
    nc, t_, v_, s_, g_, y_ = b.nc, b.t, b.v, b.s, b.g, b.y
    T = H["T"]
    l1w = H["l1w"]
    CH = 256
    nCH = T // CH
    with ExitStack() as S:
        sb = lambda n, s, d: S.enter_context(nc.sbuf_tensor(n, s, d))
        w1sb = sb("p8w1", [128, 8 * D_MLP], BF16)
        w2sb = sb("p8w2", [128, 32 * 1024], BF16)
        b1v = sb("p8b1", [128, 32], F32)
        b2v = sb("p8b2", [128, 8], F32)
        l2g = sb("p8l2g", [128, 8], F32)
        l2b = sb("p8l2b", [128, 8], F32)
        h2f = [sb("p8h%d" % i, [128, 8 * CH], BF16) for i in range(2)]
        g1 = sb("p8g1", [128, 32 * CH], BF16)
        mo = sb("p8mo", [128, CH], F32)
        mcomb = sb("p8mc", [128, 2 * 1024], F32)
        x1l = [sb("p8x1_%d" % i, [128, 1024], F32) for i in range(2)]
        x2l = [sb("p8x2_%d" % i, [128, 1024], F32) for i in range(2)]
        mol = [sb("p8mol%d" % i, [128, 1024], F32) for i in range(2)]
        ps_g = [S.enter_context(nc.psum_tensor("p8pg%d" % i, [128, CH], F32)) for i in range(2)]
        ps_m = [S.enter_context(nc.psum_tensor("p8m%d" % i, [128, CH], F32)) for i in range(2)]
        ps_tf = [S.enter_context(nc.psum_tensor("p8tf%d" % i, [128, 128], F32)) for i in range(2)]
        ldw = b.sem("p8ldw")
        for cc in range(8):
            ldw.inc(g_.dma_start(w1sb[:, cc * D_MLP:(cc + 1) * D_MLP],
                                 l1w["mlp_W1"][cc * 128:(cc + 1) * 128, :]), 16)
        for fc in range(32):
            ldw.inc(g_.dma_start(w2sb[:, fc * 1024:(fc + 1) * 1024],
                                 l1w["mlp_W2"][fc * 128:(fc + 1) * 128, :]), 16)
        ldv = b.sem("p8ldv")
        vecload(b, ldv, y_, b1v, l1w["mlp_b1"], 32)
        vecload(b, ldv, y_, b2v, l1w["mlp_b2"], 8)
        vecload(b, ldv, y_, l2g, l1w["ln2_g"], 8)
        vecload(b, ldv, y_, l2b, l1w["ln2_b"], 8)
        ldh = DBuf(b, y_, 2, "p8ldh")
        ldf = DBuf(b, y_, 2, "p8ldf")
        stf = DBuf(b, y_, 2, "p8stf")
        ldw.wait(t_)
        ldv.wait(v_)
        ldv.wait(s_)
        ldv.wait(g_)
        gfree = [None, None]
        lnc8 = [None, None]
        combc = None

        def ldh_issue(i):
            ldh.gate(i)
            for cc in range(8):
                ldh.slot(i).inc(y_.dma_start(
                    h2f[i % 2][:, cc * CH:(cc + 1) * CH],
                    H["d_h2"][i * CH:(i + 1) * CH, cc * 128:(cc + 1) * 128],
                    transpose=True), 16)

        def ln_issue(i):
            ldh.wait(i, g_)
            hfi = h2f[i % 2]
            for cc in range(8):
                b.gm(g_.tensor_scalar(hfi[:, cc * CH:(cc + 1) * CH],
                                      hfi[:, cc * CH:(cc + 1) * CH],
                                      l2g[:, cc:cc + 1], l2b[:, cc:cc + 1],
                                      AL.mult, AL.add))
            lnc8[i % 2] = b.cg.n

        ldh_issue(0)
        if nCH > 1:
            ldh_issue(1)
        for k2 in range(2):
            ldf.gate(k2)
            ldf.slot(k2).inc(y_.dma_start(x1l[k2][:, :], H["d_x1"][k2 * 128:(k2 + 1) * 128, :]), 16)
            ldf.slot(k2).inc(y_.dma_start(x2l[k2][:, :], H["d_x2"][k2 * 128:(k2 + 1) * 128, :]), 16)
            ldf.slot(k2).inc(y_.dma_start(mol[k2][:, :], H["d_moe"][k2 * 128:(k2 + 1) * 128, :]), 16)
        ln_issue(0)
        for j in range(nCH):
            hf = h2f[j % 2]
            t_.wait_ge(b.cg.h, lnc8[j % 2])
            for fc in range(32):
                pb = fc % 2
                if gfree[pb] is not None:
                    t_.wait_ge(b.cs.h, gfree[pb])
                last = None
                for cc in range(8):
                    last = t_.matmul(
                        ps_g[pb][:, :],
                        w1sb[:, cc * D_MLP + fc * 128: cc * D_MLP + (fc + 1) * 128],
                        hf[:, cc * CH:(cc + 1) * CH],
                        start=(cc == 0), stop=(cc == 7))
                b.tm(last)
                b.wt(s_)
                b.sm(s_.activation(g1[:, fc * CH:(fc + 1) * CH], ps_g[pb][:, :],
                                   AF.Gelu, bias=b1v[:, fc:fc + 1]))
                gfree[pb] = b.cs.n
            ldh.done(j, b.ct)
            if j + 2 < nCH:
                ldh_issue(j + 2)
            if j + 1 < nCH:
                ln_issue(j + 1)
            b.ws(t_)
            mfree = [None, None]
            for co in range(8):
                pb = co % 2
                if mfree[pb] is not None:
                    t_.wait_ge(b.cs.h, mfree[pb])
                last = None
                for fc in range(32):
                    last = t_.matmul(
                        ps_m[pb][:, :],
                        w2sb[:, fc * 1024 + co * 128: fc * 1024 + (co + 1) * 128],
                        g1[:, fc * CH: fc * CH + CH],
                        start=(fc == 0), stop=(fc == 31))
                b.tm(last)
                b.wt(s_)
                b.sm(s_.activation(mo[:, :], ps_m[pb][:, :], AF.Identity,
                                   bias=b2v[:, co:co + 1]))
                mfree[pb] = b.cs.n
                b.ws(t_)
                tf_free = getattr(b, "_p8tfree", [None, None])
                for tt in range(CH // 128):
                    sl = tt % 2
                    if tf_free[sl] is not None:
                        t_.wait_ge(b.cs.h, tf_free[sl])
                    b.tm(t_.transpose(ps_tf[sl][:, :], mo[:, tt * 128:(tt + 1) * 128],
                                      C["ident_f"][:, :]))
                    b.wt(s_)
                    if combc is not None and co == 0 and tt == 0:
                        s_.wait_ge(b.cv.h, combc)
                    b.sm(s_.copy(
                        mcomb[:, tt * 1024 + co * 128: tt * 1024 + (co + 1) * 128],
                        ps_tf[sl][:, :]))
                    tf_free[sl] = b.cs.n
                b._p8tfree = tf_free
            for tt in range(CH // 128):
                r0 = j * CH + tt * 128
                k2 = (CH // 128) * j + tt
                ldf.wait(k2, v_)
                stf.wait_all(v_)
                b.ws(v_)
                ob = x1l[k2 % 2]
                b.vm(v_.tensor_add(ob[:, :], ob[:, :], x2l[k2 % 2][:, :]))
                b.wv(v_)
                b.vm(v_.tensor_add(ob[:, :], ob[:, :], mol[k2 % 2][:, :]))
                b.wv(v_)
                b.vm(v_.tensor_add(ob[:, :], ob[:, :], mcomb[:, tt * 1024:(tt + 1) * 1024]))
                b.wv(v_)
                stf.gate(k2)
                b.wv(y_)
                stf.slot(k2).inc(y_.dma_start(H["out_t"][r0:r0 + 128, :], ob[:, :]), 16)
                ldf.done(k2, stf.sems[k2 % 2])
                if k2 + 2 < 2 * nCH:
                    r2 = (k2 + 2) * 128
                    ldf.gate(k2 + 2)
                    ldf.slot(k2 + 2).inc(y_.dma_start(x1l[k2 % 2][:, :],
                                                      H["d_x1"][r2:r2 + 128, :]), 16)
                    ldf.slot(k2 + 2).inc(y_.dma_start(x2l[k2 % 2][:, :],
                                                      H["d_x2"][r2:r2 + 128, :]), 16)
                    ldf.slot(k2 + 2).inc(y_.dma_start(mol[k2 % 2][:, :],
                                                      H["d_moe"][r2:r2 + 128, :]), 16)
            combc = b.cv.n
        stf.wait_all(y_)
        H["p8_stf"] = stf


_NCORES = 8
_T_FULL = 4096


def kernel(**inputs):
    from concourse.bass_utils import run_bass_kernel_spmd
    T = _T_FULL
    nc = build_program(T, _NCORES, upto="full")
    consts = host_constants(T)
    x_full = np.ascontiguousarray(np.asarray(inputs["L0_attn_x"], dtype=np.float32))
    in_maps = []
    for c in range(_NCORES):
        m = {}
        for k, v in inputs.items():
            if k == "L0_attn_x":
                m[k] = np.ascontiguousarray(x_full[c])
            else:
                m[k] = np.ascontiguousarray(np.asarray(v, dtype=np.float32))
        for k, v in consts.items():
            m[k] = np.ascontiguousarray(v)
        in_maps.append(m)
    res = run_bass_kernel_spmd(nc, in_maps, core_ids=list(range(_NCORES)))
    out = np.stack([np.asarray(r["out"], dtype=np.float32) for r in res.results], axis=0)
    return out


# version 20 (final)
# speedup vs baseline: 1.1038x; 1.1038x over previous
import math
from contextlib import ExitStack

import numpy as np
import ml_dtypes
import concourse.bass as bass
import concourse.mybir as mybir

DEBUG_P1B = False

F32 = mybir.dt.float32
BF16 = mybir.dt.bfloat16
AF = mybir.ActivationFunctionType
AL = mybir.AluOpType
AX = mybir.AxisListType.X

D = 1024
NH = 8
HD = 128
I_FFN = 2816
NI = I_FFN // 128
NE = 8
F_E = 1024
D_MLP = 4096
QMAX = 127.0
MAGIC = 1.5 * 2 ** 23
NEGBIG = -3.0e38
EPS_LN = 1e-5
RT128 = float(math.sqrt(128.0))


def bfc(x):
    return np.asarray(x, dtype=ml_dtypes.bfloat16)


class Sem:
    def __init__(self, nc, stack, name):
        self.h = stack.enter_context(nc.semaphore(name))
        self.n = 0

    def inc(self, instr, k=1):
        instr.then_inc(self.h, k)
        self.n += k
        return instr

    def wait(self, eng, val=None):
        eng.wait_ge(self.h, self.n if val is None else val)


class Bld:
    def __init__(self, nc, stack):
        self.nc = nc
        self.stack = stack
        self.t, self.v, self.s, self.g, self.y = (
            nc.tensor, nc.vector, nc.scalar, nc.gpsimd, nc.sync)
        self._i = 0
        self.ct = self.sem("ch_t")
        self.cv = self.sem("ch_v")
        self.cs = self.sem("ch_s")
        self.cg = self.sem("ch_g")

    def sem(self, name=None):
        self._i += 1
        s = Sem(self.nc, self.stack, name or "s%d" % self._i)
        if not hasattr(self, "all_sems"):
            self.all_sems = []
        self.all_sems.append(s)
        return s

    def full_barrier(self):
        for eng in (self.t, self.v, self.s, self.g, self.y):
            for s in self.all_sems:
                if s.n:
                    eng.wait_ge(s.h, s.n)

    def begin_phase(self):
        self._ph_mark = len(self.all_sems)
        self._ph_snap = self.nc.snapshot_sems()
        self.full_barrier()

    def end_phase(self):
        self.full_barrier()
        freed = self.nc.allocated_since(self._ph_snap)
        if freed:
            for eng in (self.t, self.v, self.s, self.g, self.y):
                eng.drain()
            self.nc.all_engine_barrier()
            self.nc.clear_and_free_semaphores(freed)
            self.nc.all_engine_barrier()
        del self.all_sems[self._ph_mark:]

    def chain(self, eng):
        return {id(self.t): self.ct, id(self.v): self.cv,
                id(self.s): self.cs, id(self.g): self.cg}[id(eng)]

    def tm(self, ins):
        return self.ct.inc(ins)

    def vm(self, ins):
        return self.cv.inc(ins)

    def sm(self, ins):
        return self.cs.inc(ins)

    def gm(self, ins):
        return self.cg.inc(ins)

    def wt(self, eng):
        self.ct.wait(eng)

    def wv(self, eng):
        self.cv.wait(eng)

    def ws(self, eng):
        self.cs.wait(eng)

    def wg(self, eng):
        self.cg.wait(eng)

    def wall(self, eng):
        self.wt(eng); self.wv(eng); self.ws(eng); self.wg(eng)


class DBuf:

    def __init__(self, b, eng, nslots=2, name=None):
        self.b = b
        self.eng = eng
        self.sems = [b.sem((name or "db") + str(i)) for i in range(nslots)]
        self.free = [None] * nslots
        self.k = nslots

    def slot(self, i):
        return self.sems[i % self.k]

    def gate(self, i):
        f = self.free[i % self.k]
        if f is not None:
            sem, cnt = f
            self.eng.wait_ge(sem.h, cnt)
        s = self.sems[i % self.k]
        if s.n:
            self.eng.wait_ge(s.h, s.n)

    def done(self, i, sem):
        self.free[i % self.k] = (sem, sem.n)

    def wait(self, i, eng):
        self.sems[i % self.k].wait(eng)

    def wait_all(self, eng):
        for s in self.sems:
            s.wait(eng)


def host_constants(T):
    c = {}
    c["c_ident_bf"] = bfc(np.eye(128, dtype=np.float32))
    c["c_ident_f32"] = np.eye(128, dtype=np.float32)
    c["c_onesrow_f32"] = np.ones((1, 128), dtype=np.float32)
    c["c_ones128_f32"] = np.ones((128, 128), dtype=np.float32)
    c["c_onescol_f32"] = np.ones((128, 1), dtype=np.float32)
    m = np.zeros((4, 128, 512), dtype=np.float32)
    fq = np.arange(512)[None, :]
    pk = np.arange(128)[:, None]
    for j in range(4):
        m[j] = (fq >= j * 128 + pk).astype(np.float32)
    c["c_mask_l0"] = bfc(m.reshape(512, 512))
    a = np.zeros((4, 128, 512), dtype=np.float32)
    fk = np.arange(512)[None, :]
    pq = np.arange(128)[:, None]
    for j in range(4):
        a[j] = np.where(fk <= j * 128 + pq, 0.0, NEGBIG).astype(np.float32)
    c["c_maskadd_l1"] = a.reshape(512, 512)
    t = np.array([[(i >> 2 & 1) * 2 - 1, (i >> 1 & 1) * 2 - 1, (i & 1) * 2 - 1]
                  for i in range(8)], dtype=np.float32)
    dirs = t / np.sqrt(3.0, dtype=np.float32)
    c["c_dirsT"] = np.ascontiguousarray(dirs.T)
    dp = np.zeros((1, 8 * 128), dtype=np.float32)
    for h in range(8):
        dp[0, h * 128:h * 128 + 3] = dirs[h]
    c["c_dirs_pad"] = dp
    eh = np.zeros((128, 64), dtype=np.float32)
    for h in range(8):
        eh[:, h * 8 + h] = 1.0
    c["c_eh"] = eh
    half = 512
    freqs = np.exp(-np.arange(half, dtype=np.float32)
                   * np.float32(math.log(10000.0) / half)).astype(np.float32)
    th = np.arange(T, dtype=np.float32)[:, None] * freqs[None, :]
    c["c_rope_cos"] = np.cos(th).astype(np.float32)
    c["c_rope_sin"] = np.sin(th).astype(np.float32)
    s_ = np.arange(128)[:, None]
    t_ = np.arange(128)[None, :]
    c["c_tril_incl"] = (s_ <= t_).astype(np.float32)
    return c


CONST_SPECS = {
    "c_ident_bf": ([128, 128], BF16),
    "c_ident_f32": ([128, 128], F32),
    "c_onesrow_f32": ([1, 128], F32),
    "c_ones128_f32": ([128, 128], F32),
    "c_onescol_f32": ([128, 1], F32),
    "c_mask_l0": ([512, 512], BF16),
    "c_maskadd_l1": ([512, 512], F32),
    "c_dirsT": ([3, 8], F32),
    "c_dirs_pad": ([1, 1024], F32),
    "c_eh": ([128, 64], F32),
    "c_rope_cos": (None, F32),
    "c_rope_sin": (None, F32),
    "c_tril_incl": ([128, 128], F32),
}


def build_program(T, n_cores, upto="full"):
    nT = T // 128
    nC = T // 512
    N_ALL = n_cores * T
    CAP = min(int(math.ceil(N_ALL / NE) * 1.0), N_ALL)

    nc = bass.Bass("TRN2", target_bir_lowering=False)

    x_in = nc.dram_tensor("L0_attn_x", [T, D], F32, kind="ExternalInput")
    w_qkv = nc.dram_tensor("L0_attn_w_qkv", [D, 3 * D], F32, kind="ExternalInput")
    w_out0 = nc.dram_tensor("L0_attn_w_out", [D, D], F32, kind="ExternalInput")
    head_sc = nc.dram_tensor("L0_attn_head_scales", [8], F32, kind="ExternalInput")
    clip_x = nc.dram_tensor("L0_ffn_clip_x", [1], F32, kind="ExternalInput")
    clip_dn = nc.dram_tensor("L0_ffn_clip_down", [1], F32, kind="ExternalInput")
    w_gate = nc.dram_tensor("L0_ffn_w_gate", [I_FFN, D], F32, kind="ExternalInput")
    w_up = nc.dram_tensor("L0_ffn_w_up", [I_FFN, D], F32, kind="ExternalInput")
    w_down = nc.dram_tensor("L0_ffn_w_down", [D, I_FFN], F32, kind="ExternalInput")
    l1w = {}
    for nm, shp in [("Wq", [D, D]), ("Wk", [D, D]), ("Wv", [D, D]),
                    ("gate_W", [D, D]), ("gate_b", [D]), ("p_W", [D, D]),
                    ("p_b", [D]), ("clock_W", [D, 1]), ("clock_b", [1]),
                    ("out_W", [D, D]), ("out_b", [D]), ("ln1_g", [D]),
                    ("ln1_b", [D]), ("ln2_g", [D]), ("ln2_b", [D]),
                    ("mlp_W1", [D, D_MLP]), ("mlp_b1", [D_MLP]),
                    ("mlp_W2", [D_MLP, D]), ("mlp_b2", [D])]:
        l1w[nm] = nc.dram_tensor("L1_attn_" + nm, shp, F32, kind="ExternalInput")
    gw_moe = nc.dram_tensor("L1_ffn_gate_w", [NE, D], F32, kind="ExternalInput")
    w1_moe = nc.dram_tensor("L1_ffn_w1", [NE, F_E, D], F32, kind="ExternalInput")
    w2_moe = nc.dram_tensor("L1_ffn_w2", [NE, D, F_E], F32, kind="ExternalInput")
    out_t = nc.dram_tensor("out", [T, D], F32, kind="ExternalOutput")

    cin = {k: nc.dram_tensor(k, s or [T, 512], dt, kind="ExternalInput")
           for k, (s, dt) in CONST_SPECS.items()}

    dram = lambda n, s, d: nc.dram_tensor(n, s, d, kind="Internal")
    d_xn = dram("d_xn", [T, D], F32)
    d_xn_bf = dram("d_xn_bf", [T, D], BF16)
    d_q0 = dram("d_q0", [D, T], BF16)
    d_k0 = dram("d_k0", [D, T], BF16)
    d_v0 = dram("d_v0", [D, T], BF16)
    d_attn0 = dram("d_attn0", [T, D], F32)
    d_xq = dram("d_xq", [T, D], BF16)
    d_h = dram("d_h", [I_FFN, T], F32)
    d_wg_int = dram("d_wg_int", [I_FFN, D], BF16)
    d_wu_int = dram("d_wu_int", [I_FFN, D], BF16)
    d_wd_int = dram("d_wd_int", [D, I_FFN], BF16)
    d_x1 = dram("d_x1", [T, D], F32)
    d_z = dram("d_z", [T, D], F32)
    d_z_bf = dram("d_z_bf", [T, D], BF16)
    d_zT = dram("d_zT", [D, T], BF16)
    d_zin0 = dram("d_zin0", [T, D], BF16)
    d_p = dram("d_p", [D, T], F32)
    d_pexp0 = dram("d_pexp0", [D, T], BF16)
    d_q1 = dram("d_q1", [T, D], BF16)
    d_k1 = dram("d_k1", [T, D], BF16)
    d_v1 = dram("d_v1", [T, D], BF16)
    d_av = dram("d_av", [T, D], BF16)
    d_x2 = dram("d_x2", [T, D], F32)
    d_h2 = dram("d_h2", [T, D], BF16)
    d_moe = dram("d_moe", [T, D], F32)
    d_w1T = dram("d_w1T", [NE * D, F_E], BF16)
    d_w2T = dram("d_w2T", [NE * F_E, D], BF16)
    d_w1bf = dram("d_w1bf", [F_E, D], BF16)
    d_w2bf = dram("d_w2bf", [D, F_E], BF16)
    d_cc1i = dram("d_cc1i", [1, 16], F32)
    d_cc1o = dram("d_cc1o", [1, 16], F32)
    d_cc2i = dram("d_cc2i", [1, 16], F32)
    d_cc2o = dram("d_cc2o", [1, 16], F32)
    d_R = dram("d_R", [T, NE], F32)
    d_dbg1 = dram("d_dbg1", [512, 512], F32)
    d_dbg2 = dram("d_dbg2", [512, 136], F32)
    d_Rall = dram("d_Rall", [N_ALL, NE], F32)
    rg = [list(range(n_cores))]

    H = dict(T=T, nT=nT, nC=nC, N_ALL=N_ALL, CAP=CAP, rg=rg, x_in=x_in,
             w_qkv=w_qkv, w_out0=w_out0, head_sc=head_sc, clip_x=clip_x,
             clip_dn=clip_dn, w_gate=w_gate, w_up=w_up, w_down=w_down,
             l1w=l1w, gw_moe=gw_moe, w1_moe=w1_moe, w2_moe=w2_moe,
             out_t=out_t, cin=cin, d_xn=d_xn, d_xn_bf=d_xn_bf, d_q0=d_q0,
             d_k0=d_k0, d_v0=d_v0, d_attn0=d_attn0, d_xq=d_xq, d_h=d_h,
             d_wg_int=d_wg_int, d_wu_int=d_wu_int, d_wd_int=d_wd_int,
             d_x1=d_x1, d_z=d_z, d_z_bf=d_z_bf, d_zT=d_zT, d_zin0=d_zin0, d_p=d_p,
             d_pexp0=d_pexp0, d_q1=d_q1, d_k1=d_k1, d_v1=d_v1, d_x2=d_x2,
             d_av=d_av, d_h2=d_h2, d_moe=d_moe, d_w1T=d_w1T, d_w2T=d_w2T,
             d_w1bf=d_w1bf, d_w2bf=d_w2bf, d_cc1i=d_cc1i, d_cc1o=d_cc1o,
             d_cc2i=d_cc2i, d_cc2o=d_cc2o, d_R=d_R, d_Rall=d_Rall,
             d_dbg1=d_dbg1, d_dbg2=d_dbg2)

    with ExitStack() as ST:
        b = Bld(nc, ST)
        C = load_consts(b, ST, H)

        def run(fn, *a):
            b.begin_phase()
            with nc.named_scope(fn.__name__):
                fn(b, *a)
            b.end_phase()

        run(p0_ln0, ST, H, C)
        run(p1a_qkv, ST, H, C)
        run(p1b_attn0, ST, H, C)
        run(p2_quant, ST, H, C)
        run(p3_gateup, ST, H, C)
        run(p4_down_combine, ST, H, C)
        if upto == "l0":
            run(debug_copy, ST, H, d_x1, out_t)
        else:
            run(stage_moe_weights, ST, H, C)
            with ExitStack() as S56:
                H["S56"] = S56
                run(p5_pclock, S56, H, C)
                run(p6_gating, S56, H, C)
            run(p7_attn1, ST, H, C)
            run(p7b_outproj, ST, H, C)
            run(p9_moe, ST, H, C)
            run(p8_mlp_final, ST, H, C)
    return nc


def load_consts(b, ST, H):
    nc, y_, v_ = b.nc, b.y, b.v
    cin = H["cin"]
    sb = lambda n, s, d: ST.enter_context(nc.sbuf_tensor(n, s, d))
    C = {}
    C["ident_bf"] = sb("ident_bf", [128, 128], BF16)
    C["ident_f"] = sb("ident_f", [128, 128], F32)
    C["onesrow"] = sb("onesrow", [1, 128], F32)
    C["ones128"] = sb("ones128", [128, 128], F32)
    C["onescol"] = sb("onescol", [128, 1], F32)
    C["tril"] = sb("tril", [128, 128], F32)
    C["mask_l0"] = sb("mask_l0", [128, 4 * 512], BF16)
    C["maskadd_l1"] = sb("maskadd_l1", [128, 4 * 512], F32)
    C["dirsT"] = sb("dirsT", [3, 8], F32)
    C["dirs_pad"] = sb("dirs_pad", [1, 1024], F32)
    C["eh"] = sb("eh", [128, 64], F32)
    C["hs_row"] = sb("hs_row", [1, 8], F32)
    C["gmax1"] = sb("gmax1", [1, 16], F32)
    C["gmax2"] = sb("gmax2", [1, 16], F32)
    C["scal"] = sb("scal", [128, 16], F32)
    C["bias_h"] = sb("bias_h", [128, 8], F32)
    C["lrec_all"] = sb("lrec_all", [128, 32], F32)
    C["sw_g"] = sb("sw_g", [128, NI], F32)
    C["sw_u"] = sb("sw_u", [128, NI], F32)
    C["sw_d"] = sb("sw_d", [128, 8], F32)
    H["cc1"] = b.sem("cc1")
    H["cc2"] = b.sem("cc2")
    H["ccag"] = b.sem("ccag")
    ld = b.sem("ldc")
    for dst, src in [(C["ident_bf"][:, :], cin["c_ident_bf"][:, :]),
                     (C["ident_f"][:, :], cin["c_ident_f32"][:, :]),
                     (C["onesrow"][:, :], cin["c_onesrow_f32"][:, :]),
                     (C["ones128"][:, :], cin["c_ones128_f32"][:, :]),
                     (C["onescol"][:, :], cin["c_onescol_f32"][:, :]),
                     (C["tril"][:, :], cin["c_tril_incl"][:, :]),
                     (C["dirsT"][:, :], cin["c_dirsT"][:, :]),
                     (C["eh"][:, :], cin["c_eh"][:, :]),
                     (C["dirs_pad"][:, :], cin["c_dirs_pad"][:, :]),
                     (C["hs_row"][:, :], H["head_sc"].ap()[None, :])]:
        ld.inc(y_.dma_start(dst, src), 16)
    for j in range(4):
        ld.inc(y_.dma_start(C["mask_l0"][:, j * 512:(j + 1) * 512],
                            cin["c_mask_l0"][j * 128:(j + 1) * 128, :]), 16)
        ld.inc(y_.dma_start(C["maskadd_l1"][:, j * 512:(j + 1) * 512],
                            cin["c_maskadd_l1"][j * 128:(j + 1) * 128, :]), 16)
    ld.wait(v_)
    C["ld_const"] = ld
    return C


def ln_rowstats(b, st1, xin, c_mean, c_sq, c_var, c_rstd, c_tmp, sqd):
    v_, s_ = b.v, b.s
    b.wv(s_)
    b.vm(v_.reduce_sum(st1[:, c_mean:c_mean + 1], xin[:, :], axis=AX))
    b.sm(s_.activation(sqd[:, :], xin[:, :], AF.Square,
                       accum_out=st1[:, c_sq:c_sq + 1]))
    b.wv(v_)
    b.ws(v_)
    b.vm(v_.tensor_scalar_mul(st1[:, c_mean:c_mean + 1], st1[:, c_mean:c_mean + 1], 1.0 / D))
    b.wv(v_)
    b.vm(v_.tensor_mul(st1[:, c_var:c_var + 1], st1[:, c_mean:c_mean + 1],
                       st1[:, c_mean:c_mean + 1]))
    b.vm(v_.tensor_scalar(st1[:, c_tmp:c_tmp + 1], st1[:, c_sq:c_sq + 1],
                          1.0 / D, None, AL.mult))
    b.wv(v_)
    b.vm(v_.tensor_sub(st1[:, c_var:c_var + 1], st1[:, c_tmp:c_tmp + 1],
                       st1[:, c_var:c_var + 1]))
    b.wv(v_)
    b.vm(v_.tensor_scalar_add(st1[:, c_var:c_var + 1], st1[:, c_var:c_var + 1], EPS_LN))
    b.wv(s_)
    b.sm(s_.activation(st1[:, c_rstd:c_rstd + 1], st1[:, c_var:c_var + 1], AF.Sqrt))
    b.ws(v_)
    b.vm(v_.reciprocal(st1[:, c_rstd:c_rstd + 1], st1[:, c_rstd:c_rstd + 1]))
    b.wv(v_)
    b.vm(v_.tensor_mul(st1[:, c_tmp:c_tmp + 1], st1[:, c_rstd:c_rstd + 1],
                       st1[:, c_rstd:c_rstd + 1]))
    b.wv(v_)
    b.vm(v_.tensor_mul(st1[:, c_tmp:c_tmp + 1], st1[:, c_tmp:c_tmp + 1],
                       st1[:, c_var:c_var + 1]))
    b.wv(v_)
    b.vm(v_.tensor_scalar(st1[:, c_tmp:c_tmp + 1], st1[:, c_tmp:c_tmp + 1],
                          -0.5, 1.5, AL.mult, AL.add))
    b.wv(v_)
    b.vm(v_.tensor_mul(st1[:, c_rstd:c_rstd + 1], st1[:, c_rstd:c_rstd + 1],
                       st1[:, c_tmp:c_tmp + 1]))
    b.wv(v_)


def p0_ln0(b, ST, H, C):
    nc, t_, v_, s_, g_, y_ = b.nc, b.t, b.v, b.s, b.g, b.y
    T, nT = H["T"], H["nT"]
    with ExitStack() as S:
        sb = lambda n, s, d: S.enter_context(nc.sbuf_tensor(n, s, d))
        xt = [sb("p0x%d" % i, [128, D], F32) for i in range(2)]
        xn = [sb("p0n%d" % i, [128, D], F32) for i in range(2)]
        sqd = sb("p0sq", [128, D], F32)
        st1 = sb("p0st", [128, 8], F32)
        ps_tr = S.enter_context(nc.psum_tensor("p0ps", [1, 128], F32))
        ldb = DBuf(b, y_, 2, "p0ld")
        stb = DBuf(b, g_, 2, "p0st")
        b.vm(v_.memset(st1[:, 5:6], -1.0))
        for i in range(nT + 1):
            if i < nT:
                ldb.gate(i)
                ldb.slot(i).inc(y_.dma_start(xt[i % 2][:, :],
                                             H["x_in"][i * 128:(i + 1) * 128, :]), 16)
            if i == 0:
                continue
            j = i - 1
            xj, nj = xt[j % 2], xn[j % 2]
            ldb.wait(j, v_)
            ldb.wait(j, s_)
            if j >= 2:
                stb.sems[j % 2].wait(v_)
            ln_rowstats(b, st1, xj, 0, 1, 2, 3, 4, sqd)
            b.vm(v_.tensor_scalar(nj[:, :], xj[:, :], st1[:, 0:1], st1[:, 3:4],
                                  AL.subtract, AL.mult))
            b.wv(v_)
            b.vm(v_.reduce_max(st1[:, 4:5], nj[:, :], axis=AX,
                               apply_absolute_value=True))
            b.wv(v_)
            b.vm(v_.tensor_max(st1[:, 5:6], st1[:, 5:6], st1[:, 4:5]))
            stb.gate(j)
            b.wv(g_)
            stb.slot(j).inc(g_.dma_start(H["d_xn"][j * 128:(j + 1) * 128, :], nj[:, :]), 16)
            stb.slot(j).inc(g_.dma_start(H["d_xn_bf"][j * 128:(j + 1) * 128, :], nj[:, :]), 16)
            ldb.done(j, b.cv)
        b.wv(t_)
        b.tm(t_.matmul(ps_tr[:, :], st1[:, 5:6], C["ident_f"][:, :],
                       start=True, stop=True))
        b.wt(v_)
        b.vm(v_.memset(C["gmax1"][:, :], 0.0))
        b.wv(v_)
        b.vm(v_.reduce_max(C["gmax1"][:, 0:1], ps_tr[:, :], axis=AX))
        b.wv(y_)
        stp = b.sem("p0cc")
        stp.inc(y_.dma_start(H["d_cc1i"][:, :], C["gmax1"][:, :]), 16)
        stp.wait(g_)
        b.wg(g_)
        cc1 = H["cc1"]
        cc1.inc(g_.collective_compute("AllReduce", AL.max, replica_groups=H["rg"],
                                      ins=[H["d_cc1i"].ap()], outs=[H["d_cc1o"].ap()]))
        H["p0_stb"] = stb


def p1a_qkv(b, ST, H, C):
    nc, t_, v_, s_, g_, y_ = b.nc, b.t, b.v, b.s, b.g, b.y
    T, nC = H["T"], H["nC"]
    with ExitStack() as S:
        sb = lambda n, s, d: S.enter_context(nc.sbuf_tensor(n, s, d))
        wq = sb("wqkv", [128, 8 * 3072], BF16)
        xnf = [sb("xnf%d" % i, [128, 8 * 512], BF16) for i in range(2)]
        evb = sb("evb", [128, 3 * 512], BF16)
        tmpf = sb("tmpf", [128, 512], F32)
        sqf = sb("sqf", [128, 2 * 512], F32)
        kpr = sb("kpr", [1, 512], F32)
        kf32 = sb("kf32", [128, 512], F32)
        qn2 = sb("qn2", [8, nC * 512], F32)
        kn2 = sb("kn2", [8, nC * 512], F32)
        nrm = sb("nrm", [8, 4], F32)
        negb = sb("negb", [1, 8], F32)
        bias_h = C["bias_h"]
        ps_a = [S.enter_context(nc.psum_tensor("p1a%d" % i, [128, 512], F32)) for i in range(2)]
        ps_q = S.enter_context(nc.psum_tensor("p1q", [8, 512], F32))
        ps_k = S.enter_context(nc.psum_tensor("p1k", [8, 512], F32))
        ps_kp = S.enter_context(nc.psum_tensor("p1kp", [1, 512], F32))
        ps_b = S.enter_context(nc.psum_tensor("p1bb", [128, 8], F32))
        ps_geo = S.enter_context(nc.psum_tensor("p1geo", [128, 512], F32))

        ldw = b.sem("p1_ldw")
        for cc in range(8):
            ldw.inc(g_.dma_start(wq[:, cc * 3072:(cc + 1) * 3072],
                                 H["w_qkv"][cc * 128:(cc + 1) * 128, :]), 16)
        ldx = DBuf(b, y_, 2, "p1ldx")
        stq = DBuf(b, g_, 3, "p1st")
        ldw.wait(t_)
        b.wv(s_)
        ps_free = [None, None]
        pending = []

        def q_epi(j, h, ps, ev, idx, sq):
            def emit():
                stq.sems[idx % 3].wait(s_)
                b.wt(s_)
                b.sm(s_.copy(ev, ps[:, :]))
                b.sm(s_.activation(sq, ps[:, :], AF.Square))
                ps_free[idx % 2] = (b.cs, b.cs.n)
                stq.gate(idx)
                b.ws(g_)
                stq.slot(idx).inc(g_.dma_start(
                    H["d_q0"][h * 128:(h + 1) * 128, j * 512:(j + 1) * 512], ev), 16)
                return ("stat", ps_q, h, sq)
            return emit

        def k_epi(j, h, ps, ev, idx, sq):
            def emit():
                stq.sems[idx % 3].wait(s_)
                b.wt(s_)
                b.sm(s_.copy(tmpf[:, :], ps[:, :]))
                b.ws(t_)
                b.tm(t_.matmul(ps_kp[:, :], C["dirsT"][0:3, h:h + 1],
                               tmpf[0:3, :], start=True, stop=True))
                b.wt(s_)
                b.sm(s_.activation(kpr[:, :], ps_kp[:, :], AF.Identity,
                                   scale=C["hs_row"][0:1, h:h + 1]))
                b.ws(t_)
                b.wv(t_)
                b.tm(t_.matmul(ps_geo[:, :],
                               C["dirs_pad"][0:1, h * 128:(h + 1) * 128],
                               kpr[:, :], start=True, stop=True))
                b.wt(v_)
                b.ws(v_)
                b.vm(v_.tensor_scalar(kf32[:, :], ps[:, :], 1.0 / RT128,
                                      None, AL.mult))
                b.wv(v_)
                b.vm(v_.tensor_add(kf32[:, :], kf32[:, :], ps_geo[:, :]))
                b.wv(s_)
                b.sm(s_.copy(ev, kf32[:, :]))
                b.sm(s_.activation(sq, kf32[:, :], AF.Square))
                ps_free[idx % 2] = (b.cs, b.cs.n)
                stq.gate(idx)
                b.ws(g_)
                stq.slot(idx).inc(g_.dma_start(
                    H["d_k0"][h * 128:(h + 1) * 128, j * 512:(j + 1) * 512], ev), 16)
                return ("stat", ps_k, h, sq)
            return emit

        def v_epi(j, h, ps, ev, idx):
            def emit():
                stq.sems[idx % 3].wait(s_)
                b.wt(s_)
                b.sm(s_.copy(ev, ps[:, :]))
                ps_free[idx % 2] = (b.cs, b.cs.n)
                stq.gate(idx)
                b.ws(g_)
                stq.slot(idx).inc(g_.dma_start(
                    H["d_v0"][h * 128:(h + 1) * 128, j * 512:(j + 1) * 512], ev), 16)
                return None
            return emit

        pend_stat = []

        def flush_one():
            if not pending:
                return
            emit = pending.pop(0)
            r = emit()
            if r is not None:
                _, pst, h, sq = r
                pend_stat.append((pst, h, sq, b.cs.n))

        def flush_stats():
            while pend_stat:
                pst, h, sq, cnt = pend_stat.pop(0)
                t_.wait_ge(b.cs.h, cnt)
                b.tm(t_.matmul(pst[:, :], C["eh"][:, h * 8:(h + 1) * 8],
                               sq, start=(h == 0), stop=(h == 7)))

        for i in range(nC + 1):
            if i < nC:
                ldx.gate(i)
                for cc in range(8):
                    ldx.slot(i).inc(y_.dma_start(
                        xnf[i % 2][:, cc * 512:(cc + 1) * 512],
                        H["d_xn_bf"][i * 512:(i + 1) * 512, cc * 128:(cc + 1) * 128],
                        transpose=True), 16)
            if i == 0:
                continue
            j = i - 1
            xf = xnf[j % 2]
            ldx.wait(j, t_)
            for co in range(24):
                idx = j * 24 + co
                pb = co % 2
                ps = ps_a[pb]
                h = co % 8
                if ps_free[pb] is not None:
                    sem, cnt = ps_free[pb]
                    t_.wait_ge(sem.h, cnt)
                last = None
                for cc in range(8):
                    last = t_.matmul(
                        ps[:, :],
                        wq[:, cc * 3072 + co * 128: cc * 3072 + (co + 1) * 128],
                        xf[:, cc * 512:(cc + 1) * 512],
                        start=(cc == 0), stop=(cc == 7))
                b.tm(last)
                flush_stats()
                ev = evb[:, (idx % 3) * 512:(idx % 3 + 1) * 512]
                sq = sqf[:, (co % 2) * 512:(co % 2 + 1) * 512]
                if co < 8:
                    pending.append(q_epi(j, h, ps, ev, idx, sq))
                elif co < 16:
                    pending.append(k_epi(j, h, ps, ev, idx, sq))
                else:
                    pending.append(v_epi(j, h, ps, ev, idx))
                flush_one()
            while pending:
                flush_one()
            flush_stats()
            b.wt(v_)
            b.vm(v_.tensor_copy(qn2[:, j * 512:(j + 1) * 512], ps_q[0:8, :]))
            b.vm(v_.tensor_copy(kn2[:, j * 512:(j + 1) * 512], ps_k[0:8, :]))
            b.wv(t_)
            ldx.done(j, b.ct)
        b.wv(v_)
        b.vm(v_.reduce_max(nrm[:, 0:1], qn2[0:8, :], axis=AX))
        b.vm(v_.reduce_max(nrm[:, 1:2], kn2[0:8, :], axis=AX))
        b.wv(v_)
        b.vm(v_.tensor_mul(nrm[:, 2:3], nrm[:, 0:1], nrm[:, 1:2]))
        b.wv(s_)
        b.sm(s_.activation(nrm[:, 3:4], nrm[:, 2:3], AF.Sqrt))
        b.ws(t_)
        b.tm(t_.matmul(ps_kp[:, 0:8], nrm[0:8, 3:4], C["ident_f"][0:8, 0:8],
                       start=True, stop=True))
        b.wt(s_)
        b.sm(s_.activation(negb[:, :], ps_kp[:, 0:8], AF.Identity, scale=-1.01))
        b.ws(v_)
        b.vm(v_.tensor_scalar_add(negb[:, :], negb[:, :], 60.0))
        b.wv(t_)
        b.tm(t_.matmul(ps_b[:, :], C["onesrow"][:, :], negb[:, :],
                       start=True, stop=True))
        b.wt(v_)
        b.vm(v_.tensor_copy(bias_h[:, :], ps_b[:, :]))
        H["bias_h"] = bias_h
        H["p1_stq"] = stq


def p1b_attn0(b, ST, H, C):
    nc, t_, v_, s_, g_, y_ = b.nc, b.t, b.v, b.s, b.g, b.y
    T, nT = H["T"], H["nT"]
    nG = T // 512
    with ExitStack() as S:
        sb = lambda n, s, d: S.enter_context(nc.sbuf_tensor(n, s, d))
        k0sb = sb("k0sb", [128, 8 * T], BF16)
        vsb = sb("vsb", [128, 8 * nT * 144], BF16)
        q0g = [sb("q0g%d" % i, [128, 8 * 512], BF16) for i in range(2)]
        pT = sb("pT", [128, 2 * 512], BF16)
        ongU = sb("ongU", [128, 32 * 144], BF16)
        outf = sb("outf", [128, 8 * 128], BF16)
        lall = sb("lall", [128, 8], F32)
        lrec = sb("lrec", [128, 8], F32)
        rscr = sb("rscr", [128, 16], F32)
        kint = S.enter_context(nc.sbuf_tensor("kint", [128, 8], mybir.dt.int32))
        w0 = sb("w0sb", [128, 8 * 1024], BF16)
        oev = sb("oev", [128, 1024], F32)
        ps_sc = [S.enter_context(nc.psum_tensor("pbs%d" % i, [128, 512], F32)) for i in range(2)]
        ps_av = [S.enter_context(nc.psum_tensor("pba%d" % i, [128, 132], F32)) for i in range(4)]
        ps_tb = S.enter_context(nc.psum_tensor("pbt", [128, 128], BF16))
        ps_o = S.enter_context(nc.psum_tensor("pbo", [128, 512], F32))
        b.vm(b.v.memset(kint[:, :], 0x7EF311C3))
        ldr = b.sem("p1b_res")
        for h in range(8):
            ldr.inc(y_.dma_start(k0sb[:, h * T:(h + 1) * T],
                                 H["d_k0"][h * 128:(h + 1) * 128, :]), 16)
        for h in range(8):
            for kt in range(nT):
                ldr.inc(y_.dma_start(
                    vsb[:, (h * nT + kt) * 144:(h * nT + kt) * 144 + 128],
                    H["d_v0"][h * 128:(h + 1) * 128, kt * 128:(kt + 1) * 128],
                    transpose=True), 16)
        ldw0 = b.sem("p1b_w0")
        for cc in range(8):
            ldw0.inc(g_.dma_start(w0[:, cc * 1024:(cc + 1) * 1024],
                                  H["w_out0"][cc * 128:(cc + 1) * 128, :]), 16)
        b.vm(v_.memset(bass.AP(vsb, 128, [[vsb.shape[1], 128], [144, 8 * nT], [1, 1]]), 1.0))
        ldq = DBuf(b, y_, 2, "p1b_q")
        stA = DBuf(b, g_, 2, "p1b_st")
        ldr.wait(t_)
        ldw0.wait(t_)
        b.wv(t_)
        b.wv(s_)
        sc_free = [None, None]
        pT_free = [None, None]
        av_free = None

        for g in range(nG + 1):
            if g < nG:
                ldq.gate(g)
                for h in range(8):
                    ldq.slot(g).inc(y_.dma_start(
                        q0g[g % 2][:, h * 512:(h + 1) * 512],
                        H["d_q0"][h * 128:(h + 1) * 128, g * 512:(g + 1) * 512]), 16)
            if g == 0:
                continue
            gg = g - 1
            qs = q0g[gg % 2]
            ktmax = 4 * gg + 3
            ldq.wait(gg, t_)
            for h in range(8):
                if av_free is not None:
                    t_.wait_ge(b.cs.h, av_free)
                pending_av = []

                def emit_av(kt, sl, vcnt, scnt):
                    j = kt - 4 * gg
                    qlo = max(0, j)
                    if j >= 0:
                        t_.wait_ge(b.cv.h, vcnt)
                    else:
                        t_.wait_ge(b.cs.h, scnt)
                    for qi in range(qlo, 4):
                        b.tm(t_.matmul(
                            ps_av[qi][:, 0:129],
                            pT[:, sl * 512 + qi * 128: sl * 512 + (qi + 1) * 128],
                            vsb[:, (h * nT + kt) * 144:(h * nT + kt) * 144 + 129],
                            start=(kt == 0), stop=(kt == 4 * gg + qi)))

                for kt in range(ktmax + 1):
                    sl = kt % 2
                    if sc_free[sl] is not None:
                        t_.wait_ge(b.cs.h, sc_free[sl])
                    b.tm(t_.matmul(ps_sc[sl][:, :],
                                   k0sb[:, h * T + kt * 128: h * T + (kt + 1) * 128],
                                   qs[:, h * 512:(h + 1) * 512],
                                   start=True, stop=True))
                    if len(pending_av) >= 2:
                        pa = pending_av.pop(0)
                        emit_av(*pa)
                        pT_free[pa[1]] = b.ct.n
                    b.wt(s_)
                    if pT_free[sl] is not None:
                        s_.wait_ge(b.ct.h, pT_free[sl])
                    b.sm(s_.activation(pT[:, sl * 512:(sl + 1) * 512],
                                       ps_sc[sl][:, :], AF.Exp,
                                       bias=H["bias_h"][:, h:h + 1]))
                    sc_free[sl] = b.cs.n
                    j = kt - 4 * gg
                    if j >= 0:
                        b.ws(v_)
                        b.vm(v_.tensor_mul(pT[:, sl * 512:(sl + 1) * 512],
                                           pT[:, sl * 512:(sl + 1) * 512],
                                           C["mask_l0"][:, j * 512:(j + 1) * 512]))
                    pending_av.append((kt, sl, b.cv.n, b.cs.n))
                while pending_av:
                    pa = pending_av.pop(0)
                    emit_av(*pa)
                    pT_free[pa[1]] = b.ct.n
                b.wt(s_)
                for qi in range(4):
                    b.sm(s_.copy(ongU[:, (qi * 8 + h) * 144: (qi * 8 + h) * 144 + 129],
                                 ps_av[qi][:, 0:129]))
                av_free = b.cs.n
            b.ws(v_)
            b.wt(v_)
            for qi in range(4):
                lv = bass.AP(ongU, (qi * 8) * 144 + 128,
                             [[32 * 144, 128], [144, 8]])
                b.vm(v_.tensor_copy(lall[:, :], lv))
                b.wv(v_)
                safe_recip(b, lrec[:, 0:8], lall[:, 0:8], rscr, kint[:, 0:8], W=8)
                for h in range(8):
                    b.vm(v_.tensor_scalar(
                        ongU[:, (qi * 8 + h) * 144: (qi * 8 + h) * 144 + 128],
                        ongU[:, (qi * 8 + h) * 144: (qi * 8 + h) * 144 + 128],
                        lrec[:, h:h + 1], None, AL.mult))
                b.wv(v_)
            b.wv(t_)
            for qi in range(4):
                qt = 4 * gg + qi
                for cc in range(8):
                    b.tm(t_.transpose(ps_tb[:, :],
                                      ongU[:, (qi * 8 + cc) * 144: (qi * 8 + cc) * 144 + 128],
                                      C["ident_bf"][:, :]))
                    b.wt(s_)
                    b.sm(s_.copy(outf[:, cc * 128:(cc + 1) * 128], ps_tb[:, :]))
                    b.ws(t_)
                osl = 0
                stA.wait_all(v_)
                for hh in range(2):
                    for cc in range(8):
                        b.tm(t_.matmul(ps_o[:, :], outf[:, cc * 128:(cc + 1) * 128],
                                       w0[:, cc * 1024 + hh * 512: cc * 1024 + (hh + 1) * 512],
                                       start=(cc == 0), stop=(cc == 7)))
                    b.wt(v_)
                    b.vm(v_.tensor_copy(oev[:, osl + hh * 512: osl + (hh + 1) * 512],
                                        ps_o[:, :]))
                    b.wv(t_)
                stA.gate(qt)
                b.wv(g_)
                stA.slot(qt).inc(g_.dma_start(
                    H["d_attn0"][qt * 128:(qt + 1) * 128, :],
                    oev[:, 0:1024]), 16)
            ldq.done(gg, b.ct)
        H["p1b_stA"] = stA


def round_clip(b, out_bf, xin, rscale_ap, lo, hi, tmp):
    v_ = b.v
    b.vm(v_.tensor_scalar(tmp[:, :], xin[:, :], rscale_ap, MAGIC, AL.mult, AL.add))
    b.wv(v_)
    b.vm(v_.tensor_scalar_add(tmp[:, :], tmp[:, :], -MAGIC))
    b.wv(v_)
    b.vm(v_.tensor_scalar(out_bf[:, :], tmp[:, :], hi, lo, AL.min, AL.max))
    b.wv(v_)


def p2_quant(b, ST, H, C):
    nc, t_, v_, s_, g_, y_ = b.nc, b.t, b.v, b.s, b.g, b.y
    T, nT = H["T"], H["nT"]
    scal = C["scal"]
    with ExitStack() as S:
        sb = lambda n, s, d: S.enter_context(nc.sbuf_tensor(n, s, d))
        p2s = sb("p2s", [1, 8], F32)
        xl = [sb("p2x%d" % i, [128, D], F32) for i in range(2)]
        tmp = sb("p2t", [128, D], F32)
        qb = [sb("p2q%d" % i, [128, D], BF16) for i in range(2)]
        ps_c = S.enter_context(nc.psum_tensor("p2ps", [128, 2], F32))
        ldc = b.sem("p2ld")
        ldc.inc(y_.dma_start(C["gmax1"][:, :], H["d_cc1o"][:, :]), 16)
        ldc.inc(y_.dma_start(p2s[0:1, 0:1], H["clip_x"].ap()[None, :]), 16)
        ldc.wait(s_)
        b.sm(s_.activation(p2s[:, 1:2], p2s[:, 0:1], AF.Sigmoid))
        b.ws(v_)
        b.vm(v_.tensor_mul(p2s[:, 2:3], p2s[:, 1:2], C["gmax1"][0:1, 0:1]))
        b.wv(v_)
        b.vm(v_.tensor_scalar_mul(p2s[:, 3:4], p2s[:, 2:3], 1.0 / QMAX))
        b.wv(v_)
        b.vm(v_.reciprocal(p2s[:, 2:3], p2s[:, 3:4]))
        b.wv(t_)
        b.tm(t_.matmul(ps_c[:, :], C["onesrow"][:, :], p2s[:, 2:4],
                       start=True, stop=True))
        b.wt(v_)
        b.vm(v_.tensor_copy(scal[:, 0:2], ps_c[:, :]))
        b.wv(v_)
        ldb = DBuf(b, y_, 2, "p2ldx")
        stb = DBuf(b, y_, 2, "p2st")
        for i in range(nT + 1):
            if i < nT:
                ldb.gate(i)
                ldb.slot(i).inc(y_.dma_start(xl[i % 2][:, :],
                                             H["d_xn"][i * 128:(i + 1) * 128, :]), 16)
            if i == 0:
                continue
            j = i - 1
            ldb.wait(j, v_)
            stb.sems[j % 2].wait(v_)
            round_clip(b, qb[j % 2], xl[j % 2], scal[:, 0:1], -QMAX - 1.0, QMAX, tmp)
            stb.gate(j)
            b.wv(y_)
            stb.slot(j).inc(y_.dma_start(H["d_xq"][j * 128:(j + 1) * 128, :],
                                         qb[j % 2][:, :]), 16)
            ldb.done(j, b.cv)
        H["p2_stb"] = stb
    b.full_barrier()
    sw_g = C["sw_g"]
    sw_u = C["sw_u"]
    sw_d = C["sw_d"]
    H["sw_g"], H["sw_u"], H["sw_d"] = sw_g, sw_u, sw_d
    stw = DBuf(b, y_, 2, "p2wst")
    with ExitStack() as S:
        sb = lambda n, s, d: S.enter_context(nc.sbuf_tensor(n, s, d))
        wl = [sb("p2w%d" % i, [128, I_FFN], F32) for i in range(2)]
        wq = [sb("p2wq%d" % i, [128, I_FFN], BF16) for i in range(2)]
        wt = sb("p2wt", [128, I_FFN], F32)
        am = sb("p2am", [128, 4], F32)
        ldb = DBuf(b, y_, 2, "p2wld")
        jobs = []
        for rc in range(NI):
            jobs.append((H["w_gate"], rc, D, H["d_wg_int"], sw_g, rc))
        for rc in range(NI):
            jobs.append((H["w_up"], rc, D, H["d_wu_int"], sw_u, rc))
        for rc in range(8):
            jobs.append((H["w_down"], rc, I_FFN, H["d_wd_int"], sw_d, rc))
        for i in range(len(jobs) + 1):
            if i < len(jobs):
                src, rc, W, dst, swt, scol = jobs[i]
                ldb.gate(i)
                ldb.slot(i).inc(y_.dma_start(wl[i % 2][:, 0:W],
                                             src[rc * 128:(rc + 1) * 128, :]), 16)
            if i == 0:
                continue
            j = i - 1
            src, rc, W, dst, swt, scol = jobs[j]
            ldb.wait(j, v_)
            stb_s = stw.sems[j % 2]
            stb_s.wait(v_)
            b.vm(v_.reduce_max(am[:, 0:1], wl[j % 2][:, 0:W], axis=AX,
                               apply_absolute_value=True))
            b.wv(v_)
            b.vm(v_.reciprocal(am[:, 1:2], am[:, 0:1]))
            b.vm(v_.tensor_scalar_mul(swt[:, scol:scol + 1], am[:, 0:1], 1.0 / QMAX))
            b.wv(v_)
            b.vm(v_.tensor_scalar_mul(am[:, 1:2], am[:, 1:2], QMAX))
            b.wv(v_)
            b.vm(v_.tensor_scalar(wt[:, 0:W], wl[j % 2][:, 0:W], am[:, 1:2],
                                  MAGIC, AL.mult, AL.add))
            b.wv(v_)
            b.vm(v_.tensor_scalar_add(wt[:, 0:W], wt[:, 0:W], -MAGIC))
            b.wv(v_)
            b.vm(v_.tensor_scalar(wq[j % 2][:, 0:W], wt[:, 0:W], QMAX,
                                  -QMAX - 1.0, AL.min, AL.max))
            stw.gate(j)
            b.wv(y_)
            stw.slot(j).inc(y_.dma_start(dst[rc * 128:(rc + 1) * 128, :],
                                         wq[j % 2][:, 0:W]), 16)
            ldb.done(j, b.cv)
    H["p2_stw"] = stw


def p3_gateup(b, ST, H, C):
    nc, t_, v_, s_, g_, y_ = b.nc, b.t, b.v, b.s, b.g, b.y
    T, nC = H["T"], H["nC"]
    scal = C["scal"]
    with ExitStack() as S:
        sb = lambda n, s, d: S.enter_context(nc.sbuf_tensor(n, s, d))
        wgT = sb("wgT", [128, 8 * I_FFN], BF16)
        wuT = sb("wuT", [128, 8 * I_FFN], BF16)
        xqf = [sb("xqf%d" % i, [128, 8 * 512], BF16) for i in range(2)]
        sg = sb("p3sg", [128, 512], F32)
        su = sb("p3su", [128, 512], F32)
        hb = [sb("p3h%d" % i, [128, 512], F32) for i in range(2)]
        swe = sb("p3swe", [128, 2 * NI], F32)
        hmax = sb("p3hm", [128, 2], F32)
        ps_g = [S.enter_context(nc.psum_tensor("p3g%d" % i, [128, 512], F32)) for i in range(2)]
        ps_u = [S.enter_context(nc.psum_tensor("p3u%d" % i, [128, 512], F32)) for i in range(2)]
        ps_t = S.enter_context(nc.psum_tensor("p3t", [1, 128], F32))
        ldr = b.sem("p3res")
        for cc in range(8):
            ldr.inc(y_.dma_start(wgT[:, cc * I_FFN:(cc + 1) * I_FFN],
                                 H["d_wg_int"][:, cc * 128:(cc + 1) * 128],
                                 transpose=True), 16)
            ldr.inc(y_.dma_start(wuT[:, cc * I_FFN:(cc + 1) * I_FFN],
                                 H["d_wu_int"][:, cc * 128:(cc + 1) * 128],
                                 transpose=True), 16)
        b.vm(v_.tensor_scalar(swe[:, 0:NI], H["sw_g"][:, :], scal[:, 1:2],
                              None, AL.mult))
        b.vm(v_.tensor_scalar(swe[:, NI:2 * NI], H["sw_u"][:, :], scal[:, 1:2],
                              None, AL.mult))
        b.vm(v_.memset(hmax[:, 0:1], 0.0))
        b.wv(s_)
        ldx = DBuf(b, y_, 2, "p3ldx")
        sth = DBuf(b, y_, 2, "p3sth")
        ldr.wait(t_)
        ps_free_g = [None, None]
        ps_free_u = [None, None]
        for i in range(nC + 1):
            if i < nC:
                ldx.gate(i)
                for cc in range(8):
                    ldx.slot(i).inc(y_.dma_start(
                        xqf[i % 2][:, cc * 512:(cc + 1) * 512],
                        H["d_xq"][i * 512:(i + 1) * 512, cc * 128:(cc + 1) * 128],
                        transpose=True), 16)
            if i == 0:
                continue
            j = i - 1
            xf = xqf[j % 2]
            ldx.wait(j, t_)
            for ic in range(NI):
                pb = ic % 2
                for psl, wT, free in ((ps_g[pb], wgT, ps_free_g),
                                      (ps_u[pb], wuT, ps_free_u)):
                    if free[pb] is not None:
                        sem, cnt = free[pb]
                        t_.wait_ge(sem.h, cnt)
                    last = None
                    for cc in range(8):
                        last = t_.matmul(
                            psl[:, :],
                            wT[:, cc * I_FFN + ic * 128: cc * I_FFN + (ic + 1) * 128],
                            xf[:, cc * 512:(cc + 1) * 512],
                            start=(cc == 0), stop=(cc == 7))
                    b.tm(last)
                b.wt(s_)
                b.wv(s_)
                b.sm(s_.activation(sg[:, :], ps_g[pb][:, :], AF.Silu,
                                   scale=swe[:, ic:ic + 1]))
                b.sm(s_.activation(su[:, :], ps_u[pb][:, :], AF.Identity,
                                   scale=swe[:, NI + ic:NI + ic + 1]))
                ps_free_g[pb] = (b.cs, b.cs.n)
                ps_free_u[pb] = (b.cs, b.cs.n)
                hsl = (j * NI + ic) % 2
                sth.sems[hsl].wait(v_)
                b.ws(v_)
                b.vm(v_.tensor_mul(hb[hsl][:, :], sg[:, :], su[:, :]))
                b.wv(v_)
                b.vm(v_.reduce_max(hmax[:, 1:2], hb[hsl][:, :], axis=AX,
                                   apply_absolute_value=True))
                b.wv(v_)
                b.vm(v_.tensor_max(hmax[:, 0:1], hmax[:, 0:1], hmax[:, 1:2]))
                sth.gate(j * NI + ic)
                b.wv(y_)
                sth.slot(j * NI + ic).inc(y_.dma_start(
                    H["d_h"][ic * 128:(ic + 1) * 128, j * 512:(j + 1) * 512],
                    hb[hsl][:, :]), 16)
            ldx.done(j, b.ct)
        b.wv(t_)
        b.tm(t_.matmul(ps_t[:, :], hmax[:, 0:1], C["ident_f"][:, :],
                       start=True, stop=True))
        b.wt(v_)
        b.vm(v_.memset(C["gmax2"][:, :], 0.0))
        b.wv(v_)
        b.vm(v_.reduce_max(C["gmax2"][:, 0:1], ps_t[:, :], axis=AX))
        b.wv(y_)
        stc = b.sem("p3cc")
        stc.inc(y_.dma_start(H["d_cc2i"][:, :], C["gmax2"][:, :]), 16)
        stc.wait(g_)
        cc2 = H["cc2"]
        cc2.inc(g_.collective_compute("AllReduce", AL.max, replica_groups=H["rg"],
                                      ins=[H["d_cc2i"].ap()], outs=[H["d_cc2o"].ap()]))
        H["p3_sth"] = sth


def p4_down_combine(b, ST, H, C):
    nc, t_, v_, s_, g_, y_ = b.nc, b.t, b.v, b.s, b.g, b.y
    T = H["T"]
    CH = 256
    nCH = T // CH
    scal = C["scal"]
    with ExitStack() as S:
        sb = lambda n, s, d: S.enter_context(nc.sbuf_tensor(n, s, d))
        p4s = sb("p4s", [1, 8], F32)
        wdT = sb("wdT", [128, NI * 1024], BF16)
        hld = [sb("p4h%d" % i, [128, NI * CH], F32) for i in range(2)]
        hq = [sb("p4hq%d" % i, [128, NI * CH], BF16) for i in range(2)]
        yf = sb("p4yf", [128, CH], F32)
        ycomb = sb("p4yc", [128, 2 * 1024], F32)
        xld = [sb("p4x%d" % i, [128, D], F32) for i in range(2)]
        ald = [sb("p4a%d" % i, [128, D], F32) for i in range(2)]
        x1t = sb("p4x1t", [128, D], F32)
        zt = sb("p4z", [128, D], F32)
        zi = sb("p4zi", [128, D], F32)
        sqd = sb("p4sq", [128, D], F32)
        st1 = sb("p4st", [128, 8], F32)
        swe = sb("p4swe", [128, 8], F32)
        ps_d = [S.enter_context(nc.psum_tensor("p4d%d" % i, [128, CH], F32)) for i in range(2)]
        ps_tf = [S.enter_context(nc.psum_tensor("p4tf%d" % i, [128, 128], F32)) for i in range(2)]
        ps_c = S.enter_context(nc.psum_tensor("p4pc", [128, 2], F32))
        ldc = b.sem("p4ld")
        ldc.inc(y_.dma_start(C["gmax2"][:, :], H["d_cc2o"][:, :]), 16)
        ldc.inc(y_.dma_start(p4s[0:1, 0:1], H["clip_dn"].ap()[None, :]), 16)
        ldc.wait(s_)
        b.sm(s_.activation(p4s[:, 1:2], p4s[:, 0:1], AF.Sigmoid))
        b.ws(v_)
        b.vm(v_.tensor_mul(p4s[:, 2:3], p4s[:, 1:2], C["gmax2"][0:1, 0:1]))
        b.wv(v_)
        b.vm(v_.tensor_scalar_mul(p4s[:, 3:4], p4s[:, 2:3], 1.0 / QMAX))
        b.wv(v_)
        b.vm(v_.reciprocal(p4s[:, 2:3], p4s[:, 3:4]))
        b.wv(t_)
        b.tm(t_.matmul(ps_c[:, :], C["onesrow"][:, :], p4s[:, 2:4],
                       start=True, stop=True))
        b.wt(v_)
        b.vm(v_.tensor_copy(scal[:, 2:4], ps_c[:, :]))
        b.wv(v_)
        b.vm(v_.tensor_scalar(swe[:, :], H["sw_d"][:, :], scal[:, 3:4],
                              None, AL.mult))
        ldr = b.sem("p4res")
        for ic in range(NI):
            ldr.inc(y_.dma_start(wdT[:, ic * 1024:(ic + 1) * 1024],
                                 H["d_wd_int"][:, ic * 128:(ic + 1) * 128],
                                 transpose=True), 16)
        ldh = DBuf(b, y_, 2, "p4ldh")
        ldxa = DBuf(b, y_, 2, "p4ldxa")
        stz = DBuf(b, g_, 2, "p4stz")
        st1b = DBuf(b, y_, 2, "p4st1")
        ldr.wait(t_)
        ps_free = [None, None]
        qdone = [None, None]
        combc = None
        ldh.gate(0)
        ldh.slot(0).inc(y_.dma_start(
            bass.AP(hld[0], 0, [[NI * CH, 128], [CH, NI], [1, CH]]),
            bass.AP(H["d_h"], 0, [[T, 128], [128 * T, NI], [1, CH]])), 16)
        for i in range(nCH + 1):
            if i + 1 < nCH:
                ldh.gate(i + 1)
                ldh.slot(i + 1).inc(y_.dma_start(
                    bass.AP(hld[(i + 1) % 2], 0, [[NI * CH, 128], [CH, NI], [1, CH]]),
                    bass.AP(H["d_h"], (i + 1) * CH, [[T, 128], [128 * T, NI], [1, CH]])), 16)
            if i < nCH:
                ldh.wait(i, v_)
                b.wt(v_)
                b.vm(v_.tensor_scalar(hld[i % 2][:, :], hld[i % 2][:, :], scal[:, 2:3],
                                      MAGIC, AL.mult, AL.add))
                b.wv(v_)
                b.vm(v_.tensor_scalar_add(hld[i % 2][:, :], hld[i % 2][:, :], -MAGIC))
                b.wv(v_)
                b.vm(v_.tensor_scalar(hq[i % 2][:, :], hld[i % 2][:, :], QMAX,
                                      -QMAX - 1.0, AL.min, AL.max))
                qdone[i % 2] = b.cv.n
                ldh.done(i, b.cv)
            if i == 0:
                continue
            j = i - 1
            t_.wait_ge(b.cv.h, qdone[j % 2])
            for hc in range(8):
                pb = hc % 2
                if ps_free[pb] is not None:
                    sem, cnt = ps_free[pb]
                    t_.wait_ge(sem.h, cnt)
                last = None
                for ic in range(NI):
                    last = t_.matmul(
                        ps_d[pb][:, :],
                        wdT[:, ic * 1024 + hc * 128: ic * 1024 + (hc + 1) * 128],
                        hq[j % 2][:, ic * CH:(ic + 1) * CH],
                        start=(ic == 0), stop=(ic == NI - 1))
                b.tm(last)
                b.wt(s_)
                b.sm(s_.activation(yf[:, :], ps_d[pb][:, :], AF.Identity,
                                   scale=swe[:, hc:hc + 1]))
                ps_free[pb] = (b.cs, b.cs.n)
                b.ws(t_)
                tf_free = getattr(b, "_p4tfree", [None, None])
                for tt in range(CH // 128):
                    sl = tt % 2
                    if tf_free[sl] is not None:
                        t_.wait_ge(b.cs.h, tf_free[sl])
                    b.tm(t_.transpose(ps_tf[sl][:, :], yf[:, tt * 128:(tt + 1) * 128],
                                      C["ident_f"][:, :]))
                    b.wt(s_)
                    if combc is not None and hc == 0 and tt == 0:
                        s_.wait_ge(b.cv.h, combc)
                    b.sm(s_.copy(
                        ycomb[:, tt * 1024 + hc * 128: tt * 1024 + (hc + 1) * 128],
                        ps_tf[sl][:, :]))
                    tf_free[sl] = b.cs.n
                b._p4tfree = tf_free
            for tt in range(CH // 128):
                r0 = j * CH + tt * 128
                k4 = (CH // 128) * j + tt
                ldxa.gate(k4)
                ldxa.slot(k4).inc(y_.dma_start(xld[k4 % 2][:, :], H["x_in"][r0:r0 + 128, :]), 16)
                ldxa.slot(k4).inc(y_.dma_start(ald[k4 % 2][:, :], H["d_attn0"][r0:r0 + 128, :]), 16)
                ldxa.wait(k4, v_)
                stz.wait_all(v_)
                st1b.wait_all(v_)
                b.vm(v_.tensor_add(x1t[:, :], xld[k4 % 2][:, :], ald[k4 % 2][:, :]))
                b.ws(v_)
                b.wv(v_)
                b.vm(v_.tensor_add(x1t[:, :], x1t[:, :], ycomb[:, tt * 1024:(tt + 1) * 1024]))
                b.wv(v_)
                st1b.gate(k4)
                b.wv(y_)
                st1b.slot(k4).inc(y_.dma_start(H["d_x1"][r0:r0 + 128, :], x1t[:, :]), 16)
                ldxa.done(k4, b.cv)
                ln_rowstats(b, st1, x1t, 0, 1, 2, 3, 4, sqd)
                b.vm(v_.tensor_scalar(zt[:, :], x1t[:, :], st1[:, 0:1], st1[:, 3:4],
                                      AL.subtract, AL.mult))
                b.wv(v_)
                ln_rowstats(b, st1, zt, 0, 1, 2, 3, 4, sqd)
                b.vm(v_.tensor_scalar(zi[:, :], zt[:, :], st1[:, 0:1], st1[:, 3:4],
                                      AL.subtract, AL.mult))
                stz.gate(k4)
                b.wv(g_)
                stz.slot(k4).inc(g_.dma_start(H["d_z"][r0:r0 + 128, :], zt[:, :]), 16)
                stz.slot(k4).inc(g_.dma_start(H["d_z_bf"][r0:r0 + 128, :], zt[:, :]), 16)
                stz.slot(k4).inc(g_.dma_start(H["d_zin0"][r0:r0 + 128, :], zi[:, :]), 16)
            combc = b.cv.n
        H["p4_stz"] = stz
        H["p4_st1"] = st1b


def debug_copy(b, ST, H, src=None, dst=None):
    nc, v_, y_ = b.nc, b.v, b.y
    T = H["T"]
    src = src if src is not None else H["d_x1"]
    dst = dst if dst is not None else H["out_t"]
    fin = b.sem("fin")
    with ExitStack() as S:
        tb = S.enter_context(nc.sbuf_tensor("dbgt", [128, D], F32))
        for i in range(T // 128):
            ld = b.sem(None)
            ld.inc(y_.dma_start(tb[:, :], src[i * 128:(i + 1) * 128, :]), 16)
            ld.wait(y_)
            fin.inc(y_.dma_start(dst[i * 128:(i + 1) * 128, :], tb[:, :]), 16)
            fin.wait(y_)
    fin.wait(y_)


def safe_recip(b, out_ap, in_ap, scr, kint, W=1):
    v_ = b.v
    s0 = scr[:, 0:W]
    s1 = scr[:, W:2 * W]
    s0i = scr.bitcast(mybir.dt.int32)[:, 0:W]
    b.wv(v_)
    b.vm(v_.tensor_copy(s1, in_ap))
    b.wv(v_)
    b.vm(v_.tensor_tensor(s0i, kint, s1.tensor[:, 0:0] if False else
                          scr.bitcast(mybir.dt.int32)[:, W:2 * W], AL.subtract))
    b.wv(v_)
    for _ in range(2):
        b.vm(v_.tensor_mul(out_ap, s1, s0))
        b.wv(v_)
        b.vm(v_.tensor_scalar(out_ap, out_ap, -1.0, 2.0, AL.mult, AL.add))
        b.wv(v_)
        b.vm(v_.tensor_mul(s0, s0, out_ap))
        b.wv(v_)
    b.vm(v_.tensor_copy(out_ap, s0))
    b.wv(v_)


def softplus_emit(b, out_ap, in_ap, tmp_a, tmp_r, ones=None):
    v_, s_ = b.v, b.s
    b.ws(s_)
    b.wv(s_)
    b.sm(s_.activation(tmp_a, in_ap, AF.Abs))
    b.ws(s_)
    b.sm(s_.activation(tmp_a, tmp_a, AF.Exp, scale=-1.0))
    if ones is not None:
        b.ws(s_)
        b.sm(s_.activation(tmp_a, tmp_a, AF.Ln, bias=ones))
    else:
        b.ws(v_)
        b.vm(v_.tensor_scalar_add(tmp_a, tmp_a, 1.0))
        b.wv(s_)
        b.sm(s_.activation(tmp_a, tmp_a, AF.Ln))
    b.vm(v_.tensor_scalar_max(tmp_r, in_ap, 0.0))
    b.ws(v_)
    b.wv(v_)
    b.vm(v_.tensor_add(out_ap, tmp_r, tmp_a))
    b.wv(s_)


def vecload(b, sem, eng, dst_sb, src_dram, n):
    sem.inc(eng.dma_start(bass.AP(dst_sb, 0, [[dst_sb.shape[1], 128], [1, n]]),
                          bass.AP(src_dram, 0, [[1, 128], [128, n], [1, 1]]),
                          allow_slow_non_contiguous=True), 16)


def stage_moe_weights(b, ST, H, C):
    nc, g_, y_ = b.nc, b.g, b.y
    t_, s_ = b.t, b.s
    T, nT = H["T"], H["nT"]
    with ExitStack() as S:
        stg = S.enter_context(nc.sbuf_tensor("stgw", [128, 8 * 1024], BF16))
        ztl = [S.enter_context(nc.sbuf_tensor("ztl%d" % i, [128, 1024], BF16))
               for i in range(2)]
        ztT = [S.enter_context(nc.sbuf_tensor("ztT%d" % i, [128, 1024], BF16))
               for i in range(2)]
        ps_t = [S.enter_context(nc.psum_tensor("stps%d" % i, [128, 128], BF16))
                for i in range(2)]
        ldzt = DBuf(b, s_, 2, "stldz")
        stzt = DBuf(b, s_, 2, "ststz")
        ptfree = [None, None]
        ttdone = [None, None]
        for tt in range(nT + 1):
            if tt < nT:
                ldzt.gate(tt)
                ldzt.slot(tt).inc(s_.dma_start(ztl[tt % 2][:, :],
                                               H["d_z_bf"][tt * 128:(tt + 1) * 128, :]), 16)
            if tt == 0:
                continue
            j = tt - 1
            sl = j % 2
            ldzt.wait(j, t_)
            stzt.wait(j, t_)
            for cc in range(8):
                if ptfree[cc % 2] is not None:
                    t_.wait_ge(b.cs.h, ptfree[cc % 2])
                b.tm(t_.transpose(ps_t[cc % 2][:, :],
                                  ztl[sl][:, cc * 128:(cc + 1) * 128],
                                  C["ident_bf"][:, :]))
                b.wt(s_)
                b.sm(s_.copy(ztT[sl][:, cc * 128:(cc + 1) * 128], ps_t[cc % 2][:, :]))
                ptfree[cc % 2] = b.cs.n
            stzt.gate(j)
            b.ws(s_)
            stzt.slot(j).inc(s_.dma_start(
                bass.AP(H["d_zT"], j * 128, [[T, 128], [128 * T, 8], [1, 128]]),
                bass.AP(ztT[sl], 0, [[1024, 128], [128, 8], [1, 128]])), 16)
            ldzt.done(j, b.ct)
        cast_s = b.sem("mw_cast")
        tr_s = b.sem("mw_tr")
        st_s = b.sem("mw_st")
        for e in range(NE):
            for (wsrc, dbf, dT, R) in ((H["w1_moe"], H["d_w1bf"], H["d_w1T"], F_E),
                                       (H["w2_moe"], H["d_w2bf"], H["d_w2T"], D)):
                Ccols = D if R == F_E else F_E
                g_.wait_ge(tr_s.h, tr_s.n)
                cast_s.inc(g_.dma_start(dbf.ap(), wsrc.ap()[e]), 16)
                y_.wait_ge(cast_s.h, cast_s.n)
                y_.wait_ge(st_s.h, st_s.n)
                for cc in range(Ccols // 128):
                    tr_s.inc(y_.dma_start(stg[:, cc * 1024:cc * 1024 + R],
                                          dbf.ap()[:, cc * 128:(cc + 1) * 128],
                                          transpose=True), 16)
                y_.wait_ge(tr_s.h, tr_s.n)
                st_s.inc(y_.dma_start(
                    bass.AP(dT, e * Ccols * R, [[R, 128], [128 * R, Ccols // 128], [1, R]]),
                    bass.AP(stg, 0, [[8 * 1024, 128], [1024, Ccols // 128], [1, R]])), 16)
        y_.wait_ge(st_s.h, st_s.n)
        H["mw_st"] = st_s


def p5_pclock(b, ST, H, C):
    nc, t_, v_, s_, g_, y_ = b.nc, b.t, b.v, b.s, b.g, b.y
    T, nC = H["T"], H["nC"]
    l1w = H["l1w"]
    clkrow = ST.enter_context(nc.sbuf_tensor("clkrow", [1, T], F32))
    ln1g = ST.enter_context(nc.sbuf_tensor("ln1g", [128, 8], F32))
    ln1b = ST.enter_context(nc.sbuf_tensor("ln1b", [128, 8], F32))
    H["clkrow"], H["ln1g"], H["ln1b"] = clkrow, ln1g, ln1b
    with ExitStack() as S:
        sb = lambda n, s, d: S.enter_context(nc.sbuf_tensor(n, s, d))
        wp = sb("p5wp", [128, 8 * 1024], BF16)
        wclk = sb("p5wc", [128, 8], BF16)
        pbv = sb("p5pb", [128, 8], F32)
        cbv = sb("p5cb", [1, 4], F32)
        cwrk = sb("p5cw", [1, 1536], F32)
        zf = [sb("p5z%d" % i, [128, 8 * 512], BF16) for i in range(2)]
        pf = [sb("p5pf%d" % i, [128, 512], F32) for i in range(2)]
        pmax = sb("p5pm", [128, 18], F32)
        pld = sb("p5pld", [128, T], F32)
        pex = sb("p5pex", [128, T], BF16)
        ps_p = [S.enter_context(nc.psum_tensor("p5p%d" % i, [128, 512], F32)) for i in range(2)]
        ps_c = S.enter_context(nc.psum_tensor("p5c", [1, 512], F32))
        ldw = b.sem("p5ldw")
        for cc in range(8):
            ldw.inc(g_.dma_start(wp[:, cc * 1024:(cc + 1) * 1024],
                                 l1w["p_W"][cc * 128:(cc + 1) * 128, :]), 16)
        ldw.inc(g_.dma_start(bass.AP(wclk, 0, [[8, 128], [1, 8]]),
                             bass.AP(l1w["clock_W"], 0, [[1, 128], [128, 8], [1, 1]]),
                             allow_slow_non_contiguous=True), 16)
        ldv = b.sem("p5ldv")
        vecload(b, ldv, y_, pbv, l1w["p_b"], 8)
        vecload(b, ldv, y_, ln1g, l1w["ln1_g"], 8)
        vecload(b, ldv, y_, ln1b, l1w["ln1_b"], 8)
        ldv.inc(y_.dma_start(cbv[0:1, 0:1], H["l1w"]["clock_b"].ap()[None, :]), 16)
        b.vm(v_.memset(pmax[:, 0:8], -3.0e38))
        ldz = DBuf(b, y_, 2, "p5ldz")
        stp = DBuf(b, y_, 2, "p5stp")
        ldw.wait(t_)
        ldv.wait(v_)
        ldv.wait(s_)
        ps_free = [None, None]
        for i in range(nC + 1):
            if i < nC:
                ldz.gate(i)
                for cc in range(8):
                    ldz.slot(i).inc(y_.dma_start(
                        zf[i % 2][:, cc * 512:(cc + 1) * 512],
                        H["d_zin0"][i * 512:(i + 1) * 512, cc * 128:(cc + 1) * 128],
                        transpose=True), 16)
            if i == 0:
                continue
            j = i - 1
            zc = zf[j % 2]
            ldz.wait(j, v_)
            b.wt(v_)
            for cc in range(8):
                b.vm(v_.tensor_scalar(zc[:, cc * 512:(cc + 1) * 512],
                                      zc[:, cc * 512:(cc + 1) * 512],
                                      ln1g[:, cc:cc + 1], ln1b[:, cc:cc + 1],
                                      AL.mult, AL.add))
            b.wv(t_)
            for co in range(8):
                pb = co % 2
                if ps_free[pb] is not None:
                    sem, cnt = ps_free[pb]
                    t_.wait_ge(sem.h, cnt)
                last = None
                for cc in range(8):
                    last = t_.matmul(ps_p[pb][:, :],
                                     wp[:, cc * 1024 + co * 128: cc * 1024 + (co + 1) * 128],
                                     zc[:, cc * 512:(cc + 1) * 512],
                                     start=(cc == 0), stop=(cc == 7))
                b.tm(last)
                b.wt(s_)
                b.wv(s_)
                stp.sems[pb].wait(s_)
                b.sm(s_.activation(pf[pb][:, :], ps_p[pb][:, :], AF.Identity,
                                   bias=pbv[:, co:co + 1]))
                ps_free[pb] = (b.cs, b.cs.n)
                b.ws(v_)
                b.wv(v_)
                b.vm(v_.reduce_max(pmax[:, 16:17], pf[pb][:, :], axis=AX))
                b.wv(v_)
                b.vm(v_.tensor_max(pmax[:, co:co + 1], pmax[:, co:co + 1],
                                   pmax[:, 16:17]))
                stp.gate(j * 8 + co)
                b.ws(y_)
                stp.slot(j * 8 + co).inc(y_.dma_start(
                    H["d_p"][co * 128:(co + 1) * 128, j * 512:(j + 1) * 512],
                    pf[pb][:, :]), 16)
            last = None
            for cc in range(8):
                last = t_.matmul(ps_c[:, :], wclk[:, cc:cc + 1],
                                 zc[:, cc * 512:(cc + 1) * 512],
                                 start=(cc == 0), stop=(cc == 7))
            b.tm(last)
            b.wt(s_)
            b.ws(s_)
            b.wv(s_)
            b.sm(s_.activation(cwrk[0:1, 0:512], ps_c[:, :], AF.Identity,
                               bias=cbv[0:1, 0:1]))
            b.ws(t_)
            b.ws(v_)
            softplus_emit(b, clkrow[0:1, j * 512:(j + 1) * 512],
                          cwrk[0:1, 0:512], cwrk[0:1, 512:1024],
                          cwrk[0:1, 1024:1536])
            ldz.done(j, b.ct)
        b.wv(v_)
        b.vm(v_.tensor_scalar_mul(pmax[:, 8:16], pmax[:, 0:8], -1.0))
        b.wv(s_)
        ld2 = DBuf(b, y_, 1, "p5ld2")
        st2 = DBuf(b, y_, 1, "p5st2")
        stp.wait_all(y_)
        for co in range(8):
            ld2.gate(co)
            ld2.slot(co).inc(y_.dma_start(pld[:, :],
                                          H["d_p"][co * 128:(co + 1) * 128, :]), 16)
            ld2.wait(co, s_)
            st2.sems[0].wait(s_)
            b.sm(s_.activation(pex[:, :], pld[:, :], AF.Exp,
                               bias=pmax[:, 8 + co:9 + co]))
            st2.gate(co)
            b.ws(y_)
            st2.slot(co).inc(y_.dma_start(H["d_pexp0"][co * 128:(co + 1) * 128, :],
                                          pex[:, :]), 16)
            ld2.done(co, b.cs)
        H["p5_st2"] = st2


def cumsum_tile(b, C, ps_pair, ps_col, carry_sb, src_f32, out_f32, n_half=2,
                post=None):
    t_, v_, s_ = b.t, b.v, b.s
    for hh in range(n_half):
        sl = hh % 2
        b.wv(t_)
        b.tm(t_.matmul(ps_pair[sl][:, :], C["tril"][:, :],
                       src_f32[:, hh * 512:(hh + 1) * 512], start=True, stop=False))
        b.tm(t_.matmul(ps_pair[sl][:, :], C["onesrow"][:, :],
                       carry_sb[0:1, hh * 512:(hh + 1) * 512], start=False, stop=True))
        b.tm(t_.matmul(ps_col[:, :], C["onescol"][:, :],
                       src_f32[:, hh * 512:(hh + 1) * 512], start=True, stop=True))
        b.wt(v_)
        if post == "min60":
            b.vm(v_.tensor_scalar(out_f32[:, hh * 512:(hh + 1) * 512],
                                  ps_pair[sl][:, :], 60.0, None, AL.min))
        elif post == "eps":
            b.vm(v_.tensor_scalar(out_f32[:, hh * 512:(hh + 1) * 512],
                                  ps_pair[sl][:, :], 1e-8, None, AL.add))
        else:
            b.vm(v_.tensor_copy(out_f32[:, hh * 512:(hh + 1) * 512],
                                ps_pair[sl][:, :]))
        b.vm(v_.tensor_add(carry_sb[0:1, hh * 512:(hh + 1) * 512],
                           carry_sb[0:1, hh * 512:(hh + 1) * 512], ps_col[0:1, :]))
        b.wv(t_)


def p6_gating(b, ST, H, C):
    nc, t_, v_, s_, g_, y_ = b.nc, b.t, b.v, b.s, b.g, b.y
    T, nT = H["T"], H["nT"]
    l1w = H["l1w"]
    with ExitStack() as S:
        sb = lambda n, s, d: S.enter_context(nc.sbuf_tensor(n, s, d))
        wqW = sb("p6wq", [128, 8 * 1024], BF16)
        wkW = sb("p6wk", [128, 8 * 1024], BF16)
        wvW = sb("p6wv", [128, 8 * 1024], BF16)
        wgW = sb("p6wg", [128, 8 * 1024], BF16)
        gbb = sb("p6gbb", [128, 1024], F32)
        gbrow = sb("p6gbr", [1, 1024], F32)
        zin = [sb("p6z%d" % i, [128, 8 * 128], BF16) for i in range(2)]
        rc = [sb("p6rc%d" % i, [128, 512], F32) for i in range(2)]
        rs = [sb("p6rs%d" % i, [128, 512], F32) for i in range(2)]
        px = [sb("p6px%d" % i, [128, 1024], BF16) for i in range(2)]
        clkt = [sb("p6clk%d" % i, [128, 1], F32) for i in range(2)]
        qraw = [sb("p6q%d" % i, [128, 1024], F32) for i in range(2)]
        kraw = [sb("p6k%d" % i, [128, 1024], F32) for i in range(2)]
        gjr = [sb("p6gj%d" % i, [128, 1024], F32) for i in range(2)]
        vbf = [sb("p6v%d" % i, [128, 1024], BF16) for i in range(2)]
        spb = sb("p6sp", [128, 1024], F32)
        csb = sb("p6cs", [128, 1024], F32)
        gjcp = sb("p6gcp", [128, 1024], F32)
        pexf = sb("p6pe", [128, 1024], F32)
        pcum = sb("p6pc", [128, 1024], F32)
        rpc = sb("p6rpc", [128, 1024], F32)
        rop = sb("p6rop", [128, 1024], F32)
        dmp = sb("p6dmp", [128, 1024], F32)
        outq = sb("p6oq", [128, 1024], F32)
        outqb = sb("p6oqb", [128, 1024], BF16)
        cgj = sb("p6cgj", [1, 1024], F32)
        cpx = sb("p6cpx", [1, 1024], F32)
        nrm = sb("p6nrm", [128, 6], F32)
        krs = S.enter_context(nc.sbuf_tensor("p6krs", [128, 1], mybir.dt.int32))
        ps_pr = [S.enter_context(nc.psum_tensor("p6p%d" % i, [128, 512], F32)) for i in range(4)]
        ps_cs = [S.enter_context(nc.psum_tensor("p6c%d" % i, [128, 512], F32)) for i in range(2)]
        ps_col = S.enter_context(nc.psum_tensor("p6col", [1, 512], F32))
        ps_clk = S.enter_context(nc.psum_tensor("p6ck", [128, 1], F32))
        ldw = b.sem("p6ldw")
        for wsb_, nm in ((wqW, "Wq"), (wkW, "Wk"), (wvW, "Wv"), (wgW, "gate_W")):
            for cc in range(8):
                ldw.inc(g_.dma_start(wsb_[:, cc * 1024:(cc + 1) * 1024],
                                     l1w[nm][cc * 128:(cc + 1) * 128, :]), 16)
        b.vm(v_.memset(krs[:, :], 0x5F375A86))
        b.wv(v_)
        H["krs6"] = krs
        H["outqb6"] = outqb
        ldv = b.sem("p6ldv")
        ldv.inc(y_.dma_start(gbrow[0:1, :], l1w["gate_b"].ap()[None, :]), 16)
        ldv.wait(t_)
        for hh in range(2):
            b.tm(t_.matmul(ps_cs[hh][:, :], C["onesrow"][:, :],
                           gbrow[0:1, hh * 512:(hh + 1) * 512], start=True, stop=True))
        b.wt(v_)
        for hh in range(2):
            b.vm(v_.tensor_copy(gbb[:, hh * 512:(hh + 1) * 512], ps_cs[hh][:, :]))
        b.vm(v_.memset(cgj[:, :], 0.0))
        b.vm(v_.memset(cpx[:, :], 0.0))
        b.wv(t_)
        ldz = DBuf(b, y_, 2, "p6ldz")
        ldrop = DBuf(b, y_, 2, "p6lrp")
        ldpx = DBuf(b, y_, 2, "p6lpx")
        stqk = DBuf(b, y_, 2, "p6stq")
        ldw.wait(t_)
        cin = H["cin"]
        relV = [None, None]
        relS = [None, None]
        evc = [None, None]
        lnc = [None, None]
        prfree = {0: None, 1: None}
        clkc = [None, None]
        for i in range(-1, nT):
            ip = i + 1
            if ip < nT:
                jp = ip % 2
                ldz.gate(ip)
                for cc in range(8):
                    ldz.slot(ip).inc(y_.dma_start(
                        zin[jp][:, cc * 128:(cc + 1) * 128],
                        H["d_zin0"][ip * 128:(ip + 1) * 128, cc * 128:(cc + 1) * 128],
                        transpose=True), 16)
                ldrop.gate(ip)
                ldrop.slot(ip).inc(y_.dma_start(rc[jp][:, :],
                                                cin["c_rope_cos"][ip * 128:(ip + 1) * 128, :]), 16)
                ldrop.slot(ip).inc(y_.dma_start(rs[jp][:, :],
                                                cin["c_rope_sin"][ip * 128:(ip + 1) * 128, :]), 16)
                ldpx.gate(ip)
                for cc in range(8):
                    ldpx.slot(ip).inc(y_.dma_start(
                        px[jp][:, cc * 128:(cc + 1) * 128],
                        H["d_pexp0"][cc * 128:(cc + 1) * 128, ip * 128:(ip + 1) * 128],
                        transpose=True), 16)
                ldz.wait(ip, g_)
                if relV[jp] is not None:
                    g_.wait_ge(b.ct.h, evc[jp][1])
                for cc in range(8):
                    b.gm(g_.tensor_scalar(zin[jp][:, cc * 128:(cc + 1) * 128],
                                          zin[jp][:, cc * 128:(cc + 1) * 128],
                                          H["ln1g"][:, cc:cc + 1],
                                          H["ln1b"][:, cc:cc + 1],
                                          AL.mult, AL.add))
                lnc[jp] = b.cg.n
            if i >= 0:
                jc = i % 2
                b.sm(s_.activation(dmp[:, :], qraw[jc][:, :], AF.Square,
                                   accum_out=nrm[:, 0:1]))
                b.sm(s_.activation(rop[:, :], kraw[jc][:, :], AF.Square,
                                   accum_out=nrm[:, 3:4]))
                ldpx.wait(i, s_)
                b.ws(s_)
                b.sm(s_.activation(pexf[:, :], px[jc][:, :], AF.Identity,
                                   scale=clkt[jc][:, 0:1]))
                sq_c = b.cs.n
                v_.wait_ge(b.cs.h, evc[jc][0])
                b.vm(v_.tensor_add(gjr[jc][:, :], gjr[jc][:, :], gbb[:, :]))
                v_.wait_ge(b.cs.h, sq_c)
                rsqrt_emit(b, nrm, 0, H["krs6"])
                rsqrt_emit(b, nrm, 3, H["krs6"])
                b.wv(s_)
                softplus_emit(b, spb[:, :], gjr[jc][:, :], dmp[:, :], rop[:, :],
                              ones=C["onescol"][:, 0:1])
                b.ws(t_)
                b.ws(v_)
                b.wv(t_)
                b.wv(v_)
                cumsum_tile(b, C, ps_cs, ps_col, cgj, spb, csb, post="min60")
                b.wv(s_)
                b.sm(s_.activation(gjcp[:, :], csb[:, :], AF.Exp, scale=-1.0))
                b.ws(t_)
                b.ws(v_)
                b.wv(t_)
                cumsum_tile(b, C, ps_cs, ps_col, cpx, pexf, pcum, post="eps")
                b.wv(v_)
                b.vm(v_.reciprocal(rpc[:, :], pcum[:, :]))
                ldrop.wait(i, v_)
                rope_gate(b, C, H, qraw[jc], rc[jc], rs[jc], rop, dmp, outq, nrm, 0,
                          gjcp, rpc, stqk, 3 * i + 1,
                          H["d_q1"], i)
                b.wv(v_)
                b.vm(v_.tensor_scalar(dmp[:, :], gjcp[:, :], 1e-8, None, AL.add))
                b.wv(v_)
                b.vm(v_.reciprocal(rpc[:, :], dmp[:, :]))
                rope_gate(b, C, H, kraw[jc], rc[jc], rs[jc], rop, dmp, outq, nrm, 3,
                          pexf, rpc, stqk, 3 * i + 2,
                          H["d_k1"], i)
                ldrop.done(i, b.cv)
                ldpx.done(i, b.cs)
                relV[jc] = b.cv.n
                relS[jc] = b.cs.n
            if ip < nT:
                jp = ip % 2
                zc = zin[jp]
                if clkc[1 - jp] is not None:
                    t_.wait_ge(b.cs.h, clkc[1 - jp])
                b.tm(t_.matmul(ps_clk[:, :], H["clkrow"][0:1, ip * 128:(ip + 1) * 128],
                               C["onesrow"][0:1, 0:1], start=True, stop=True))
                b.wt(s_)
                b.sm(s_.copy(clkt[jp][:, :], ps_clk[:, :]))
                clkc[jp] = b.cs.n
                t_.wait_ge(b.cg.h, lnc[jp])
                if relV[jp] is not None:
                    t_.wait_ge(b.cv.h, relV[jp])
                    t_.wait_ge(b.cs.h, relS[jp])
                for si, (wsb_, dstb, isbf) in enumerate(
                        ((wqW, qraw[jp], False), (wkW, kraw[jp], False),
                         (wgW, gjr[jp], False), (wvW, vbf[jp], True))):
                    pr = (0, 1) if si % 2 == 0 else (2, 3)
                    if prfree[si % 2] is not None:
                        t_.wait_ge(b.cs.h, prfree[si % 2])
                    for hh in range(2):
                        last = None
                        for cc in range(8):
                            last = t_.matmul(
                                ps_pr[pr[hh]][:, :],
                                zc[:, cc * 128:(cc + 1) * 128],
                                wsb_[:, cc * 1024 + hh * 512: cc * 1024 + (hh + 1) * 512],
                                start=(cc == 0), stop=(cc == 7))
                        b.tm(last)
                    b.wt(s_)
                    if isbf:
                        for ssem in stqk.sems:
                            ssem.wait(s_)
                    for hh in range(2):
                        b.sm(s_.copy(dstb[:, hh * 512:(hh + 1) * 512],
                                     ps_pr[pr[hh]][:, :]))
                    prfree[si % 2] = b.cs.n
                evc[jp] = (b.cs.n, b.ct.n)
                stqk.gate(3 * ip)
                b.ws(y_)
                stqk.slot(3 * ip).inc(y_.dma_start(H["d_v1"][ip * 128:(ip + 1) * 128, :],
                                                   vbf[jp][:, :]), 16)
                ldz.done(ip, b.ct)
        H["p6_stqk"] = stqk


def rsqrt_emit(b, nrm, nb, krs):
    v_ = b.v
    ni = nrm.bitcast(mybir.dt.int32)
    b.vm(v_.tensor_scalar(ni[:, nb + 1:nb + 2], ni[:, nb:nb + 1], 1, None,
                          AL.logical_shift_right))
    b.wv(v_)
    b.vm(v_.tensor_tensor(ni[:, nb + 1:nb + 2], krs[:, :], ni[:, nb + 1:nb + 2],
                          AL.subtract))
    b.wv(v_)
    for _ in range(2):
        b.vm(v_.tensor_mul(nrm[:, nb + 2:nb + 3], nrm[:, nb + 1:nb + 2],
                           nrm[:, nb + 1:nb + 2]))
        b.wv(v_)
        b.vm(v_.tensor_mul(nrm[:, nb + 2:nb + 3], nrm[:, nb + 2:nb + 3],
                           nrm[:, nb:nb + 1]))
        b.wv(v_)
        b.vm(v_.tensor_scalar(nrm[:, nb + 2:nb + 3], nrm[:, nb + 2:nb + 3],
                              -0.5, 1.5, AL.mult, AL.add))
        b.wv(v_)
        b.vm(v_.tensor_mul(nrm[:, nb + 1:nb + 2], nrm[:, nb + 1:nb + 2],
                           nrm[:, nb + 2:nb + 3]))
        b.wv(v_)


def rope_gate(b, C, H, xraw, rc, rs, rop, dmp, outq, nrm, nb, mulA, mulB,
              stqk, stidx, dst_dram, j):
    v_, s_, g_, y_ = b.v, b.s, b.g, b.y
    b.wv(v_)
    b.vm(v_.tensor_scalar(xraw[:, :], xraw[:, :], nrm[:, nb + 1:nb + 2], None,
                          AL.mult))
    b.wv(v_)
    xe = bass.AP(xraw, 0, [[1024, 128], [2, 512]])
    xo = bass.AP(xraw, 1, [[1024, 128], [2, 512]])
    re_ = bass.AP(rop, 0, [[1024, 128], [2, 512]])
    ro_ = bass.AP(rop, 1, [[1024, 128], [2, 512]])
    de_ = bass.AP(dmp, 0, [[1024, 128], [2, 512]])
    b.vm(v_.tensor_mul(re_, xe, rc[:, :]))
    b.vm(v_.tensor_mul(de_, xo, rs[:, :]))
    b.wv(v_)
    b.vm(v_.tensor_sub(re_, re_, de_))
    b.vm(v_.tensor_mul(ro_, xe, rs[:, :]))
    b.wv(v_)
    b.vm(v_.tensor_mul(de_, xo, rc[:, :]))
    b.wv(v_)
    b.vm(v_.tensor_add(ro_, ro_, de_))
    b.wv(v_)
    outqb = H["outqb6"]
    for ssem in stqk.sems:
        ssem.wait(v_)
    b.vm(v_.tensor_mul(outq[:, :], rop[:, :], mulA[:, :]))
    b.wv(v_)
    b.vm(v_.tensor_mul(outqb[:, :], outq[:, :], mulB[:, :]))
    stqk.gate(stidx)
    b.wv(y_)
    stqk.slot(stidx).inc(y_.dma_start(dst_dram[j * 128:(j + 1) * 128, :],
                                      outqb[:, :]), 16)


def p7_attn1(b, ST, H, C):
    nc, t_, v_, s_, g_, y_ = b.nc, b.t, b.v, b.s, b.g, b.y
    T, nT = H["T"], H["nT"]
    with ExitStack() as S:
        sb = lambda n, s, d: S.enter_context(nc.sbuf_tensor(n, s, d))
        k1f = sb("p7k", [128, 8 * T], BF16)
        v1sb = sb("p7v", [128, nT * 1024], BF16)
        q1f = [sb("p7q%d" % i, [128, 8 * 128], BF16) for i in range(2)]
        scrow = [sb("p7sc%d" % i, [128, T], F32) for i in range(2)]
        pbq = sb("p7pb", [128, T], BF16)
        pT2 = sb("p7pT", [128, 4 * 128], BF16)
        avtok = [sb("p7av%d" % i, [128, 1024], BF16) for i in range(2)]
        mrow = sb("p7m", [128, 4], F32)
        ps_sc = [S.enter_context(nc.psum_tensor("p7s%d" % i, [128, 512], F32)) for i in range(2)]
        ps_tb = [S.enter_context(nc.psum_tensor("p7t%d" % i, [128, 128], BF16)) for i in range(2)]
        ps_av = [S.enter_context(nc.psum_tensor("p7a%d" % i, [128, 512], F32)) for i in range(2)]
        ldr = b.sem("p7res")
        for cc in range(8):
            ldr.inc(y_.dma_start(k1f[:, cc * T:(cc + 1) * T],
                                 H["d_k1"][:, cc * 128:(cc + 1) * 128],
                                 transpose=True), 16)
        ldr.inc(y_.dma_start(
            bass.AP(v1sb, 0, [[nT * 1024, 128], [1024, nT], [1, 1024]]),
            bass.AP(H["d_v1"], 0, [[1024, 128], [128 * 1024, nT], [1, 1024]])), 16)
        ldq = DBuf(b, y_, 2, "p7ldq")
        stv = DBuf(b, g_, 2, "p7stv")
        ldr.wait(t_)
        sc_free = [None, None]
        scd = [None, None]
        tbfree = [None, None]
        pt4free = [None] * 4
        avfree = None
        for i in range(nT + 2):
            if i < nT:
                ldq.gate(i)
                for cc in range(8):
                    ldq.slot(i).inc(y_.dma_start(
                        q1f[i % 2][:, cc * 128:(cc + 1) * 128],
                        H["d_q1"][i * 128:(i + 1) * 128, cc * 128:(cc + 1) * 128],
                        transpose=True), 16)
            if 1 <= i <= nT:
                qt = i - 1
                qf = q1f[qt % 2]
                nk = qt + 1
                nch = (nk + 3) // 4
                ldq.wait(qt, t_)
                for kc in range(nch):
                    pb = kc % 2
                    if sc_free[pb] is not None:
                        t_.wait_ge(b.cs.h, sc_free[pb])
                    kw = min(512, nk * 128 - kc * 512)
                    last = None
                    for cc in range(8):
                        last = t_.matmul(
                            ps_sc[pb][:, 0:kw],
                            qf[:, cc * 128:(cc + 1) * 128],
                            k1f[:, cc * T + kc * 512: cc * T + kc * 512 + kw],
                            start=(cc == 0), stop=(cc == 7))
                    b.tm(last)
                    b.wt(s_)
                    b.sm(s_.activation(scrow[qt % 2][:, kc * 512:kc * 512 + kw],
                                       ps_sc[pb][:, 0:kw], AF.Identity,
                                       scale=1.0 / 32.0))
                    sc_free[pb] = b.cs.n
                scd[qt % 2] = b.cs.n
                ldq.done(qt, b.ct)
            if i >= 2:
                qt = i - 2
                scq = scrow[qt % 2]
                nk = qt + 1
                jdiag = qt % 4
                kcd = qt // 4
                kw = min(512, nk * 128 - kcd * 512)
                v_.wait_ge(b.cs.h, scd[qt % 2])
                b.vm(v_.tensor_add(scq[:, kcd * 512:kcd * 512 + kw],
                                   scq[:, kcd * 512:kcd * 512 + kw],
                                   C["maskadd_l1"][:, jdiag * 512:jdiag * 512 + kw]))
                b.wv(v_)
                b.vm(v_.reduce_max(mrow[:, 0:1], scq[:, 0:nk * 128], axis=AX))
                b.wv(v_)
                b.vm(v_.tensor_scalar_mul(mrow[:, 1:2], mrow[:, 0:1], -1.0))
                b.wv(s_)
                b.ws(s_)
                b.sm(s_.activation(pbq[:, 0:nk * 128], scq[:, 0:nk * 128], AF.Exp,
                                   bias=mrow[:, 1:2], accum_out=mrow[:, 2:3]))
                expc = b.cs.n
                b.ws(v_)
                b.vm(v_.reciprocal(C["lrec_all"][:, qt:qt + 1], mrow[:, 2:3]))
                copyc = {}
                pend = []

                def emit_av(kp):
                    t_.wait_ge(b.cs.h, copyc[kp])
                    if kp == 0 and avfree is not None:
                        t_.wait_ge(b.cs.h, avfree)
                    for hh in range(2):
                        b.tm(t_.matmul(
                            ps_av[hh][:, :],
                            pT2[:, (kp % 4) * 128:(kp % 4 + 1) * 128],
                            v1sb[:, kp * 1024 + hh * 512: kp * 1024 + (hh + 1) * 512],
                            start=(kp == 0), stop=(kp == nk - 1)))
                    pt4free[kp % 4] = b.ct.n

                for kt in range(nk):
                    sl2 = kt % 2
                    sl4 = kt % 4
                    if tbfree[sl2] is not None:
                        t_.wait_ge(b.cs.h, tbfree[sl2])
                    if kt == 0:
                        t_.wait_ge(b.cs.h, expc)
                    b.tm(t_.transpose(ps_tb[sl2][:, :], pbq[:, kt * 128:(kt + 1) * 128],
                                      C["ident_bf"][:, :]))
                    b.wt(s_)
                    if pt4free[sl4] is not None:
                        s_.wait_ge(b.ct.h, pt4free[sl4])
                    b.sm(s_.copy(pT2[:, sl4 * 128:(sl4 + 1) * 128], ps_tb[sl2][:, :]))
                    copyc[kt] = b.cs.n
                    tbfree[sl2] = b.cs.n
                    if len(pend) >= 2:
                        emit_av(pend.pop(0))
                    pend.append(kt)
                while pend:
                    emit_av(pend.pop(0))
                b.wt(s_)
                stv.sems[qt % 2].wait(s_)
                for hh in range(2):
                    b.sm(s_.copy(avtok[qt % 2][:, hh * 512:(hh + 1) * 512], ps_av[hh][:, :]))
                avfree = b.cs.n
                stv.gate(qt)
                b.ws(g_)
                stv.slot(qt).inc(g_.dma_start(H["d_av"][qt * 128:(qt + 1) * 128, :],
                                              avtok[qt % 2][:, :]), 16)
        H["p7_stv"] = stv


def p7b_outproj(b, ST, H, C):
    nc, t_, v_, s_, g_, y_ = b.nc, b.t, b.v, b.s, b.g, b.y
    T, nT = H["T"], H["nT"]
    l1w = H["l1w"]
    with ExitStack() as S:
        sb = lambda n, s, d: S.enter_context(nc.sbuf_tensor(n, s, d))
        w1o = sb("p7wo", [128, 8 * 1024], BF16)
        obb = sb("p7obb", [128, 1024], F32)
        obr = sb("p7obr", [1, 1024], F32)
        avf = [sb("p7af%d" % i, [128, 8 * 128], BF16) for i in range(2)]
        zld = [sb("p7z%d" % i, [128, 1024], F32) for i in range(2)]
        x2t = sb("p7x2", [128, 1024], F32)
        h2b = sb("p7h2", [128, 1024], BF16)
        sqd = sb("p7sq", [128, D], F32)
        st1 = sb("p7st", [128, 8], F32)
        ps_o = [S.enter_context(nc.psum_tensor("p7o%d" % i, [128, 512], F32)) for i in range(2)]
        ldw = b.sem("p7bw")
        for cc in range(8):
            ldw.inc(g_.dma_start(w1o[:, cc * 1024:(cc + 1) * 1024],
                                 l1w["out_W"][cc * 128:(cc + 1) * 128, :]), 16)
        ldv = b.sem("p7bv")
        ldv.inc(y_.dma_start(obr[0:1, :], l1w["out_b"].ap()[None, :]), 16)
        ldv.wait(t_)
        for hh in range(2):
            b.tm(t_.matmul(ps_o[hh][:, :], C["onesrow"][:, :],
                           obr[0:1, hh * 512:(hh + 1) * 512], start=True, stop=True))
        b.wt(v_)
        for hh in range(2):
            b.vm(v_.tensor_copy(obb[:, hh * 512:(hh + 1) * 512], ps_o[hh][:, :]))
        b.wv(t_)
        ldq = DBuf(b, y_, 2, "p7bq")
        ldz = DBuf(b, y_, 2, "p7bz")
        stx = DBuf(b, y_, 2, "p7bx")
        sth = DBuf(b, g_, 2, "p7bh")
        ldw.wait(t_)
        for i in range(nT + 1):
            if i < nT:
                ldq.gate(i)
                for cc in range(8):
                    ldq.slot(i).inc(y_.dma_start(
                        avf[i % 2][:, cc * 128:(cc + 1) * 128],
                        H["d_av"][i * 128:(i + 1) * 128, cc * 128:(cc + 1) * 128],
                        transpose=True), 16)
                ldz.gate(i)
                ldz.slot(i).inc(y_.dma_start(zld[i % 2][:, :],
                                             H["d_z"][i * 128:(i + 1) * 128, :]), 16)
            if i == 0:
                continue
            qt = i - 1
            ldq.wait(qt, t_)
            b.wv(t_)
            for cc in range(8):
                for hh in range(2):
                    b.tm(t_.matmul(ps_o[hh][:, :],
                                   avf[qt % 2][:, cc * 128:(cc + 1) * 128],
                                   w1o[:, cc * 1024 + hh * 512: cc * 1024 + (hh + 1) * 512],
                                   start=(cc == 0), stop=(cc == 7)))
            ldz.wait(qt, v_)
            b.wt(v_)
            stx.wait_all(v_)
            sth.wait_all(v_)
            b.wv(v_)
            for hh in range(2):
                b.vm(v_.tensor_scalar(x2t[:, hh * 512:(hh + 1) * 512],
                                      ps_o[hh][:, :], C["lrec_all"][:, qt:qt + 1],
                                      None, AL.mult))
            b.wv(v_)
            b.vm(v_.tensor_add(x2t[:, :], x2t[:, :], obb[:, :]))
            b.wv(v_)
            b.vm(v_.tensor_add(x2t[:, :], x2t[:, :], zld[qt % 2][:, :]))
            b.wv(v_)
            stx.gate(qt)
            b.wv(y_)
            stx.slot(qt).inc(y_.dma_start(H["d_x2"][qt * 128:(qt + 1) * 128, :],
                                          x2t[:, :]), 16)
            ldz.done(qt, b.cv)
            ln_rowstats(b, st1, x2t, 0, 1, 2, 3, 4, sqd)
            b.vm(v_.tensor_scalar(h2b[:, :], x2t[:, :], st1[:, 0:1], st1[:, 3:4],
                                  AL.subtract, AL.mult))
            sth.gate(qt)
            b.wv(g_)
            sth.slot(qt).inc(g_.dma_start(H["d_h2"][qt * 128:(qt + 1) * 128, :],
                                          h2b[:, :]), 16)
            ldq.done(qt, b.ct)
        H["p7_stx"] = stx
        H["p7_sth"] = sth


def p9_moe(b, ST, H, C):
    nc, t_, v_, s_, g_, y_ = b.nc, b.t, b.v, b.s, b.g, b.y
    T, nT, nC = H["T"], H["nT"], H["nC"]
    N_ALL, CAP = H["N_ALL"], H["CAP"]
    ncol = N_ALL // 128
    CH = min(1024, T)
    nCH = T // CH
    ntt = CH // 128
    with ExitStack() as S:
        sb = lambda n, s, d: S.enter_context(nc.sbuf_tensor(n, s, d))
        gwT = sb("p9gw", [128, 8 * 8], BF16)
        zf = [sb("p9zf%d" % i, [128, 8 * CH], BF16) for i in range(2)]
        probs = sb("p9pr", [128, nT * 8], F32)
        assign = sb("p9as", [128, nT * 8], F32)
        Rown = sb("p9ro", [128, nT * 8], F32)
        Pdup = sb("p9pd", [128, 4 * 16], F32)
        Scum = sb("p9scu", [128, 4 * 8], F32)
        rankc = sb("p9rk", [128, 4 * 8], F32)
        tmp8 = sb("p9t8", [128, 4 * 8], F32)
        rall = sb("p9ra", [128, ncol * 8], F32)
        cmpb = sb("p9cm", [128, ncol * 8], F32)
        red8 = sb("p9rd", [128, 8], F32)
        bis = sb("p9bi", [128, 6 * 8], F32)
        rmx = sb("p9rmx", [128, nT], F32)
        rsum = sb("p9rsu", [128, nT], F32)
        ssel = sb("p9ss", [128, nT * 8], F32)
        w1T = [sb("p9w1%d" % i, [128, 8 * 1024], BF16) for i in range(2)]
        w2T = [sb("p9w2%d" % i, [128, 8 * 1024], BF16) for i in range(2)]
        ge = sb("p9ge", [128, 8 * CH], BF16)
        yac = sb("p9ya", [128, ntt * 1024], F32)
        st1 = sb("p9st1", [128, 4], F32)
        ps_R = S.enter_context(nc.psum_tensor("p9pR_", [128, nT * 8], F32))
        ps_hy = [S.enter_context(nc.psum_tensor("p9hy%d" % i, [128, 512], F32)) for i in range(4)]
        ps_c = S.enter_context(nc.psum_tensor("p9c", [1, 8], F32))
        ps_b = S.enter_context(nc.psum_tensor("p9b", [128, 8], F32))
        ldg = b.sem("p9ldg")
        for cc in range(8):
            ldg.inc(g_.dma_start(gwT[:, cc * 8:(cc + 1) * 8],
                                 bass.AP(H["gw_moe"], cc * 128, [[1, 128], [1024, 8], [1, 1]]),
                                 allow_slow_non_contiguous=True), 16)
        ldg.wait(t_)
        stR = DBuf(b, y_, 2, "p9stR")
        ldzf = DBuf(b, y_, 2, "p9ldz")
        ldw = DBuf(b, s_, 2, "p9ldw")
        ldw2 = DBuf(b, y_, 2, "p9lw2")

        def ldw_issue(idx, e_src):
            sl = idx % 2
            ldw.gate(idx)
            ldw.slot(idx).inc(s_.dma_start(
                bass.AP(w1T[sl], 0, [[8 * 1024, 128], [1024, 4], [1, 1024]]),
                bass.AP(H["d_w1T"], e_src * D * F_E,
                        [[F_E, 128], [128 * F_E, 4], [1, F_E]])), 16)
            ldw.slot(idx).inc(s_.dma_start(
                bass.AP(w1T[sl], 4 * 1024, [[8 * 1024, 128], [1024, 4], [1, 1024]]),
                bass.AP(H["d_w1T"], e_src * D * F_E + 4 * 128 * F_E,
                        [[F_E, 128], [128 * F_E, 4], [1, F_E]])), 16)
            ldw2.gate(idx)
            ldw2.slot(idx).inc(y_.dma_start(
                bass.AP(w2T[sl], 0, [[8 * 1024, 128], [1024, 8], [1, 1024]]),
                bass.AP(H["d_w2T"], e_src * F_E * D,
                        [[D, 128], [128 * D, 8], [1, D]])), 16)

        for pe_ in range(2):
            ldw_issue(pe_, pe_)
        for ch in range(nCH):
            if ch == 0:
                ldzf.gate(0)
                for cc in range(8):
                    ldzf.slot(0).inc(y_.dma_start(
                        zf[0][:, cc * CH:(cc + 1) * CH],
                        H["d_zT"][cc * 128:(cc + 1) * 128, 0:CH]), 16)
            if ch + 1 < nCH:
                ldzf.gate(ch + 1)
                for cc in range(8):
                    ldzf.slot(ch + 1).inc(y_.dma_start(
                        zf[(ch + 1) % 2][:, cc * CH:(cc + 1) * CH],
                        H["d_zT"][cc * 128:(cc + 1) * 128, (ch + 1) * CH:(ch + 2) * CH]), 16)
            ldzf.wait(ch, t_)
            zfc = zf[ch % 2]
            for tt in range(CH // 128):
                i = ch * (CH // 128) + tt
                last = None
                for cc in range(8):
                    last = t_.matmul(ps_R[:, i * 8:(i + 1) * 8],
                                     zfc[:, cc * CH + tt * 128: cc * CH + tt * 128 + 128],
                                     gwT[:, cc * 8:(cc + 1) * 8],
                                     start=(cc == 0), stop=(cc == 7))
                b.tm(last)
            ldzf.done(ch, b.ct)
        b.wt(v_)
        b.vm(v_.tensor_reduce(rmx[:, :],
                              bass.AP(ps_R, 0, [[nT * 8, 128], [8, nT], [1, 8]]),
                              axis=AX, op=AL.max))
        b.wv(v_)
        b.vm(v_.tensor_tensor(probs[:, :], ps_R[:, :],
                              bass.AP(rmx, 0, [[nT, 128], [1, nT], [0, 8]]),
                              AL.subtract))
        b.wv(s_)
        b.sm(s_.activation(probs[:, :], probs[:, :], AF.Exp))
        b.ws(v_)
        b.vm(v_.tensor_reduce(rsum[:, :],
                              bass.AP(probs, 0, [[nT * 8, 128], [8, nT], [1, 8]]),
                              axis=AX, op=AL.add))
        b.wv(v_)
        b.vm(v_.reciprocal(rsum[:, :], rsum[:, :]))
        b.wv(v_)
        b.vm(v_.tensor_tensor(probs[:, :], probs[:, :],
                              bass.AP(rsum, 0, [[nT, 128], [1, nT], [0, 8]]),
                              AL.mult))
        b.wv(v_)
        b.vm(v_.memset(yac[:, :], 0.0))
        b.wv(v_)
        for gq in range(nT // 4):
            pr4 = probs[:, gq * 32:(gq + 1) * 32]
            d0 = bass.AP(Pdup, 0, [[64, 128], [16, 4], [1, 8]])
            d1 = bass.AP(Pdup, 8, [[64, 128], [16, 4], [1, 8]])
            b.vm(v_.tensor_copy(d0, pr4))
            b.vm(v_.tensor_copy(d1, pr4))
            b.vm(v_.memset(Scum[:, :], 0.0))
            b.vm(v_.memset(rankc[:, :], 0.0))
            b.wv(v_)
            for dd in range(1, 8):
                shifted = bass.AP(Pdup, dd, [[64, 128], [16, 4], [1, 8]])
                b.vm(v_.tensor_tensor(tmp8[:, :], shifted, pr4, AL.is_gt))
                b.wv(v_)
                b.vm(v_.tensor_add(rankc[:, :], rankc[:, :], tmp8[:, :]))
                b.vm(v_.tensor_tensor(tmp8[:, :], tmp8[:, :], shifted, AL.mult))
                b.wv(v_)
                b.vm(v_.tensor_add(Scum[:, :], Scum[:, :], tmp8[:, :]))
                b.wv(v_)
            b.vm(v_.tensor_scalar(assign[:, gq * 32:(gq + 1) * 32], Scum[:, :],
                                  0.9, None, AL.is_lt))
            b.vm(v_.tensor_sub(tmp8[:, :], pr4, rankc[:, :]))
            b.wv(v_)
            b.vm(v_.tensor_scalar_add(tmp8[:, :], tmp8[:, :], 99.0))
            b.wv(v_)
            b.vm(v_.tensor_mul(tmp8[:, :], tmp8[:, :],
                               assign[:, gq * 32:(gq + 1) * 32]))
            b.wv(v_)
            b.vm(v_.tensor_scalar_add(Rown[:, gq * 32:(gq + 1) * 32], tmp8[:, :],
                                      -100.0))
            b.wv(v_)
            stR.gate(gq)
            b.wv(y_)
            stR.slot(gq).inc(y_.dma_start(
                bass.AP(H["d_R"], gq * 512 * 8, [[8, 128], [128 * 8, 4], [1, 8]]),
                bass.AP(Rown, gq * 32, [[nT * 8, 128], [8, 4], [1, 8]])), 16)
        stR.wait_all(g_)
        ccg = H["ccag"]
        ccg.inc(g_.collective_compute("AllGather", AL.bypass, replica_groups=H["rg"],
                                      ins=[H["d_R"].ap()], outs=[H["d_Rall"].ap()]))
        ccg.wait(y_)
        ldra = b.sem("p9lra")
        ldra.inc(y_.dma_start(rall[:, :], H["d_Rall"].ap().rearrange(
            "(p n) e -> p (n e)", p=128)), 16)
        ldra.wait(v_)
        b.vm(v_.memset(bis[:, 0:8], -101.0))
        b.vm(v_.memset(bis[:, 8:16], 1.0))
        b.wv(v_)
        for it in range(30):
            b.vm(v_.tensor_add(bis[:, 16:24], bis[:, 0:8], bis[:, 8:16]))
            b.wv(v_)
            b.vm(v_.tensor_scalar_mul(bis[:, 16:24], bis[:, 16:24], 0.5))
            b.wv(v_)
            b.vm(v_.tensor_tensor(cmpb[:, :], rall[:, :],
                                  bass.AP(bis, 16, [[48, 128], [0, ncol], [1, 8]]),
                                  AL.is_ge))
            b.wv(v_)
            b.vm(v_.tensor_reduce(red8[:, :],
                                  bass.AP(cmpb, 0, [[ncol * 8, 128], [1, 8], [8, ncol]]),
                                  axis=AX, op=AL.add))
            b.wv(t_)
            b.tm(t_.matmul(ps_b[:, :], C["ones128"][:, :], red8[:, :],
                           start=True, stop=True))
            b.wt(v_)
            b.vm(v_.tensor_scalar(bis[:, 24:32], ps_b[:, :], float(CAP), None,
                                  AL.is_ge))
            b.wv(v_)
            b.vm(v_.tensor_sub(bis[:, 32:40], bis[:, 16:24], bis[:, 0:8]))
            b.wv(v_)
            b.vm(v_.tensor_mul(bis[:, 32:40], bis[:, 32:40], bis[:, 24:32]))
            b.wv(v_)
            b.vm(v_.tensor_add(bis[:, 0:8], bis[:, 0:8], bis[:, 32:40]))
            b.vm(v_.tensor_scalar(bis[:, 40:48], bis[:, 24:32], -1.0, 1.0,
                                  AL.mult, AL.add))
            b.wv(v_)
            b.vm(v_.tensor_sub(bis[:, 32:40], bis[:, 8:16], bis[:, 16:24]))
            b.wv(v_)
            b.vm(v_.tensor_mul(bis[:, 32:40], bis[:, 32:40], bis[:, 40:48]))
            b.wv(v_)
            b.vm(v_.tensor_sub(bis[:, 8:16], bis[:, 8:16], bis[:, 32:40]))
            b.wv(v_)
        b.vm(v_.tensor_tensor(ssel[:, :], Rown[:, :],
                              bass.AP(bis, 0, [[48, 128], [0, nT], [1, 8]]),
                              AL.is_ge))
        b.wv(v_)
        b.vm(v_.tensor_mul(ssel[:, :], ssel[:, :], assign[:, :]))
        b.wv(v_)
        b.vm(v_.tensor_mul(ssel[:, :], ssel[:, :], probs[:, :]))
        b.wv(v_)
        b.wv(s_)
        ldz2 = DBuf(b, g_, 2, "p9ldz2")
        stym = DBuf(b, y_, 2, "p9sty")
        hyfree = [None] * 4

        def _wait_free(pb):
            if hyfree[pb] is not None:
                kind, cnt = hyfree[pb]
                t_.wait_ge(b.cs.h if kind == 's' else b.cv.h, cnt)
        ldz2.free[0] = (b.ct, b.ct.n)
        ldz2.free[1] = (b.ct, b.ct.n)
        for ch in range(nCH):
            if ch == 0:
                ldz2.gate(0)
                for cc in range(8):
                    ldz2.slot(0).inc(g_.dma_start(
                        zf[0][:, cc * CH:(cc + 1) * CH],
                        H["d_zT"][cc * 128:(cc + 1) * 128, 0:CH]), 16)
            if ch + 1 < nCH:
                ldz2.gate(ch + 1)
                for cc in range(8):
                    ldz2.slot(ch + 1).inc(g_.dma_start(
                        zf[(ch + 1) % 2][:, cc * CH:(cc + 1) * CH],
                        H["d_zT"][cc * 128:(cc + 1) * 128, (ch + 1) * CH:(ch + 2) * CH]), 16)
            if ch > 0:
                stym.wait_all(v_)
                b.wt(v_)
                b.vm(v_.memset(yac[:, :], 0.0))
                b.wv(t_)
            ldz2.wait(ch, t_)
            zfc = zf[ch % 2]
            for e in range(NE):
                ei = ch * NE + e
                ldw.wait(ei, t_)
                ldw2.wait(ei, t_)
                wa, wb = w1T[ei % 2], w2T[ei % 2]
                nth = CH // 512
                for fc in range(8):
                    pr = (0, 1) if fc % 2 == 0 else (2, 3)
                    _wait_free(pr[0])
                    _wait_free(pr[1])
                    lasts = [None] * nth
                    for cc in range(8):
                        for th in range(nth):
                            lasts[th] = t_.matmul(
                                ps_hy[pr[th % 2]][:, :],
                                wa[:, cc * 1024 + fc * 128: cc * 1024 + (fc + 1) * 128],
                                zfc[:, cc * CH + th * 512: cc * CH + (th + 1) * 512],
                                start=(cc == 0), stop=(cc == 7))
                    for th in range(nth):
                        b.tm(lasts[th])
                    b.wt(s_)
                    for th in range(nth):
                        b.sm(s_.activation(ge[:, fc * CH + th * 512: fc * CH + (th + 1) * 512],
                                           ps_hy[pr[th % 2]][:, :], AF.Gelu))
                    hyfree[pr[0]] = ('s', b.cs.n)
                    hyfree[pr[1]] = ('s', b.cs.n)
                b.ws(t_)
                for tt in range(ntt):
                    pr = (0, 1) if tt % 2 == 0 else (2, 3)
                    _wait_free(pr[0])
                    _wait_free(pr[1])
                    lasts = [None, None]
                    for fc in range(8):
                        for hh in range(2):
                            lasts[hh] = t_.matmul(
                                ps_hy[pr[hh]][:, :],
                                ge[:, fc * CH + tt * 128: fc * CH + (tt + 1) * 128],
                                wb[:, fc * 1024 + hh * 512: fc * 1024 + (hh + 1) * 512],
                                start=(fc == 0), stop=(fc == 7))
                    for hh in range(2):
                        b.tm(lasts[hh])
                    b.wt(v_)
                    gt = ch * ntt + tt
                    for hh in range(2):
                        b.vm(v_.scalar_tensor_tensor(
                            yac[:, tt * 1024 + hh * 512: tt * 1024 + (hh + 1) * 512],
                            ps_hy[pr[hh]][:, :], ssel[:, gt * 8 + e: gt * 8 + e + 1],
                            yac[:, tt * 1024 + hh * 512: tt * 1024 + (hh + 1) * 512],
                            AL.mult, AL.add))
                    hyfree[pr[0]] = ('v', b.cv.n)
                    hyfree[pr[1]] = ('v', b.cv.n)
                ldw.done(ei, b.ct)
                ldw2.done(ei, b.ct)
                en = (e + 2) % NE
                if ei + 2 < nCH * NE:
                    ldw_issue(ei + 2, en)
            b.wv(y_)
            stym.gate(ch)
            stym.slot(ch).inc(y_.dma_start(
                bass.AP(H["d_moe"], ch * CH * 1024,
                        [[1024, 128], [128 * 1024, ntt], [1, 1024]]),
                bass.AP(yac, 0, [[ntt * 1024, 128], [1024, ntt], [1, 1024]])), 16)
            ldz2.done(ch, b.ct)
        H["p9_sty"] = stym


def p8_mlp_final(b, ST, H, C):
    nc, t_, v_, s_, g_, y_ = b.nc, b.t, b.v, b.s, b.g, b.y
    T = H["T"]
    l1w = H["l1w"]
    CH = 256
    nCH = T // CH
    with ExitStack() as S:
        sb = lambda n, s, d: S.enter_context(nc.sbuf_tensor(n, s, d))
        w1sb = sb("p8w1", [128, 8 * D_MLP], BF16)
        w2sb = sb("p8w2", [128, 32 * 1024], BF16)
        b1v = sb("p8b1", [128, 32], F32)
        b2v = sb("p8b2", [128, 8], F32)
        l2g = sb("p8l2g", [128, 8], F32)
        l2b = sb("p8l2b", [128, 8], F32)
        h2f = [sb("p8h%d" % i, [128, 8 * CH], BF16) for i in range(2)]
        g1 = sb("p8g1", [128, 32 * CH], BF16)
        mo = sb("p8mo", [128, CH], F32)
        mcomb = sb("p8mc", [128, 2 * 1024], F32)
        x1l = [sb("p8x1_%d" % i, [128, 1024], F32) for i in range(2)]
        x2l = [sb("p8x2_%d" % i, [128, 1024], F32) for i in range(2)]
        mol = [sb("p8mol%d" % i, [128, 1024], F32) for i in range(2)]
        ps_g = [S.enter_context(nc.psum_tensor("p8pg%d" % i, [128, CH], F32)) for i in range(2)]
        ps_m = [S.enter_context(nc.psum_tensor("p8m%d" % i, [128, CH], F32)) for i in range(2)]
        ps_tf = [S.enter_context(nc.psum_tensor("p8tf%d" % i, [128, 128], F32)) for i in range(2)]
        ldw = b.sem("p8ldw")
        for cc in range(8):
            ldw.inc(g_.dma_start(w1sb[:, cc * D_MLP:(cc + 1) * D_MLP],
                                 l1w["mlp_W1"][cc * 128:(cc + 1) * 128, :]), 16)
        for fc in range(32):
            ldw.inc(g_.dma_start(w2sb[:, fc * 1024:(fc + 1) * 1024],
                                 l1w["mlp_W2"][fc * 128:(fc + 1) * 128, :]), 16)
        ldv = b.sem("p8ldv")
        vecload(b, ldv, y_, b1v, l1w["mlp_b1"], 32)
        vecload(b, ldv, y_, b2v, l1w["mlp_b2"], 8)
        vecload(b, ldv, y_, l2g, l1w["ln2_g"], 8)
        vecload(b, ldv, y_, l2b, l1w["ln2_b"], 8)
        ldh = DBuf(b, y_, 2, "p8ldh")
        ldf = DBuf(b, y_, 2, "p8ldf")
        stf = DBuf(b, y_, 2, "p8stf")
        ldw.wait(t_)
        ldv.wait(v_)
        ldv.wait(s_)
        ldv.wait(g_)
        gfree = [None, None]
        lnc8 = [None, None]
        combc = None

        def ldh_issue(i):
            ldh.gate(i)
            for cc in range(8):
                ldh.slot(i).inc(y_.dma_start(
                    h2f[i % 2][:, cc * CH:(cc + 1) * CH],
                    H["d_h2"][i * CH:(i + 1) * CH, cc * 128:(cc + 1) * 128],
                    transpose=True), 16)

        def ln_issue(i):
            ldh.wait(i, g_)
            hfi = h2f[i % 2]
            for cc in range(8):
                b.gm(g_.tensor_scalar(hfi[:, cc * CH:(cc + 1) * CH],
                                      hfi[:, cc * CH:(cc + 1) * CH],
                                      l2g[:, cc:cc + 1], l2b[:, cc:cc + 1],
                                      AL.mult, AL.add))
            lnc8[i % 2] = b.cg.n

        ldh_issue(0)
        if nCH > 1:
            ldh_issue(1)
        for k2 in range(2):
            ldf.gate(k2)
            ldf.slot(k2).inc(y_.dma_start(x1l[k2][:, :], H["d_x1"][k2 * 128:(k2 + 1) * 128, :]), 16)
            ldf.slot(k2).inc(y_.dma_start(x2l[k2][:, :], H["d_x2"][k2 * 128:(k2 + 1) * 128, :]), 16)
            ldf.slot(k2).inc(y_.dma_start(mol[k2][:, :], H["d_moe"][k2 * 128:(k2 + 1) * 128, :]), 16)
        ln_issue(0)
        for j in range(nCH):
            hf = h2f[j % 2]
            t_.wait_ge(b.cg.h, lnc8[j % 2])
            for fc in range(32):
                pb = fc % 2
                if gfree[pb] is not None:
                    t_.wait_ge(b.cs.h, gfree[pb])
                last = None
                for cc in range(8):
                    last = t_.matmul(
                        ps_g[pb][:, :],
                        w1sb[:, cc * D_MLP + fc * 128: cc * D_MLP + (fc + 1) * 128],
                        hf[:, cc * CH:(cc + 1) * CH],
                        start=(cc == 0), stop=(cc == 7))
                b.tm(last)
                b.wt(s_)
                b.sm(s_.activation(g1[:, fc * CH:(fc + 1) * CH], ps_g[pb][:, :],
                                   AF.Gelu, bias=b1v[:, fc:fc + 1]))
                gfree[pb] = b.cs.n
            ldh.done(j, b.ct)
            if j + 2 < nCH:
                ldh_issue(j + 2)
            if j + 1 < nCH:
                ln_issue(j + 1)
            b.ws(t_)
            mfree = [None, None]
            for co in range(8):
                pb = co % 2
                if mfree[pb] is not None:
                    t_.wait_ge(b.cs.h, mfree[pb])
                last = None
                for fc in range(32):
                    last = t_.matmul(
                        ps_m[pb][:, :],
                        w2sb[:, fc * 1024 + co * 128: fc * 1024 + (co + 1) * 128],
                        g1[:, fc * CH: fc * CH + CH],
                        start=(fc == 0), stop=(fc == 31))
                b.tm(last)
                b.wt(s_)
                b.sm(s_.activation(mo[:, :], ps_m[pb][:, :], AF.Identity,
                                   bias=b2v[:, co:co + 1]))
                mfree[pb] = b.cs.n
                b.ws(t_)
                tf_free = getattr(b, "_p8tfree", [None, None])
                for tt in range(CH // 128):
                    sl = tt % 2
                    if tf_free[sl] is not None:
                        t_.wait_ge(b.cs.h, tf_free[sl])
                    b.tm(t_.transpose(ps_tf[sl][:, :], mo[:, tt * 128:(tt + 1) * 128],
                                      C["ident_f"][:, :]))
                    b.wt(s_)
                    if combc is not None and co == 0 and tt == 0:
                        s_.wait_ge(b.cv.h, combc)
                    b.sm(s_.copy(
                        mcomb[:, tt * 1024 + co * 128: tt * 1024 + (co + 1) * 128],
                        ps_tf[sl][:, :]))
                    tf_free[sl] = b.cs.n
                b._p8tfree = tf_free
            for tt in range(CH // 128):
                r0 = j * CH + tt * 128
                k2 = (CH // 128) * j + tt
                ldf.wait(k2, v_)
                stf.wait_all(v_)
                b.ws(v_)
                ob = x1l[k2 % 2]
                b.vm(v_.tensor_add(ob[:, :], ob[:, :], x2l[k2 % 2][:, :]))
                b.wv(v_)
                b.vm(v_.tensor_add(ob[:, :], ob[:, :], mol[k2 % 2][:, :]))
                b.wv(v_)
                b.vm(v_.tensor_add(ob[:, :], ob[:, :], mcomb[:, tt * 1024:(tt + 1) * 1024]))
                b.wv(v_)
                stf.gate(k2)
                b.wv(y_)
                stf.slot(k2).inc(y_.dma_start(H["out_t"][r0:r0 + 128, :], ob[:, :]), 16)
                ldf.done(k2, stf.sems[k2 % 2])
                if k2 + 2 < 2 * nCH:
                    r2 = (k2 + 2) * 128
                    ldf.gate(k2 + 2)
                    ldf.slot(k2 + 2).inc(y_.dma_start(x1l[k2 % 2][:, :],
                                                      H["d_x1"][r2:r2 + 128, :]), 16)
                    ldf.slot(k2 + 2).inc(y_.dma_start(x2l[k2 % 2][:, :],
                                                      H["d_x2"][r2:r2 + 128, :]), 16)
                    ldf.slot(k2 + 2).inc(y_.dma_start(mol[k2 % 2][:, :],
                                                      H["d_moe"][r2:r2 + 128, :]), 16)
            combc = b.cv.n
        stf.wait_all(y_)
        H["p8_stf"] = stf


_NCORES = 8
_T_FULL = 4096


def kernel(**inputs):
    from concourse.bass_utils import run_bass_kernel_spmd
    T = _T_FULL
    nc = build_program(T, _NCORES, upto="full")
    consts = host_constants(T)
    x_full = np.ascontiguousarray(np.asarray(inputs["L0_attn_x"], dtype=np.float32))
    in_maps = []
    for c in range(_NCORES):
        m = {}
        for k, v in inputs.items():
            if k == "L0_attn_x":
                m[k] = np.ascontiguousarray(x_full[c])
            else:
                m[k] = np.ascontiguousarray(np.asarray(v, dtype=np.float32))
        for k, v in consts.items():
            m[k] = np.ascontiguousarray(v)
        in_maps.append(m)
    res = run_bass_kernel_spmd(nc, in_maps, core_ids=list(range(_NCORES)))
    out = np.stack([np.asarray(r["out"], dtype=np.float32) for r in res.results], axis=0)
    return out
